# version 05
# baseline (speedup 1.0000x reference)
import numpy as np
import ml_dtypes
from collections import defaultdict
from contextlib import ExitStack

import concourse.bass as bass
import concourse.mybir as mybir
from concourse.expressions import RuntimeValue
from concourse.bass_utils import run_bass_kernel_spmd

F32 = mybir.dt.float32
BF16 = mybir.dt.bfloat16
AF = mybir.ActivationFunctionType
ALU = mybir.AluOpType
ds = bass.ds

D = 2048
T = 8192
TOK = 2048
NCK = 16
QT = 512
NQT = T // QT
NFC = 11
SCALE_A = float(1.0 / np.float64(np.sqrt(np.float32(T))))
SCALE_B = float(1.0 / np.float64(np.sqrt(np.float32(128))))
BOUND_B = float(np.sqrt(np.float64(128.0)))
LN_EPS = 1e-5
RMS_EPS = 1.1920929e-07
NEG = -1.0e9

ENGINES = ("sync", "scalar", "vector", "tensor", "gpsimd")
DBG = False


class Rec:

    def __init__(self):
        self.q = {e: [] for e in ENGINES}
        self.cnt = defaultdict(int)
        self.dynm = {}
        self.dyn = None

    def op(self, eng, fn, inc=None, n=1):
        self.q[eng].append((fn, inc, n, None))
        if inc:
            self.cnt[inc] += n
        return self.cnt[inc] if inc else None

    def dma(self, eng, out, in_, inc="q" + "X", **kw):
        return self.op(eng, lambda E, o=out, i=in_, k=kw: E.dma_start(out=o, in_=i, **k), inc=inc, n=16)

    def dmad(self, eng, fn, inc):
        self.q[eng].append((None, inc, 16, fn))
        self.cnt[inc] += 16
        return self.cnt[inc]

    def wait(self, eng, sem, val=None):
        v = self.cnt[sem] if val is None else val
        assert v <= self.cnt[sem], (sem, v, self.cnt[sem])
        self.q[eng].append((("W", sem, v), None, 0, None))

    def mark(self, sem):
        return self.cnt[sem]

    def barrier(self, engines=ENGINES, sems=None):
        if sems is None:
            sems = [s for s in self.cnt if not s.startswith("cc")]
        for e in engines:
            for s in sems:
                if self.cnt[s] > 0:
                    self.wait(e, s, self.cnt[s])


def build_program():
    nc = bass.Bass("TRN2", target_bir_lowering=False, detect_race_conditions=False)
    dr = lambda name, shape, dt, kind="ExternalInput": nc.dram_tensor(name, shape, dt, kind=kind).ap()

    x_in = dr("x_in", [128, NCK * TOK], F32)
    Wqa = dr("Wqa", [128, 32768], F32); Wka = dr("Wka", [128, 32768], F32)
    Wva = dr("Wva", [128, 32768], F32); Woa = dr("Woa", [128, 32768], F32)
    Wqb = dr("Wqb", [128, 32768], F32); Wkb = dr("Wkb", [128, 16384], F32)
    Wvb = dr("Wvb", [128, 16384], F32); Wob = dr("Wob", [128, 32768], F32)
    Aall = dr("Aall", [128, 4 * 512], F32)
    Ball = dr("Ball", [32, 6144], F32)
    rW = dr("rW", [128, 128], F32)
    w13 = dr("w13", [10, NFC, 128, 4096], F32)
    w2s = dr("w2s", [10, 128, NFC * 2048], F32)
    csO = dr("csO", [64, 2 * TOK], BF16)
    maskAi = dr("maskAi", [128, 4 * QT], BF16)
    maskBi = dr("maskBi", [128, 4 * QT], BF16)
    identF = dr("identF", [128, 128], F32)
    onesC = dr("onesC", [128, 1], F32)
    onesCb = dr("onesCb", [128, 1], BF16)
    onesR = dr("onesR", [1, 128], F32)
    UTc = dr("UTc", [16, 16], F32)
    sel8 = dr("sel8", [8, 8 * 128], BF16)
    gain4 = dr("gain4", [1, 4], F32)
    gidx = dr("gidx", [128, 112], mybir.dt.uint32)
    svec = dr("svec", [1, 4], F32)
    epsv = dr("epsv", [1, 2], F32)
    out_d = dr("out_d", [128, NCK * TOK], F32, kind="ExternalOutput")
    if DBG:
        dbg_h = dr("dbg_h", [128, NCK * TOK], BF16, kind="ExternalOutput")
        dbg_qkv = dr("dbg_qkv", [6144, 2048], BF16, kind="ExternalOutput")
        dbg_att = dr("dbg_att", [512, T], BF16, kind="ExternalOutput")
        dbg_x1 = dr("dbg_x1", [128, NCK * TOK], BF16, kind="ExternalOutput")
        dbg_x2 = dr("dbg_x2", [128, NCK * TOK], BF16, kind="ExternalOutput")
        dbg_x3 = dr("dbg_x3", [128, NCK * TOK], BF16, kind="ExternalOutput")
        dbg_qkvb = dr("dbg_qkvb", [4096, 2048], BF16, kind="ExternalOutput")
        dbg_attb = dr("dbg_attb", [512, T], BF16, kind="ExternalOutput")
        dbg_fine = dr("dbg_fine", [128, 6144], F32, kind="ExternalOutput")
        dbg_ffn = dr("dbg_ffn", [128, 8192], F32, kind="ExternalOutput")
        dbg_rt = dr("dbg_rt", [8, 4096], F32, kind="ExternalOutput")
        dbg_t2 = dr("dbg_t2", [128, 64], F32, kind="ExternalOutput")

    WqaB = dr("WqaB", [128, 32768], BF16, kind="Internal")
    WkaB = dr("WkaB", [128, 32768], BF16, kind="Internal")
    WvaB = dr("WvaB", [128, 32768], BF16, kind="Internal")
    WoaB = dr("WoaB", [128, 32768], BF16, kind="Internal")
    WqbB = dr("WqbB", [128, 32768], BF16, kind="Internal")
    WkbB = dr("WkbB", [128, 16384], BF16, kind="Internal")
    WvbB = dr("WvbB", [128, 16384], BF16, kind="Internal")
    WobB = dr("WobB", [128, 32768], BF16, kind="Internal")
    w13B = dr("w13B", [10, NFC, 128, 4096], BF16, kind="Internal")
    w2sB = dr("w2sB", [10, 128, NFC * 2048], BF16, kind="Internal")
    g1a_in = dr("g1a_in", [6144, 2048], BF16, kind="Internal")
    g1a_out = dr("g1a_out", [24576, 2048], BF16, kind="Internal")
    g1b_in = dr("g1b_in", [4096, 2048], BF16, kind="Internal")
    g1b_out = dr("g1b_out", [16384, 2048], BF16, kind="Internal")
    g2a_in = dr("g2a_in", [512, T], BF16, kind="Internal")
    g2a_out = dr("g2a_out", [2048, T], BF16, kind="Internal")
    g2b_in = dr("g2b_in", [512, T], BF16, kind="Internal")
    g2b_out = dr("g2b_out", [2048, T], BF16, kind="Internal")

    ctx = ExitStack()
    sb = lambda name, shape, dt: ctx.enter_context(nc.sbuf_tensor(name, shape, dt))
    x_sb = sb("x_sb", [128, NCK * TOK], BF16)
    h_sb = sb("h_sb", [128, NCK * TOK], BF16)
    wbuf = [sb(f"wbuf{i}", [128, 4096], BF16) for i in range(2)]
    stg = [sb(f"stg{i}", [128, 1024], BF16) for i in range(2)]
    row_sb = sb("row_sb", [1, 1024], F32)
    sq_sb = sb("sq_sb", [128, 512], F32)
    silu_sb = sb("silu_sb", [128, 1024], BF16)
    cb_sb = sb("cb_sb", [128, 2048], BF16)
    c_fm = sb("c_fm", [8, 2048], BF16)
    cs_sb = sb("cs_sb", [64, 2 * TOK], BF16)
    rt_sb = sb("rt_sb", [64, 1024], BF16)
    mask_sb = sb("mask_sb", [128, 4 * QT], BF16)
    ident_sb = sb("ident_sb", [128, 128], F32)
    sel_sb = sb("sel_sb", [8, 8 * 128], BF16)
    onesC_sb = sb("onesC_sb", [128, 1], F32)
    onesCb_sb = sb("onesCb_sb", [128, 1], BF16)
    onesR_sb = sb("onesR_sb", [1, 128], F32)
    UT_sb = sb("UT_sb", [16, 16], F32)
    rW_sb = sb("rW_sb", [128, 128], BF16)
    gb_sb = sb("gb_sb", [128, 16], F32)
    gidx_sb = sb("gidx_sb", [128, 112], mybir.dt.uint32)
    tp_sb = sb("tp_sb", [128, 16], F32)
    stm_sb = sb("stm_sb", [128, 8], F32)
    msk1_sb = sb("msk1_sb", [128, 8], F32)
    cv_sb = sb("cv_sb", [128, 16], F32)
    cvT_sb = sb("cvT_sb", [16, 128], F32)
    sv_sb = sb("sv_sb", [128, 16], F32)
    AR = sb("AR", [128, 12800], BF16)

    psA = ctx.enter_context(nc.psum_tensor("psA", [128, 2048], F32))
    psB = ctx.enter_context(nc.psum_tensor("psB", [128, 2048], F32))

    sem_names = ["qS", "qC", "qG", "pe", "act", "dve", "po",
                 "cc0", "cc1", "cc2", "cc3"]
    sems = {n: ctx.enter_context(nc.semaphore(n)) for n in sem_names}
    p = Rec()

    heV = AR[:, 0:6144]
    w2V = AR[:, 6144:12288]
    vslabV = AR[:, 0:8192]
    uV = AR[0:32, 8192:10240]
    BV = AR[0:32, 10240:12288]
    AVt = AR[:, 12288:12800]
    attnT2 = [AR[:, 0:1024], AR[:, 1024:2048]]
    S2s = [AR[:, 2048:4096].bitcast(F32), AR[:, 4608:6656].bitcast(F32)]
    rzVs = [AR[:, 4096:4608], AR[:, 6656:7168]]
    sfmV = AR[0:8, 4608:8704].bitcast(F32)
    outV = AR[:, 0:2048].bitcast(F32)

    p.dma("sync", ident_sb[:], identF, "qS")
    p.dma("sync", onesC_sb[:], onesC, "qS")
    p.dma("sync", onesCb_sb[:], onesCb, "qS")
    p.dma("sync", onesR_sb[:], onesR, "qS")
    p.dma("sync", UT_sb[:], UTc, "qS")
    p.dma("sync", sel_sb[:], sel8, "qS")
    p.dma("sync", cs_sb[:], csO, "qS")
    p.dma("sync", row_sb[0:1, 0:4], gain4, "qS")
    p.dma("sync", row_sb[0:1, 4:8], svec, "qS")
    p.dma("sync", gb_sb[0:1, 12:14], epsv, "qS", allow_slow_non_contiguous=True)
    p.dma("gpsimd", rW_sb[:], rW, "qG")
    p.dma("gpsimd", x_sb[:], x_in, "qG")
    p.dma("sync", gidx_sb[:], gidx, "qS")
    p.barrier()
    p.op("tensor", lambda E: E.matmul(psA[:, 0:8], onesR_sb[:], row_sb[0:1, 0:8],
                                      start=True, stop=True), inc="pe")
    p.wait("vector", "pe")
    p.op("vector", lambda E: E.tensor_copy(gb_sb[:, 0:8], psA[:, 0:8]), inc="dve")
    p.wait("scalar", "dve")
    p.op("scalar", lambda E: E.activation(gb_sb[:, 8:12], gb_sb[:, 0:4], AF.Abs), inc="act")
    p.wait("vector", "act")
    p.op("vector", lambda E: E.tensor_scalar_mul(gb_sb[:, 8:12], gb_sb[:, 8:12], -BOUND_B), inc="dve")
    p.barrier()

    wcast = {}
    for nm, dst, src in [("Wqa", WqaB, Wqa), ("Wka", WkaB, Wka), ("Wva", WvaB, Wva),
                         ("Woa", WoaB, Woa)]:
        p.dma("gpsimd", dst, src, "qG")
        wcast[nm] = p.mark("qG")
    for e in range(10):
        p.dma("gpsimd", w13B[e], w13[e], "qG")
        wcast[f"w13_{e}"] = p.mark("qG")
        p.dma("gpsimd", w2sB[e], w2s[e], "qG")
        wcast[f"w2_{e}"] = p.mark("qG")
    for nm, dst, src in [("Wqb", WqbB, Wqb), ("Wkb", WkbB, Wkb), ("Wvb", WvbB, Wvb),
                         ("Wob", WobB, Wob)]:
        p.dma("gpsimd", dst, src, "qG")
        wcast[nm] = p.mark("qG")

    def emit_ln():
        for tq4 in range(4):
            c0 = tq4 * 512
            for dc in range(NCK):
                p.wait("scalar", "pe")
                p.op("scalar", lambda E, d=dc, c0=c0: E.activation(
                    sq_sb[:], x_sb[:, d * TOK + c0:d * TOK + c0 + 512], AF.Square), inc="act")
                p.wait("tensor", "act")
                p.op("tensor", lambda E, d=dc, c0=c0: E.matmul(
                    psA[0:1, 0:512], onesCb_sb[:], x_sb[:, d * TOK + c0:d * TOK + c0 + 512],
                    start=(d == 0), stop=(d == NCK - 1)))
                p.op("tensor", lambda E, d=dc: E.matmul(
                    psA[0:1, 512:1024], onesC_sb[:], sq_sb[:, 0:512],
                    start=(d == 0), stop=(d == NCK - 1)), inc="pe")
            p.wait("vector", "pe")
            mu = row_sb[0:1, 0:512]
            wk = row_sb[0:1, 512:1024]
            p.op("vector", lambda E: E.tensor_scalar_mul(mu, psA[0:1, 0:512], 1.0 / D))
            p.op("vector", lambda E: E.tensor_mul(wk, mu, mu))
            p.op("vector", lambda E: E.scalar_tensor_tensor(
                wk, psA[0:1, 512:1024], 1.0 / D, wk, ALU.mult, ALU.subtract), inc="dve")
            p.wait("scalar", "dve")
            p.op("scalar", lambda E: E.activation(wk, wk, AF.Sqrt, bias=gb_sb[0:1, 12:13]), inc="act")
            p.wait("vector", "act")
            p.op("vector", lambda E: E.reciprocal(wk, wk))
            p.op("vector", lambda E: E.scalar_tensor_tensor(mu, mu, -1.0, wk, ALU.mult, ALU.mult), inc="dve")
            p.wait("tensor", "dve")
            p.op("tensor", lambda E: E.matmul(psB[:, 0:512], onesR_sb[:], wk, start=True, stop=True))
            p.op("tensor", lambda E: E.matmul(psB[:, 512:1024], onesR_sb[:], mu, start=True, stop=True), inc="pe")
            p.wait("vector", "pe")
            for dc in range(NCK):
                p.op("vector", lambda E, d=dc, c0=c0: E.tensor_mul(
                    h_sb[:, d * TOK + c0:d * TOK + c0 + 512],
                    x_sb[:, d * TOK + c0:d * TOK + c0 + 512], psB[:, 0:512]))
                p.op("vector", lambda E, d=dc, c0=c0: E.tensor_add(
                    h_sb[:, d * TOK + c0:d * TOK + c0 + 512],
                    h_sb[:, d * TOK + c0:d * TOK + c0 + 512], psB[:, 512:1024]),
                    inc=("dve" if dc == NCK - 1 else None))
            p.barrier(("tensor", "scalar"), ["dve"])
        p.barrier()

    def load_AB(acol, bcol, blen):
        p.dma("gpsimd", AVt, Aall[:, acol:acol + 512], "qG")
        p.dma("gpsimd", BV[:, 0:blen], Ball[:, bcol:bcol + blen], "qG")

    def emit_u():
        p.barrier(("tensor",), ["qG"])
        for dc in range(NCK):
            for g in range(4):
                p.op("tensor", lambda E, d=dc, g=g: E.matmul(
                    psA[0:32, g * 512:(g + 1) * 512], AVt[:, d * 32:(d + 1) * 32],
                    h_sb[:, d * TOK + g * 512:d * TOK + g * 512 + 512],
                    start=(d == 0), stop=(d == NCK - 1)),
                    inc=("pe" if (dc == NCK - 1 and g == 3) else None))
        p.wait("scalar", "pe")
        p.op("scalar", lambda E: E.copy(uV[:, 0:2048], psA[0:32, 0:2048]), inc="act")
        p.wait("tensor", "act")

    def fm_proj(Wt, noc, dst, dst_rows, wmark, lora=False, scale_col=None, rope=False):
        pe_done = {}
        cons = {}
        store_done = {}
        qc_m = {}
        p.wait("scalar", "qG", wmark)
        p.dma("scalar", wbuf[0][:, 0:2048], Wt[:, 0:2048], "qC")
        qc_m[0] = p.mark("qC")
        for och in range(noc):
            wb = wbuf[och % 2]
            if och + 1 < noc:
                if och >= 1:
                    p.wait("scalar", "pe", pe_done[och - 1])
                p.dma("scalar", wbuf[(och + 1) % 2][:, 0:2048],
                      Wt[:, (och + 1) * 2048:(och + 2) * 2048], "qC")
                qc_m[och + 1] = p.mark("qC")
            p.wait("tensor", "qC", qc_m[och])
            for th in range(2):
                u = och * 2 + th
                ps = psA if u % 2 == 0 else psB
                c0 = th * 1024
                if u >= 2:
                    p.wait("tensor", *cons[u - 2])
                for dc in range(NCK):
                    for g in range(2):
                        p.op("tensor", lambda E, d=dc, g=g, ps=ps, c0=c0, wb=wb: E.matmul(
                            ps[:, g * 512:(g + 1) * 512], wb[:, d * 128:(d + 1) * 128],
                            h_sb[:, d * TOK + c0 + g * 512:d * TOK + c0 + g * 512 + 512],
                            start=(d == 0), stop=(d == NCK - 1)))
                if lora:
                    for g in range(2):
                        p.op("tensor", lambda E, g=g, ps=ps, c0=c0, och=och: E.matmul(
                            ps[:, 1024 + g * 512:1024 + (g + 1) * 512],
                            BV[:, och * 128:(och + 1) * 128],
                            uV[:, c0 + g * 512:c0 + g * 512 + 512],
                            start=True, stop=True),
                            inc=("pe" if g == 1 else None))
                else:
                    p.q["tensor"][-1] = (p.q["tensor"][-1][0], "pe", 1, None)
                    p.cnt["pe"] += 1
                if th == 1:
                    pe_done[och] = p.mark("pe")
                p.wait("scalar", "pe")
                st_ = stg[u % 2]
                if u >= 2:
                    p.wait("scalar", "qS", store_done[u - 2])
                if scale_col is not None:
                    p.op("scalar", lambda E, ps=ps, st=st_, sc=scale_col: E.activation(
                        st[:], ps[:, 0:1024], AF.Copy, scale=gb_sb[:, sc:sc + 1]), inc="act")
                else:
                    p.op("scalar", lambda E, ps=ps, st=st_: E.copy(st[:], ps[:, 0:1024]), inc="act")
                if lora:
                    p.wait("vector", "act")
                    p.op("vector", lambda E, ps=ps, st=st_: E.tensor_add(st[:], st[:], ps[:, 1024:2048]), inc="dve")
                    cons[u] = ("dve", p.mark("dve"))
                else:
                    cons[u] = ("act", p.mark("act"))
                if rope:
                    p.wait("gpsimd", *cons[u])
                    t_b0 = rt_sb[0:32, 0:1024]
                    t_b32 = rt_sb[32:64, 0:1024]
                    co_b0 = cs_sb[0:32, c0:c0 + 1024]
                    si_b32 = cs_sb[32:64, c0:c0 + 1024]
                    si_b0 = cs_sb[0:32, TOK + c0:TOK + c0 + 1024]
                    co_b32 = cs_sb[32:64, TOK + c0:TOK + c0 + 1024]
                    q1 = st_[0:32, :]
                    q2 = st_[32:64, :]
                    p.op("gpsimd", lambda E, q1=q1, si=si_b0, t=t_b32: E.tensor_mul(t, q1, si))
                    p.op("gpsimd", lambda E, q1=q1, co=co_b0: E.tensor_mul(q1, q1, co))
                    p.op("gpsimd", lambda E, q2=q2, si=si_b32, t=t_b0: E.tensor_mul(t, q2, si))
                    p.op("gpsimd", lambda E, q1=q1, t=t_b0: E.tensor_add(q1, q1, t))
                    p.op("gpsimd", lambda E, q2=q2, co=co_b32: E.tensor_mul(q2, q2, co))
                    p.op("gpsimd", lambda E, q2=q2, t=t_b32: E.tensor_sub(q2, q2, t), inc="po")
                    cons[u] = ("po", p.mark("po"))
                p.wait("sync", *cons[u])
                p.dma("sync", dst[dst_rows + och * 128:dst_rows + (och + 1) * 128, c0:c0 + 1024], st_[:], "qS")
                store_done[u] = p.mark("qS")
        p.barrier(("gpsimd", "vector", "scalar"), ["qS"])

    IOA = bass.IndirectOffsetOnAxis

    def gather_rows(dst_cols, src, idxcol):
        p.dmad("gpsimd", lambda E, dyn, dc=dst_cols, src=src, ic=idxcol: E.indirect_dma_start(
            h_sb[:, dc:dc + 2048], None, src, IOA(gidx_sb[:, ic:ic + 1], 0)), "qG")

    def v_transpose(n_tiles=64):
        marks = {}
        for t in range(n_tiles):
            stf = sq_sb[:, (t % 2) * 128:(t % 2) * 128 + 128]
            pslot = psA[:, 1024 + (t % 2) * 512:1024 + (t % 2) * 512 + 128]
            if t >= 2:
                p.wait("scalar", "pe", marks[t - 2][0])
            p.op("scalar", lambda E, t=t, stf=stf: E.copy(
                stf, h_sb[:, 16384 + t * 128:16384 + (t + 1) * 128]), inc="act")
            p.wait("tensor", "act")
            if t >= 2:
                p.wait("tensor", "dve", marks[t - 2][1])
            p.op("tensor", lambda E, stf=stf, ps=pslot: E.transpose(ps, stf, ident_sb[:]), inc="pe")
            p.wait("vector", "pe")
            p.op("vector", lambda E, t=t, ps=pslot: E.tensor_copy(
                h_sb[:, 24576 + t * 128:24576 + (t + 1) * 128], ps), inc="dve")
            marks[t] = (p.mark("pe"), p.mark("dve"))
        p.barrier(("tensor", "scalar", "gpsimd"), ["dve"])

    def collective(cin, cout, ccname):
        p.barrier(("gpsimd",), ["qS", "qG", "dve", "act", "po", "pe"])
        p.op("gpsimd", lambda E, i=cin, o=cout: E.collective_compute(
            "AllGather", ALU.bypass, replica_groups=[[0, 1, 2, 3], [4, 5, 6, 7]],
            ins=[i.opt()], outs=[o.opt()]), inc=ccname, n=1)
        for e in ENGINES:
            p.wait(e, ccname, 1)

    def head_norm(colbase, gaincol):
        for blk in range(16):
            c0 = colbase + blk * 512
            p.wait("scalar", "pe")
            p.op("scalar", lambda E, c0=c0: E.activation(
                sq_sb[:], h_sb[:, c0:c0 + 512], AF.Square), inc="act")
            p.wait("tensor", "act")
            p.op("tensor", lambda E: E.matmul(
                psA[0:1, 0:512], onesC_sb[:], sq_sb[:], start=True, stop=True), inc="pe")
            p.wait("scalar", "pe")
            p.op("scalar", lambda E: E.activation(row_sb[0:1, 0:512], psA[0:1, 0:512],
                                                  AF.Sqrt, bias=gb_sb[0:1, 13:14], scale=1.0 / 128.0), inc="act")
            p.wait("vector", "act")
            p.op("vector", lambda E: E.reciprocal(row_sb[0:1, 0:512], row_sb[0:1, 0:512]), inc="dve")
            p.wait("tensor", "dve")
            p.op("tensor", lambda E: E.matmul(
                psA[:, 1024:1536], onesR_sb[:], row_sb[0:1, 0:512], start=True, stop=True), inc="pe")
            p.wait("vector", "pe")
            if gaincol is not None:
                p.op("vector", lambda E, c0=c0, gc=gaincol: E.scalar_tensor_tensor(
                    h_sb[:, c0:c0 + 512], h_sb[:, c0:c0 + 512], gb_sb[:, gc:gc + 1],
                    psA[:, 1024:1536], ALU.mult, ALU.mult), inc="dve")
            else:
                p.op("vector", lambda E, c0=c0: E.tensor_mul(
                    h_sb[:, c0:c0 + 512], h_sb[:, c0:c0 + 512], psA[:, 1024:1536]), inc="dve")
            p.wait("tensor", "dve")
            p.wait("scalar", "dve")

    def head_attn(is_b, j, g2in):
        qh = h_sb[:, 0:8192]
        kh = h_sb[:, 8192:16384]
        vhf = h_sb[:, 16384:24576]
        vh = h_sb[:, 24576:32768]
        if not is_b:
            for g in range(16):
                p.op("vector", lambda E, g=g: E.reduce_sum(
                    cv_sb[:, g:g + 1], vhf[:, g * 512:(g + 1) * 512], axis=mybir.AxisListType.X),
                    inc="dve")
                p.wait("vector", "dve")
            p.wait("tensor", "dve")
            p.op("tensor", lambda E: E.transpose(psA[0:16, 0:128], cv_sb[:], ident_sb[:]), inc="pe")
            p.wait("vector", "pe")
            p.op("vector", lambda E: E.tensor_copy(cvT_sb[:], psA[0:16, 0:128]), inc="dve")
            p.wait("tensor", "dve")
            p.op("tensor", lambda E: E.matmul(psA[:, 0:16], cvT_sb[:], UT_sb[:],
                                              start=True, stop=True), inc="pe")
            p.wait("vector", "pe")
            p.op("vector", lambda E: E.tensor_copy(sv_sb[:], psA[:, 0:16]), inc="dve")
            p.wait("tensor", "dve")

        avs = [psB[:, 0:512], psB[:, 512:1024]]
        zs = [psB[0:1, 1024:1536], psB[0:1, 1536:2048]]
        bcs = [psB[:, 1024:1536], psB[:, 1536:2048]]
        rows = [row_sb[0:1, 0:512], row_sb[0:1, 512:1024]]
        exp_m = {}; av_m = {}; s2_m = {}
        zmm_m = {}; recip_m = {}; rzcopy_m = {}; norm_m = {}; store_m = {}
        av_last = {}
        gp_box = [0]

        def tail1(q):
            sl = q % 2
            if not is_b:
                p.wait("vector", "pe", av_last[q])
                p.op("vector", lambda E, sl=sl, q=q: E.tensor_scalar_add(
                    avs[sl], avs[sl], sv_sb[:, q:q + 1]), inc="dve")
            p.wait("tensor", "dve")
            for g in range(2):
                p.op("tensor", lambda E, g=g, sl=sl: E.matmul(
                    zs[sl], onesC_sb[:], S2s[sl][:, g * 512:(g + 1) * 512],
                    start=(g == 0), stop=(g == 1)), inc=("pe" if g == 1 else None))
            zmm_m[q] = p.mark("pe")
            p.wait("vector", "pe", zmm_m[q])
            if not is_b and q < 15:
                p.op("vector", lambda E, sl=sl, q=q: E.tensor_scalar_add(
                    zs[sl], zs[sl], float(128 * (60 - 4 * q))), inc="dve")
                p.wait("vector", "dve")
            p.op("vector", lambda E, sl=sl: E.reciprocal(rows[sl], zs[sl]), inc="dve")
            recip_m[q] = p.mark("dve")

        def tail2(q):
            sl = q % 2
            if q >= 2:
                p.wait("tensor", "act", rzcopy_m[q - 2])
            p.wait("tensor", "dve", recip_m[q])
            p.op("tensor", lambda E, sl=sl: E.matmul(
                bcs[sl], onesR_sb[:], rows[sl], start=True, stop=True), inc="pe")
            bc_m = p.mark("pe")
            p.wait("scalar", "pe", bc_m)
            if q >= 2:
                p.wait("scalar", "dve", norm_m[q - 2])
            p.op("scalar", lambda E, sl=sl: E.copy(rzVs[sl][:, 0:512], bcs[sl]), inc="act")
            rzcopy_m[q] = p.mark("act")
            p.wait("vector", "act", rzcopy_m[q])
            if q >= 2:
                p.wait("vector", "qS", store_m[q - 2])
            p.op("vector", lambda E, sl=sl: E.tensor_mul(
                stg[sl][:, 0:512], avs[sl], rzVs[sl][:, 0:512]), inc="dve")
            norm_m[q] = p.mark("dve")
            p.wait("sync", "dve", norm_m[q])
            p.dma("sync", g2in[j * 128:(j + 1) * 128, q * QT:(q + 1) * QT], stg[sl][:, 0:512], "qS")
            store_m[q] = p.mark("qS")

        for q in range(NQT):
            npair = 2 * (q + 1)
            sl = q % 2
            if q >= 2:
                p.wait("vector", "pe", zmm_m[q - 2])
            p.op("vector", lambda E, sl=sl: E.memset(S2s[sl][:], 0.0), inc="dve")
            for cp in range(npair):
                gp = gp_box[0]
                slot = gp % 2
                at = attnT2[slot]
                scp = psA[:, slot * 1024:(slot + 1) * 1024]
                if gp >= 2:
                    p.wait("tensor", "act", exp_m[gp - 2])
                for ci in range(2):
                    c = 2 * cp + ci
                    p.op("tensor", lambda E, c=c, q=q, sc=scp, ci=ci: E.matmul(
                        sc[:, ci * 512:(ci + 1) * 512], kh[:, c * 128:(c + 1) * 128],
                        qh[:, q * QT:(q + 1) * QT], start=True, stop=True),
                        inc=("pe" if ci == 1 else None))
                pe_m = p.mark("pe")
                dve_need = False
                for ci in range(2):
                    c = 2 * cp + ci
                    d_ = c - 4 * q
                    if 0 <= d_ <= 3:
                        if not dve_need:
                            p.wait("vector", "pe", pe_m)
                            dve_need = True
                        op_ = "tensor_add" if is_b else "tensor_mul"
                        p.op("vector", lambda E, sc=scp, ci=ci, d_=d_, op_=op_: getattr(E, op_)(
                            sc[:, ci * 512:(ci + 1) * 512], sc[:, ci * 512:(ci + 1) * 512],
                            mask_sb[:, d_ * QT:(d_ + 1) * QT]), inc="dve")
                if dve_need:
                    p.wait("scalar", "dve")
                else:
                    p.wait("scalar", "pe", pe_m)
                if gp >= 2:
                    p.wait("scalar", "pe", av_m[gp - 2])
                    p.wait("scalar", "dve", s2_m[gp - 2])
                if is_b:
                    p.op("scalar", lambda E, sc=scp, at=at, j=j: E.activation(
                        at, sc, AF.Exp, bias=gb_sb[:, 8 + j:9 + j], scale=SCALE_B), inc="act")
                else:
                    p.op("scalar", lambda E, sc=scp, at=at: E.activation(
                        at, sc, AF.Exp, scale=SCALE_A), inc="act")
                exp_m[gp] = p.mark("act")
                p.wait("vector", "act", exp_m[gp])
                p.op("vector", lambda E, at=at, sl=sl: E.tensor_add(S2s[sl][:], S2s[sl][:], at), inc="dve")
                s2_m[gp] = p.mark("dve")
                p.wait("tensor", "act", exp_m[gp])
                if cp == 0 and q >= 2:
                    p.wait("tensor", "dve", norm_m[q - 2])
                for ci in range(2):
                    c = 2 * cp + ci
                    p.op("tensor", lambda E, c=c, at=at, ci=ci, sl=sl,
                         s=(cp == 0 and ci == 0), st=(cp == npair - 1 and ci == 1): E.matmul(
                        avs[sl], vh[:, c * 128:(c + 1) * 128], at[:, ci * 512:(ci + 1) * 512],
                        start=s, stop=st), inc=("pe" if ci == 1 else None))
                av_m[gp] = p.mark("pe")
                gp_box[0] = gp + 1
                if q >= 1 and cp == 1:
                    tail1(q - 1)
                if q >= 1 and cp == min(4, npair - 1):
                    tail2(q - 1)
            av_last[q] = av_m[gp_box[0] - 1]
        tail1(NQT - 1)
        tail2(NQT - 1)

    def o_proj(g2out, Wot, lora, scale_col, aocol, wmark):
        for hdc in range(NCK):
            p.dmad("gpsimd", lambda E, dyn, hd=hdc, ic=aocol + hdc: E.indirect_dma_start(
                h_sb[:, hd * TOK:(hd + 1) * TOK], None,
                g2out.rearrange("r (b x) -> (r b) x", b=4),
                IOA(gidx_sb[:, ic:ic + 1], 0)), "qG")
        p.barrier(("tensor", "scalar", "vector", "gpsimd"), ["qG"])
        if lora:
            load_AB(1536, 4096, 2048)
            emit_u()
        pe_done = {}
        cons = {}
        qc_m = {}
        p.wait("scalar", "qG", wmark)
        p.dma("scalar", wbuf[0][:, 0:2048], Wot[:, 0:2048], "qC")
        qc_m[0] = p.mark("qC")
        for och in range(NCK):
            wb = wbuf[och % 2]
            if och + 1 < NCK:
                if och >= 1:
                    p.wait("scalar", "pe", pe_done[och - 1])
                p.dma("scalar", wbuf[(och + 1) % 2][:, 0:2048],
                      Wot[:, (och + 1) * 2048:(och + 2) * 2048], "qC")
                qc_m[och + 1] = p.mark("qC")
            p.wait("tensor", "qC", qc_m[och])
            for th in range(2):
                u = och * 2 + th
                ps = psA if u % 2 == 0 else psB
                c0 = th * 1024
                if u >= 2:
                    p.wait("tensor", *cons[u - 2])
                for dc in range(NCK):
                    for g in range(2):
                        p.op("tensor", lambda E, d=dc, g=g, ps=ps, c0=c0, wb=wb: E.matmul(
                            ps[:, g * 512:(g + 1) * 512], wb[:, d * 128:(d + 1) * 128],
                            h_sb[:, d * TOK + c0 + g * 512:d * TOK + c0 + g * 512 + 512],
                            start=(d == 0), stop=(d == NCK - 1)))
                if lora:
                    for g in range(2):
                        p.op("tensor", lambda E, g=g, ps=ps, c0=c0, och=och: E.matmul(
                            ps[:, 1024 + g * 512:1024 + (g + 1) * 512],
                            BV[:, och * 128:(och + 1) * 128],
                            uV[:, c0 + g * 512:c0 + g * 512 + 512],
                            start=True, stop=True), inc=("pe" if g == 1 else None))
                else:
                    p.q["tensor"][-1] = (p.q["tensor"][-1][0], "pe", 1, None)
                    p.cnt["pe"] += 1
                if th == 1:
                    pe_done[och] = p.mark("pe")
                p.wait("scalar", "pe")
                st_ = stg[u % 2]
                if u >= 2:
                    p.wait("scalar", *cons[u - 2])
                if scale_col is not None:
                    p.op("scalar", lambda E, ps=ps, st=st_, sc=scale_col: E.activation(
                        st[:], ps[:, 0:1024], AF.Copy, scale=gb_sb[:, sc:sc + 1]), inc="act")
                else:
                    p.op("scalar", lambda E, ps=ps, st=st_: E.copy(st[:], ps[:, 0:1024]), inc="act")
                p.wait("vector", "act")
                if lora:
                    p.op("vector", lambda E, ps=ps, st=st_: E.tensor_add(st[:], st[:], ps[:, 1024:2048]))
                p.op("vector", lambda E, st=st_, och=och, c0=c0: E.tensor_add(
                    x_sb[:, och * TOK + c0:och * TOK + c0 + 1024],
                    x_sb[:, och * TOK + c0:och * TOK + c0 + 1024], st[:]), inc="dve")
                cons[u] = ("dve", p.mark("dve"))
        p.barrier()

    def attn_sublayer(is_b):
        emit_ln()
        if DBG and not is_b and not hasattr(attn_sublayer, "_h_dumped"):
            attn_sublayer._h_dumped = True
            p.barrier()
            p.dma("sync", dbg_h, h_sb[:], "qS")
            p.barrier()
        if not is_b:
            p.dma("sync", mask_sb[:], maskAi, "qS")
            fm_proj(WqaB, 16, g1a_in, 0, wcast["Wqa"])
            fm_proj(WkaB, 16, g1a_in, 2048, wcast["Wka"])
            fm_proj(WvaB, 16, g1a_in, 4096, wcast["Wva"])
            collective(g1a_in, g1a_out, "cc0")
            for j in range(4):
                for s in range(4):
                    gather_rows(s * 2048, g1a_out, 12 * j + s)
                    gather_rows(8192 + s * 2048, g1a_out, 12 * j + 4 + s)
                    gather_rows(16384 + s * 2048, g1a_out, 12 * j + 8 + s)
                p.barrier(("tensor", "scalar", "vector", "gpsimd"), ["qG"])
                v_transpose()
                head_attn(False, j, g2a_in)
                p.barrier()
            collective(g2a_in, g2a_out, "cc1")
            o_proj(g2a_out, WoaB, False, None, 80, wcast["Woa"])
        else:
            p.dma("sync", mask_sb[:], maskBi, "qS")
            load_AB(0, 0, 2048)
            emit_u()
            fm_proj(WqbB, 16, g1b_in, 0, wcast["Wqb"], lora=True, scale_col=4, rope=True)
            load_AB(512, 2048, 1024)
            emit_u()
            fm_proj(WkbB, 8, g1b_in, 2048, wcast["Wkb"], lora=True, scale_col=5, rope=True)
            load_AB(1024, 3072, 1024)
            emit_u()
            fm_proj(WvbB, 8, g1b_in, 3072, wcast["Wvb"], lora=True, scale_col=6)
            collective(g1b_in, g1b_out, "cc2")
            for jj in range(2):
                for s in range(4):
                    gather_rows(8192 + s * 2048, g1b_out, 64 + 4 * jj + s)
                    gather_rows(16384 + s * 2048, g1b_out, 72 + 4 * jj + s)
                p.barrier(("tensor", "scalar", "vector", "gpsimd"), ["qG"])
                v_transpose()
                head_norm(8192, None)
                for j2 in range(2):
                    j = jj * 2 + j2
                    for s in range(4):
                        gather_rows(s * 2048, g1b_out, 48 + 4 * j + s)
                    p.barrier(("tensor", "scalar", "vector", "gpsimd"), ["qG"])
                    head_norm(0, j)
                    head_attn(True, j, g2b_in)
                    p.barrier()
            collective(g2b_in, g2b_out, "cc3")
            o_proj(g2b_out, WobB, True, 7, 96, wcast["Wob"])

    def emit_router():
        for dc in range(NCK):
            for g in range(4):
                p.op("tensor", lambda E, d=dc, g=g: E.matmul(
                    psB[0:8, g * 512:(g + 1) * 512], rW_sb[:, d * 8:(d + 1) * 8],
                    h_sb[:, d * TOK + g * 512:d * TOK + g * 512 + 512],
                    start=(d == 0), stop=(d == NCK - 1)),
                    inc=("pe" if (dc == NCK - 1 and g == 3) else None))
        p.wait("scalar", "pe")
        p.op("scalar", lambda E: E.activation(sfmV[:, 0:2048], psB[0:8, 0:2048], AF.Sigmoid), inc="act")
        for tc in range(NCK):
            p.wait("tensor", "act")
            if tc >= 1:
                p.wait("tensor", "dve")
            p.op("tensor", lambda E, tc=tc: E.transpose(
                psA[:, 1536:1544], sfmV[0:8, tc * 128:(tc + 1) * 128], ident_sb[0:8, 0:8]), inc="pe")
            p.wait("vector", "pe")
            m1 = tp_sb[:, 0:1]; m2 = tp_sb[:, 1:2]; r_ = tp_sb[:, 3:4]
            chain = [
                lambda E: E.tensor_copy(stm_sb[:], psA[:, 1536:1544]),
                lambda E: E.reduce_max(m1, stm_sb[:], axis=mybir.AxisListType.X),
                lambda E: E.tensor_scalar(msk1_sb[:], stm_sb[:], m1, None, ALU.subtract),
                lambda E: E.tensor_single_scalar(msk1_sb[:], msk1_sb[:], 0.0, ALU.is_equal),
                lambda E: E.scalar_tensor_tensor(msk1_sb[:], msk1_sb[:], NEG, stm_sb[:], ALU.mult, ALU.add),
                lambda E: E.reduce_max(m2, msk1_sb[:], axis=mybir.AxisListType.X),
                lambda E: E.tensor_add(tp_sb[:, 2:3], m1, m2),
                lambda E: E.reciprocal(r_, tp_sb[:, 2:3]),
                lambda E: E.tensor_single_scalar(msk1_sb[:], stm_sb[:], m2, ALU.is_ge),
                lambda E: E.scalar_tensor_tensor(
                    ctm_holder[0][:], stm_sb[:], r_, msk1_sb[:], ALU.mult, ALU.mult),
            ]
            for fn_ in chain:
                p.op("vector", fn_, inc="dve")
                p.wait("vector", "dve")
            if DBG and tc == 0 and not getattr(emit_router, "_td", False):
                emit_router._td = True
                p.barrier()
                p.dma("gpsimd", dbg_t2[:, 0:8], stm_sb[:], "qG", allow_slow_non_contiguous=True)
                p.dma("gpsimd", dbg_t2[:, 8:16], msk1_sb[:], "qG", allow_slow_non_contiguous=True)
                p.dma("gpsimd", dbg_t2[:, 16:32], tp_sb[:], "qG", allow_slow_non_contiguous=True)
                p.dma("gpsimd", dbg_t2[:, 32:40], ctm_holder[0][:], "qG", allow_slow_non_contiguous=True)
                p.barrier()
            p.wait("tensor", "dve")
            p.op("tensor", lambda E: E.transpose(psA[0:8, 1536:1664], ctm_holder[0][:], ident_sb[:]), inc="pe")
            p.wait("vector", "pe")
            p.op("vector", lambda E, tc=tc: E.tensor_copy(c_fm[:, tc * 128:(tc + 1) * 128], psA[0:8, 1536:1664]), inc="dve")
        p.barrier(("tensor", "scalar", "gpsimd"), ["dve"])
        if DBG and not getattr(emit_router, "_rd", False):
            emit_router._rd = True
            p.barrier()
            p.dma("gpsimd", dbg_rt[0:8, 0:2048], sfmV[:, 0:2048], "qG")
            p.dma("gpsimd", dbg_rt[0:8, 2048:4096], c_fm[:], "qG")
            p.barrier()

    def emit_ffn(final):
        emit_ln()
        emit_router()
        quarters = [(0, 3), (3, 6), (6, 9), (9, 11)]
        for Ei in range(10):
            routed = Ei < 8
            if routed:
                p.wait("tensor", "dve")
                for g in range(4):
                    p.op("tensor", lambda E, g=g, Ei=Ei: E.matmul(
                        psA[:, g * 512:(g + 1) * 512], sel_sb[:, Ei * 128:(Ei + 1) * 128],
                        c_fm[:, g * 512:(g + 1) * 512], start=True, stop=True),
                        inc=("pe" if g == 3 else None))
                p.wait("scalar", "pe")
                p.op("scalar", lambda E: E.copy(cb_sb[:], psA[:, 0:2048]), inc="act")
                p.wait("vector", "act")
                p.wait("tensor", "act")
            pe_fc = {}
            qc_fc = {}
            p.wait("scalar", "qG", wcast[f"w13_{Ei}"])
            p.wait("scalar", "qG", wcast[f"w2_{Ei}"])
            p.dma("scalar", wbuf[0][:], w13B[Ei, 0], "qC")
            qc_fc[0] = p.mark("qC")
            for (f0, f1) in quarters:
                nf = f1 - f0
                p.wait("scalar", "pe")
                p.dma("scalar", w2V[:, 0:nf * 2048], w2sB[Ei][:, f0 * 2048:f1 * 2048], "qC")
                w2_m = p.mark("qC")
                he_marks = []
                for fi in range(nf):
                    fc = f0 + fi
                    wb = wbuf[fc % 2]
                    if fc + 1 < NFC:
                        if fc - 1 in pe_fc:
                            p.wait("scalar", "pe", pe_fc[fc - 1])
                        p.dma("scalar", wbuf[(fc + 1) % 2][:], w13B[Ei, fc + 1], "qC")
                        qc_fc[fc + 1] = p.mark("qC")
                    p.wait("tensor", "qC", qc_fc[fc])
                    for th in range(2):
                        u = fi * 2 + th
                        ps = psA if u % 2 == 0 else psB
                        c0 = th * 1024
                        if u >= 2:
                            p.wait("tensor", "dve", he_marks[u - 2])
                        for dc in range(NCK):
                            for g in range(2):
                                p.op("tensor", lambda E, d=dc, g=g, ps=ps, c0=c0, wb=wb: E.matmul(
                                    ps[:, g * 512:(g + 1) * 512], wb[:, d * 128:(d + 1) * 128],
                                    h_sb[:, d * TOK + c0 + g * 512:d * TOK + c0 + g * 512 + 512],
                                    start=(d == 0), stop=(d == NCK - 1)))
                        for dc in range(NCK):
                            for g in range(2):
                                p.op("tensor", lambda E, d=dc, g=g, ps=ps, c0=c0, wb=wb: E.matmul(
                                    ps[:, 1024 + g * 512:1024 + (g + 1) * 512],
                                    wb[:, 2048 + d * 128:2048 + (d + 1) * 128],
                                    h_sb[:, d * TOK + c0 + g * 512:d * TOK + c0 + g * 512 + 512],
                                    start=(d == 0), stop=(d == NCK - 1)),
                                    inc=("pe" if (dc == NCK - 1 and g == 1) else None))
                        if th == 1:
                            pe_fc[fc] = p.mark("pe")
                        p.wait("scalar", "pe")
                        p.wait("scalar", "dve")
                        p.op("scalar", lambda E, ps=ps: E.activation(
                            silu_sb[:], ps[:, 0:1024], AF.Silu), inc="act")
                        p.wait("vector", "act")
                        he_sl = heV[:, fi * 2048 + c0:fi * 2048 + c0 + 1024]
                        p.op("vector", lambda E, ps=ps, he=he_sl: E.tensor_mul(he, silu_sb[:], ps[:, 1024:2048]))
                        if routed:
                            p.op("vector", lambda E, he=he_sl, c0=c0: E.tensor_mul(
                                he, he, cb_sb[:, c0:c0 + 1024]), inc="dve")
                        else:
                            p.q["vector"][-1] = (p.q["vector"][-1][0], "dve", 1, None)
                            p.cnt["dve"] += 1
                        he_marks.append(p.mark("dve"))
                if DBG and Ei == 0 and f0 == 0 and not getattr(emit_ffn, "_fd", False):
                    emit_ffn._fd = True
                    p.barrier()
                    p.dma("gpsimd", dbg_ffn[:, 0:6144], heV, "qG")
                    p.dma("gpsimd", dbg_ffn[:, 6144:8192], cb_sb[:], "qG")
                    p.barrier()
                p.wait("tensor", "qC", w2_m)
                p.wait("tensor", "dve")
                for th in range(2):
                    c0 = th * 1024
                    for op_ in range(8):
                        u = th * 8 + op_
                        ps = psA if u % 2 == 0 else psB
                        if u >= 2:
                            p.wait("tensor", "dve")
                        for fi in range(nf):
                            for oc in range(2):
                                och = op_ * 2 + oc
                                for g in range(2):
                                    p.op("tensor", lambda E, fi=fi, oc=oc, och=och, g=g, ps=ps, c0=c0, s=(fi == 0), st=(fi == nf - 1): E.matmul(
                                        ps[:, oc * 1024 + g * 512:oc * 1024 + g * 512 + 512],
                                        w2V[:, fi * 2048 + och * 128:fi * 2048 + (och + 1) * 128],
                                        heV[:, fi * 2048 + c0 + g * 512:fi * 2048 + c0 + g * 512 + 512],
                                        start=s, stop=st),
                                        inc=("pe" if (fi == nf - 1 and oc == 1 and g == 1) else None))
                        p.wait("vector", "pe")
                        for oc in range(2):
                            och = op_ * 2 + oc
                            if final:
                                p.op("vector", lambda E, ps=ps, oc=oc, och=och, c0=c0: E.tensor_add(
                                    x_sb[:, och * TOK + c0:och * TOK + c0 + 1024],
                                    x_sb[:, och * TOK + c0:och * TOK + c0 + 1024],
                                    ps[:, oc * 1024:(oc + 1) * 1024]),
                                    inc=("dve" if oc == 1 else None))
                            else:
                                p.op("vector", lambda E, ps=ps, oc=oc, och=och, c0=c0: E.tensor_add(
                                    x_sb[:, och * TOK + c0:och * TOK + c0 + 1024],
                                    x_sb[:, och * TOK + c0:och * TOK + c0 + 1024],
                                    ps[:, oc * 1024:(oc + 1) * 1024]),
                                    inc=("dve" if oc == 1 else None))
        p.barrier()

    ctm_holder = [ctx.enter_context(nc.sbuf_tensor("ctm_sb", [128, 8], F32))]


    def dump(dst, src):
        if DBG:
            p.barrier()
            p.dma("gpsimd", dst, src, "qG")
            p.barrier()

    def dump_sb(dst, src):
        if DBG:
            p.barrier()
            p.dma("sync", dst, src[:], "qS")
            p.barrier()

    if DBG:
        emit_ln._dump_h = dbg_h

    attn_sublayer(False)
    if DBG:
        dump(dbg_qkv, g1a_in)
        dump(dbg_att, g2a_in)
        dump_sb(dbg_x1, x_sb)
    emit_ffn(False)
    if DBG:
        dump_sb(dbg_x2, x_sb)
    attn_sublayer(True)
    if DBG:
        dump(dbg_qkvb, g1b_in)
        dump(dbg_attb, g2b_in)
        dump_sb(dbg_x3, x_sb)
    emit_ffn(True)

    for och in range(NCK):
        for th in range(2):
            u = och * 2 + th
            if u >= 1:
                p.wait("scalar", "qS", out_m[u - 1] if u >= 1 else 0)
            p.op("scalar", lambda E, och=och, th=th: E.copy(
                outV[:, 0:1024], x_sb[:, och * TOK + th * 1024:och * TOK + th * 1024 + 1024]), inc="act")
            p.wait("sync", "act")
            if u == 0:
                out_m = {}
            p.dma("sync", out_d[:, och * TOK + th * 1024:och * TOK + th * 1024 + 1024], outV[:, 0:1024], "qS")
            out_m[u] = p.mark("qS")
    p.wait("sync", "qS")
    p.wait("sync", "qG")
    p.wait("sync", "qC")

    with nc.Block() as block:
        def make_runner(name):
            def runner(engine):
                dyn = {}
                for fn, inc, n, dfn in p.q[name]:
                    if dfn is not None:
                        ins = dfn(engine, dyn)
                        ins.then_inc(sems[inc], n)
                    elif isinstance(fn, tuple) and fn[0] == "W":
                        engine.wait_ge(sems[fn[1]], fn[2])
                    else:
                        ins = fn(engine)
                        if inc is not None:
                            ins.then_inc(sems[inc], n)
            return runner

        r_sync = make_runner("sync"); r_scalar = make_runner("scalar")
        r_vector = make_runner("vector"); r_tensor = make_runner("tensor")
        r_gpsimd = make_runner("gpsimd")

        @block.sync
        def _(sync):
            r_sync(sync)

        @block.scalar
        def _(scalar):
            r_scalar(scalar)

        @block.vector
        def _(vector):
            r_vector(vector)

        @block.tensor
        def _(tensor):
            r_tensor(tensor)

        @block.gpsimd
        def _(gpsimd):
            r_gpsimd(gpsimd)

    ctx.close()
    return nc



def tile_stationary(W):
    d, o = W.shape
    ndc, noc = d // 128, o // 128
    return np.ascontiguousarray(
        W.reshape(ndc, 128, noc, 128).transpose(1, 2, 0, 3).reshape(128, noc * ndc * 128))


def tile_moving(W):
    d, o = W.shape
    ndc = d // 128
    return np.ascontiguousarray(W.reshape(ndc, 128, o).transpose(1, 0, 2).reshape(128, ndc * o))


def tile_a(A):
    At = np.ascontiguousarray(A.T)
    ndc = At.shape[0] // 128
    return np.ascontiguousarray(At.reshape(ndc, 128, 32).transpose(1, 0, 2).reshape(128, ndc * 32))


def tile_w13(w1, w3):
    nfc = w1.shape[1] // 128
    out = np.empty((nfc, 128, 4096), np.float32)
    for which, w in ((0, w1), (1, w3)):
        t = w.reshape(16, 128, nfc, 128).transpose(2, 1, 0, 3).reshape(nfc, 128, 2048)
        out[:, :, which * 2048:(which + 1) * 2048] = t
    return out


def tile_w2(w2):
    nfc = w2.shape[0] // 128
    return np.ascontiguousarray(
        w2.reshape(nfc, 128, 16, 128).transpose(1, 0, 2, 3).reshape(128, nfc * 2048))


_CACHE = {}


def kernel(**inputs):
    ip = {k: np.asarray(v, dtype=np.float32) for k, v in inputs.items()}
    x = ip["L0_attn_a_x"]

    shared = {}
    shared["Wqa"] = tile_stationary(ip["L0_attn_a_Wq"])
    shared["Wka"] = tile_stationary(ip["L0_attn_a_Wk"])
    shared["Wva"] = tile_stationary(ip["L0_attn_a_Wv"])
    shared["Woa"] = tile_stationary(ip["L0_attn_a_Wo"])
    shared["Wqb"] = tile_stationary(np.ascontiguousarray(ip["L0_attn_b_W_q"].T))
    shared["Wkb"] = tile_stationary(np.ascontiguousarray(ip["L0_attn_b_W_k"].T))
    shared["Wvb"] = tile_stationary(np.ascontiguousarray(ip["L0_attn_b_W_v"].T))
    shared["Wob"] = tile_stationary(np.ascontiguousarray(ip["L0_attn_b_W_o"].T))
    shared["Aall"] = np.concatenate([tile_a(ip["L0_attn_b_A_q"]), tile_a(ip["L0_attn_b_A_k"]),
                                     tile_a(ip["L0_attn_b_A_v"]), tile_a(ip["L0_attn_b_A_o"])], axis=1)
    shared["Ball"] = np.concatenate([np.ascontiguousarray(ip["L0_attn_b_B_q"].T),
                                     np.ascontiguousarray(ip["L0_attn_b_B_k"].T),
                                     np.ascontiguousarray(ip["L0_attn_b_B_v"].T),
                                     np.ascontiguousarray(ip["L0_attn_b_B_o"].T)], axis=1)
    rt = ip["L0_ffn_w_router"]
    shared["rW"] = np.ascontiguousarray(
        rt.reshape(16, 128, 8).transpose(1, 0, 2).reshape(128, 128))
    w13l = [tile_w13(ip["L0_ffn_w1"][e], ip["L0_ffn_w3"][e]) for e in range(8)]
    ws1, ws3, ws2 = ip["L0_ffn_ws1"], ip["L0_ffn_ws3"], ip["L0_ffn_ws2"]
    w13l.append(tile_w13(ws1[:, 0:1408], ws3[:, 0:1408]))
    w13l.append(tile_w13(ws1[:, 1408:2816], ws3[:, 1408:2816]))
    shared["w13"] = np.stack(w13l)
    w2l = [tile_w2(ip["L0_ffn_w2"][e]) for e in range(8)]
    w2l.append(tile_w2(ws2[0:1408]))
    w2l.append(tile_w2(ws2[1408:2816]))
    shared["w2s"] = np.stack(w2l)

    tqi = np.arange(QT)[None, :]
    tsi = np.arange(128)[:, None]
    mA = np.zeros((128, 4 * QT), ml_dtypes.bfloat16)
    mB = np.zeros((128, 4 * QT), ml_dtypes.bfloat16)
    for d_ in range(4):
        keep = (tqi - d_ * 128) >= tsi
        mA[:, d_ * QT:(d_ + 1) * QT] = keep.astype(ml_dtypes.bfloat16)
        mB[:, d_ * QT:(d_ + 1) * QT] = np.where(keep, 0.0, NEG).astype(ml_dtypes.bfloat16)
    shared["maskAi"] = mA
    shared["maskBi"] = mB
    shared["identF"] = np.eye(128, dtype=np.float32)
    shared["onesC"] = np.ones((128, 1), np.float32)
    shared["onesCb"] = np.ones((128, 1), ml_dtypes.bfloat16)
    shared["onesR"] = np.ones((1, 128), np.float32)
    g_ = np.arange(16)
    shared["UTc"] = (g_[:, None] > g_[None, :]).astype(np.float32)
    sel = np.zeros((8, 8 * 128), ml_dtypes.bfloat16)
    for e in range(8):
        sel[e, e * 128:(e + 1) * 128] = 1
    shared["sel8"] = sel
    shared["svec"] = np.array([[ip["L0_attn_b_s_q"][0], ip["L0_attn_b_s_k"][0],
                                ip["L0_attn_b_s_v"][0], ip["L0_attn_b_s_o"][0]]], np.float32)
    shared["epsv"] = np.array([[LN_EPS, RMS_EPS]], np.float32)

    inv = 1.0 / (10000.0 ** (np.arange(0, 64, 2, dtype=np.float32) / 64.0))
    freqs = np.outer(np.arange(T, dtype=np.float32), inv)
    cosf = np.cos(freqs).T
    sinf = np.sin(freqs).T

    in_maps = []
    for c in range(8):
        b, r = c // 4, c % 4
        xs = np.ascontiguousarray(x[b, r * TOK:(r + 1) * TOK, :].T)
        m = dict(shared)
        m["x_in"] = np.ascontiguousarray(
            xs.reshape(16, 128, TOK).transpose(1, 0, 2).reshape(128, 16 * TOK))
        cs = np.empty((64, 2 * TOK), ml_dtypes.bfloat16)
        cs[0:32, 0:TOK] = cosf[:, r * TOK:(r + 1) * TOK].astype(ml_dtypes.bfloat16)
        cs[32:64, 0:TOK] = sinf[:, r * TOK:(r + 1) * TOK].astype(ml_dtypes.bfloat16)
        cs[0:32, TOK:] = sinf[:, r * TOK:(r + 1) * TOK].astype(ml_dtypes.bfloat16)
        cs[32:64, TOK:] = cosf[:, r * TOK:(r + 1) * TOK].astype(ml_dtypes.bfloat16)
        m["csO"] = cs
        m["gain4"] = np.ascontiguousarray(ip["L0_attn_b_q_gain"][4 * r:4 * r + 4][None, :])
        pp = np.arange(128, dtype=np.uint32)
        gi = np.zeros((128, 112), np.uint32)
        for j in range(4):
            for s_ in range(4):
                gi[:, 12 * j + s_] = 512 * r + s_ * 6144 + j * 128 + pp
                gi[:, 12 * j + 4 + s_] = 512 * r + s_ * 6144 + 2048 + j * 128 + pp
                gi[:, 12 * j + 8 + s_] = 512 * r + s_ * 6144 + 4096 + j * 128 + pp
        for j in range(4):
            for s_ in range(4):
                gi[:, 48 + 4 * j + s_] = 512 * r + s_ * 4096 + j * 128 + pp
        for jj in range(2):
            for s_ in range(4):
                gi[:, 64 + 4 * jj + s_] = 256 * r + s_ * 4096 + 2048 + jj * 128 + pp
                gi[:, 72 + 4 * jj + s_] = 256 * r + s_ * 4096 + 3072 + jj * 128 + pp
        for hdc in range(16):
            gi[:, 80 + hdc] = (hdc * 128 + pp) * 4 + r
            gi[:, 96 + hdc] = (hdc * 128 + pp) * 4 + r
        m["gidx"] = gi
        in_maps.append(m)

    if "nc" not in _CACHE:
        _CACHE["nc"] = build_program()
    nc = _CACHE["nc"]
    res = run_bass_kernel_spmd(nc, in_maps, core_ids=list(range(8)))

    out = np.empty((2, T, D), np.float32)
    for c in range(8):
        b, r = c // 4, c % 4
        o = res.results[c]["out_d"]
        out[b, r * TOK:(r + 1) * TOK, :] = o.reshape(128, 16, TOK).transpose(1, 0, 2).reshape(D, TOK).T
    return out


# version 06
# speedup vs baseline: 1.0003x; 1.0003x over previous
import numpy as np
import ml_dtypes
from collections import defaultdict
from contextlib import ExitStack

import concourse.bass as bass
import concourse.mybir as mybir
from concourse.expressions import RuntimeValue
from concourse.bass_utils import run_bass_kernel_spmd

F32 = mybir.dt.float32
BF16 = mybir.dt.bfloat16
AF = mybir.ActivationFunctionType
ALU = mybir.AluOpType
ds = bass.ds

D = 2048
T = 8192
TOK = 2048
NCK = 16
QT = 512
NQT = T // QT
NFC = 11
SCALE_A = float(1.0 / np.float64(np.sqrt(np.float32(T))))
SCALE_B = float(1.0 / np.float64(np.sqrt(np.float32(128))))
BOUND_B = float(np.sqrt(np.float64(128.0)))
LN_EPS = 1e-5
RMS_EPS = 1.1920929e-07
NEG = -1.0e9

ENGINES = ("sync", "scalar", "vector", "tensor", "gpsimd")
DBG = False


class Rec:

    def __init__(self):
        self.q = {e: [] for e in ENGINES}
        self.cnt = defaultdict(int)
        self.dynm = {}
        self.dyn = None

    def op(self, eng, fn, inc=None, n=1):
        self.q[eng].append((fn, inc, n, None))
        if inc:
            self.cnt[inc] += n
        return self.cnt[inc] if inc else None

    def dma(self, eng, out, in_, inc="q" + "X", **kw):
        return self.op(eng, lambda E, o=out, i=in_, k=kw: E.dma_start(out=o, in_=i, **k), inc=inc, n=16)

    def dmad(self, eng, fn, inc):
        self.q[eng].append((None, inc, 16, fn))
        self.cnt[inc] += 16
        return self.cnt[inc]

    def wait(self, eng, sem, val=None):
        v = self.cnt[sem] if val is None else val
        assert v <= self.cnt[sem], (sem, v, self.cnt[sem])
        self.q[eng].append((("W", sem, v), None, 0, None))

    def mark(self, sem):
        return self.cnt[sem]

    def barrier(self, engines=ENGINES, sems=None):
        if sems is None:
            sems = [s for s in self.cnt if not s.startswith("cc")]
        for e in engines:
            for s in sems:
                if self.cnt[s] > 0:
                    self.wait(e, s, self.cnt[s])


def build_program():
    nc = bass.Bass("TRN2", target_bir_lowering=False, detect_race_conditions=False)
    dr = lambda name, shape, dt, kind="ExternalInput": nc.dram_tensor(name, shape, dt, kind=kind).ap()

    x_in = dr("x_in", [128, NCK * TOK], F32)
    Wqa = dr("Wqa", [128, 32768], F32); Wka = dr("Wka", [128, 32768], F32)
    Wva = dr("Wva", [128, 32768], F32); Woa = dr("Woa", [128, 32768], F32)
    Wqb = dr("Wqb", [128, 32768], F32); Wkb = dr("Wkb", [128, 16384], F32)
    Wvb = dr("Wvb", [128, 16384], F32); Wob = dr("Wob", [128, 32768], F32)
    Aall = dr("Aall", [128, 4 * 512], F32)
    Ball = dr("Ball", [32, 6144], F32)
    rW = dr("rW", [128, 128], F32)
    w13 = dr("w13", [10, NFC, 128, 4096], F32)
    w2s = dr("w2s", [10, 128, NFC * 2048], F32)
    csO = dr("csO", [64, 2 * TOK], BF16)
    maskAi = dr("maskAi", [128, 4 * QT], BF16)
    maskBi = dr("maskBi", [128, 4 * QT], BF16)
    identF = dr("identF", [128, 128], F32)
    onesC = dr("onesC", [128, 1], F32)
    onesCb = dr("onesCb", [128, 1], BF16)
    onesR = dr("onesR", [1, 128], F32)
    UTc = dr("UTc", [16, 16], F32)
    sel8 = dr("sel8", [8, 8 * 128], BF16)
    gain4 = dr("gain4", [1, 4], F32)
    gidx = dr("gidx", [128, 112], mybir.dt.uint32)
    svec = dr("svec", [1, 4], F32)
    epsv = dr("epsv", [1, 2], F32)
    out_d = dr("out_d", [128, NCK * TOK], F32, kind="ExternalOutput")
    if DBG:
        dbg_h = dr("dbg_h", [128, NCK * TOK], BF16, kind="ExternalOutput")
        dbg_qkv = dr("dbg_qkv", [6144, 2048], BF16, kind="ExternalOutput")
        dbg_att = dr("dbg_att", [512, T], BF16, kind="ExternalOutput")
        dbg_x1 = dr("dbg_x1", [128, NCK * TOK], BF16, kind="ExternalOutput")
        dbg_x2 = dr("dbg_x2", [128, NCK * TOK], BF16, kind="ExternalOutput")
        dbg_x3 = dr("dbg_x3", [128, NCK * TOK], BF16, kind="ExternalOutput")
        dbg_qkvb = dr("dbg_qkvb", [4096, 2048], BF16, kind="ExternalOutput")
        dbg_attb = dr("dbg_attb", [512, T], BF16, kind="ExternalOutput")
        dbg_fine = dr("dbg_fine", [128, 6144], F32, kind="ExternalOutput")
        dbg_ffn = dr("dbg_ffn", [128, 8192], F32, kind="ExternalOutput")
        dbg_rt = dr("dbg_rt", [8, 4096], F32, kind="ExternalOutput")
        dbg_t2 = dr("dbg_t2", [128, 64], F32, kind="ExternalOutput")

    WqaB = dr("WqaB", [128, 32768], BF16, kind="Internal")
    WkaB = dr("WkaB", [128, 32768], BF16, kind="Internal")
    WvaB = dr("WvaB", [128, 32768], BF16, kind="Internal")
    WoaB = dr("WoaB", [128, 32768], BF16, kind="Internal")
    WqbB = dr("WqbB", [128, 32768], BF16, kind="Internal")
    WkbB = dr("WkbB", [128, 16384], BF16, kind="Internal")
    WvbB = dr("WvbB", [128, 16384], BF16, kind="Internal")
    WobB = dr("WobB", [128, 32768], BF16, kind="Internal")
    w13B = dr("w13B", [10, NFC, 128, 4096], BF16, kind="Internal")
    w2sB = dr("w2sB", [10, 128, NFC * 2048], BF16, kind="Internal")
    g1a_in = dr("g1a_in", [6144, 2048], BF16, kind="Internal")
    g1a_out = dr("g1a_out", [24576, 2048], BF16, kind="Internal")
    g1b_in = dr("g1b_in", [4096, 2048], BF16, kind="Internal")
    g1b_out = dr("g1b_out", [16384, 2048], BF16, kind="Internal")
    g2a_in = dr("g2a_in", [512, T], BF16, kind="Internal")
    g2a_out = dr("g2a_out", [2048, T], BF16, kind="Internal")
    g2b_in = dr("g2b_in", [512, T], BF16, kind="Internal")
    g2b_out = dr("g2b_out", [2048, T], BF16, kind="Internal")

    ctx = ExitStack()
    sb = lambda name, shape, dt: ctx.enter_context(nc.sbuf_tensor(name, shape, dt))
    x_sb = sb("x_sb", [128, NCK * TOK], BF16)
    h_sb = sb("h_sb", [128, NCK * TOK], BF16)
    wbuf = [sb(f"wbuf{i}", [128, 4096], BF16) for i in range(2)]
    stg = [sb(f"stg{i}", [128, 1024], BF16) for i in range(2)]
    row_sb = sb("row_sb", [1, 1024], F32)
    sq_sb = sb("sq_sb", [128, 512], F32)
    silu_sb = sb("silu_sb", [128, 1024], BF16)
    cb_sb = sb("cb_sb", [128, 2048], BF16)
    c_fm = sb("c_fm", [8, 2048], BF16)
    cs_sb = sb("cs_sb", [64, 2 * TOK], BF16)
    rt_sb = sb("rt_sb", [64, 1024], BF16)
    mask_sb = sb("mask_sb", [128, 4 * QT], BF16)
    ident_sb = sb("ident_sb", [128, 128], F32)
    sel_sb = sb("sel_sb", [8, 8 * 128], BF16)
    onesC_sb = sb("onesC_sb", [128, 1], F32)
    onesCb_sb = sb("onesCb_sb", [128, 1], BF16)
    onesR_sb = sb("onesR_sb", [1, 128], F32)
    UT_sb = sb("UT_sb", [16, 16], F32)
    rW_sb = sb("rW_sb", [128, 128], BF16)
    gb_sb = sb("gb_sb", [128, 16], F32)
    gidx_sb = sb("gidx_sb", [128, 112], mybir.dt.uint32)
    tp_sb = sb("tp_sb", [128, 16], F32)
    stm_sb = sb("stm_sb", [128, 8], F32)
    msk1_sb = sb("msk1_sb", [128, 8], F32)
    cv_sb = sb("cv_sb", [128, 16], F32)
    cvT_sb = sb("cvT_sb", [16, 128], F32)
    sv_sb = sb("sv_sb", [128, 16], F32)
    AR = sb("AR", [128, 12800], BF16)

    psA = ctx.enter_context(nc.psum_tensor("psA", [128, 2048], F32))
    psB = ctx.enter_context(nc.psum_tensor("psB", [128, 2048], F32))

    sem_names = ["qS", "qC", "qG", "pe", "act", "dve", "po",
                 "cc0", "cc1", "cc2", "cc3"]
    sems = {n: ctx.enter_context(nc.semaphore(n)) for n in sem_names}
    p = Rec()

    heV = AR[:, 0:6144]
    w2V = AR[:, 6144:12288]
    vslabV = AR[:, 0:8192]
    uV = AR[0:32, 8192:10240]
    BV = AR[0:32, 10240:12288]
    AVt = AR[:, 12288:12800]
    attnT2 = [AR[:, 0:1024], AR[:, 1024:2048]]
    S2s = [AR[:, 2048:4096].bitcast(F32), AR[:, 4608:6656].bitcast(F32)]
    rzVs = [AR[:, 4096:4608], AR[:, 6656:7168]]
    sfmV = AR[0:8, 4608:8704].bitcast(F32)
    outV = AR[:, 0:2048].bitcast(F32)

    p.dma("sync", ident_sb[:], identF, "qS")
    p.dma("sync", onesC_sb[:], onesC, "qS")
    p.dma("sync", onesCb_sb[:], onesCb, "qS")
    p.dma("sync", onesR_sb[:], onesR, "qS")
    p.dma("sync", UT_sb[:], UTc, "qS")
    p.dma("sync", sel_sb[:], sel8, "qS")
    p.dma("sync", cs_sb[:], csO, "qS")
    p.dma("sync", row_sb[0:1, 0:4], gain4, "qS")
    p.dma("sync", row_sb[0:1, 4:8], svec, "qS")
    p.dma("sync", gb_sb[0:1, 12:14], epsv, "qS", allow_slow_non_contiguous=True)
    p.dma("gpsimd", rW_sb[:], rW, "qG")
    p.dma("gpsimd", x_sb[:], x_in, "qG")
    p.dma("sync", gidx_sb[:], gidx, "qS")
    p.barrier()
    p.op("tensor", lambda E: E.matmul(psA[:, 0:8], onesR_sb[:], row_sb[0:1, 0:8],
                                      start=True, stop=True), inc="pe")
    p.wait("vector", "pe")
    p.op("vector", lambda E: E.tensor_copy(gb_sb[:, 0:8], psA[:, 0:8]), inc="dve")
    p.wait("scalar", "dve")
    p.op("scalar", lambda E: E.activation(gb_sb[:, 8:12], gb_sb[:, 0:4], AF.Abs), inc="act")
    p.wait("vector", "act")
    p.op("vector", lambda E: E.tensor_scalar_mul(gb_sb[:, 8:12], gb_sb[:, 8:12], -BOUND_B), inc="dve")
    p.barrier()

    wcast = {}
    for nm, dst, src in [("Wqa", WqaB, Wqa), ("Wka", WkaB, Wka), ("Wva", WvaB, Wva),
                         ("Woa", WoaB, Woa)]:
        p.dma("gpsimd", dst, src, "qG")
        wcast[nm] = p.mark("qG")
    for e in range(10):
        p.dma("gpsimd", w13B[e], w13[e], "qG")
        wcast[f"w13_{e}"] = p.mark("qG")
        p.dma("gpsimd", w2sB[e], w2s[e], "qG")
        wcast[f"w2_{e}"] = p.mark("qG")
    for nm, dst, src in [("Wqb", WqbB, Wqb), ("Wkb", WkbB, Wkb), ("Wvb", WvbB, Wvb),
                         ("Wob", WobB, Wob)]:
        p.dma("gpsimd", dst, src, "qG")
        wcast[nm] = p.mark("qG")

    def emit_ln():
        for tq4 in range(4):
            c0 = tq4 * 512
            for dc in range(NCK):
                p.wait("scalar", "pe")
                p.op("scalar", lambda E, d=dc, c0=c0: E.activation(
                    sq_sb[:], x_sb[:, d * TOK + c0:d * TOK + c0 + 512], AF.Square), inc="act")
                p.wait("tensor", "act")
                p.op("tensor", lambda E, d=dc, c0=c0: E.matmul(
                    psA[0:1, 0:512], onesCb_sb[:], x_sb[:, d * TOK + c0:d * TOK + c0 + 512],
                    start=(d == 0), stop=(d == NCK - 1)))
                p.op("tensor", lambda E, d=dc: E.matmul(
                    psA[0:1, 512:1024], onesC_sb[:], sq_sb[:, 0:512],
                    start=(d == 0), stop=(d == NCK - 1)), inc="pe")
            p.wait("vector", "pe")
            mu = row_sb[0:1, 0:512]
            wk = row_sb[0:1, 512:1024]
            p.op("vector", lambda E: E.tensor_scalar_mul(mu, psA[0:1, 0:512], 1.0 / D))
            p.op("vector", lambda E: E.tensor_mul(wk, mu, mu))
            p.op("vector", lambda E: E.scalar_tensor_tensor(
                wk, psA[0:1, 512:1024], 1.0 / D, wk, ALU.mult, ALU.subtract), inc="dve")
            p.wait("scalar", "dve")
            p.op("scalar", lambda E: E.activation(wk, wk, AF.Sqrt, bias=gb_sb[0:1, 12:13]), inc="act")
            p.wait("vector", "act")
            p.op("vector", lambda E: E.reciprocal(wk, wk))
            p.op("vector", lambda E: E.scalar_tensor_tensor(mu, mu, -1.0, wk, ALU.mult, ALU.mult), inc="dve")
            p.wait("tensor", "dve")
            p.op("tensor", lambda E: E.matmul(psB[:, 0:512], onesR_sb[:], wk, start=True, stop=True))
            p.op("tensor", lambda E: E.matmul(psB[:, 512:1024], onesR_sb[:], mu, start=True, stop=True), inc="pe")
            p.wait("vector", "pe")
            for dc in range(NCK):
                p.op("vector", lambda E, d=dc, c0=c0: E.tensor_mul(
                    h_sb[:, d * TOK + c0:d * TOK + c0 + 512],
                    x_sb[:, d * TOK + c0:d * TOK + c0 + 512], psB[:, 0:512]))
                p.op("vector", lambda E, d=dc, c0=c0: E.tensor_add(
                    h_sb[:, d * TOK + c0:d * TOK + c0 + 512],
                    h_sb[:, d * TOK + c0:d * TOK + c0 + 512], psB[:, 512:1024]),
                    inc=("dve" if dc == NCK - 1 else None))
            p.barrier(("tensor", "scalar"), ["dve"])
        p.barrier()

    def load_AB(acol, bcol, blen):
        p.dma("gpsimd", AVt, Aall[:, acol:acol + 512], "qG")
        p.dma("gpsimd", BV[:, 0:blen], Ball[:, bcol:bcol + blen], "qG")

    def emit_u():
        p.barrier(("tensor",), ["qG"])
        for dc in range(NCK):
            for g in range(4):
                p.op("tensor", lambda E, d=dc, g=g: E.matmul(
                    psA[0:32, g * 512:(g + 1) * 512], AVt[:, d * 32:(d + 1) * 32],
                    h_sb[:, d * TOK + g * 512:d * TOK + g * 512 + 512],
                    start=(d == 0), stop=(d == NCK - 1)),
                    inc=("pe" if (dc == NCK - 1 and g == 3) else None))
        p.wait("scalar", "pe")
        p.op("scalar", lambda E: E.copy(uV[:, 0:2048], psA[0:32, 0:2048]), inc="act")
        p.wait("tensor", "act")

    def fm_proj(Wt, noc, dst, dst_rows, wmark, lora=False, scale_col=None, rope=False):
        pe_done = {}
        cons = {}
        store_done = {}
        qc_m = {}
        p.wait("scalar", "qG", wmark)
        p.dma("scalar", wbuf[0][:, 0:2048], Wt[:, 0:2048], "qC")
        qc_m[0] = p.mark("qC")
        for och in range(noc):
            wb = wbuf[och % 2]
            if och + 1 < noc:
                if och >= 1:
                    p.wait("scalar", "pe", pe_done[och - 1])
                p.dma("scalar", wbuf[(och + 1) % 2][:, 0:2048],
                      Wt[:, (och + 1) * 2048:(och + 2) * 2048], "qC")
                qc_m[och + 1] = p.mark("qC")
            p.wait("tensor", "qC", qc_m[och])
            for th in range(2):
                u = och * 2 + th
                ps = psA if u % 2 == 0 else psB
                c0 = th * 1024
                if u >= 2:
                    p.wait("tensor", *cons[u - 2])
                for dc in range(NCK):
                    for g in range(2):
                        p.op("tensor", lambda E, d=dc, g=g, ps=ps, c0=c0, wb=wb: E.matmul(
                            ps[:, g * 512:(g + 1) * 512], wb[:, d * 128:(d + 1) * 128],
                            h_sb[:, d * TOK + c0 + g * 512:d * TOK + c0 + g * 512 + 512],
                            start=(d == 0), stop=(d == NCK - 1)))
                if lora:
                    for g in range(2):
                        p.op("tensor", lambda E, g=g, ps=ps, c0=c0, och=och: E.matmul(
                            ps[:, 1024 + g * 512:1024 + (g + 1) * 512],
                            BV[:, och * 128:(och + 1) * 128],
                            uV[:, c0 + g * 512:c0 + g * 512 + 512],
                            start=True, stop=True),
                            inc=("pe" if g == 1 else None))
                else:
                    p.q["tensor"][-1] = (p.q["tensor"][-1][0], "pe", 1, None)
                    p.cnt["pe"] += 1
                if th == 1:
                    pe_done[och] = p.mark("pe")
                p.wait("scalar", "pe")
                st_ = stg[u % 2]
                if u >= 2:
                    p.wait("scalar", "qS", store_done[u - 2])
                if scale_col is not None:
                    p.op("scalar", lambda E, ps=ps, st=st_, sc=scale_col: E.activation(
                        st[:], ps[:, 0:1024], AF.Copy, scale=gb_sb[:, sc:sc + 1]), inc="act")
                else:
                    p.op("scalar", lambda E, ps=ps, st=st_: E.copy(st[:], ps[:, 0:1024]), inc="act")
                if lora:
                    p.wait("vector", "act")
                    p.op("vector", lambda E, ps=ps, st=st_: E.tensor_add(st[:], st[:], ps[:, 1024:2048]), inc="dve")
                    cons[u] = ("dve", p.mark("dve"))
                else:
                    cons[u] = ("act", p.mark("act"))
                if rope:
                    p.wait("gpsimd", *cons[u])
                    t_b0 = rt_sb[0:32, 0:1024]
                    t_b32 = rt_sb[32:64, 0:1024]
                    co_b0 = cs_sb[0:32, c0:c0 + 1024]
                    si_b32 = cs_sb[32:64, c0:c0 + 1024]
                    si_b0 = cs_sb[0:32, TOK + c0:TOK + c0 + 1024]
                    co_b32 = cs_sb[32:64, TOK + c0:TOK + c0 + 1024]
                    q1 = st_[0:32, :]
                    q2 = st_[32:64, :]
                    p.op("gpsimd", lambda E, q1=q1, si=si_b0, t=t_b32: E.tensor_mul(t, q1, si))
                    p.op("gpsimd", lambda E, q1=q1, co=co_b0: E.tensor_mul(q1, q1, co))
                    p.op("gpsimd", lambda E, q2=q2, si=si_b32, t=t_b0: E.tensor_mul(t, q2, si))
                    p.op("gpsimd", lambda E, q1=q1, t=t_b0: E.tensor_add(q1, q1, t))
                    p.op("gpsimd", lambda E, q2=q2, co=co_b32: E.tensor_mul(q2, q2, co))
                    p.op("gpsimd", lambda E, q2=q2, t=t_b32: E.tensor_sub(q2, q2, t), inc="po")
                    cons[u] = ("po", p.mark("po"))
                p.wait("sync", *cons[u])
                p.dma("sync", dst[dst_rows + och * 128:dst_rows + (och + 1) * 128, c0:c0 + 1024], st_[:], "qS")
                store_done[u] = p.mark("qS")
        p.barrier(("gpsimd", "vector", "scalar"), ["qS"])

    IOA = bass.IndirectOffsetOnAxis

    def gather_rows(dst_cols, src, idxcol):
        p.dmad("gpsimd", lambda E, dyn, dc=dst_cols, src=src, ic=idxcol: E.indirect_dma_start(
            h_sb[:, dc:dc + 2048], None, src, IOA(gidx_sb[:, ic:ic + 1], 0)), "qG")

    def v_transpose(n_tiles=64):
        marks = {}
        for t in range(n_tiles):
            stf = sq_sb[:, (t % 2) * 128:(t % 2) * 128 + 128]
            pslot = psA[:, 1024 + (t % 2) * 512:1024 + (t % 2) * 512 + 128]
            if t >= 2:
                p.wait("scalar", "pe", marks[t - 2][0])
            p.op("scalar", lambda E, t=t, stf=stf: E.copy(
                stf, h_sb[:, 16384 + t * 128:16384 + (t + 1) * 128]), inc="act")
            p.wait("tensor", "act")
            if t >= 2:
                p.wait("tensor", "dve", marks[t - 2][1])
            p.op("tensor", lambda E, stf=stf, ps=pslot: E.transpose(ps, stf, ident_sb[:]), inc="pe")
            p.wait("vector", "pe")
            p.op("vector", lambda E, t=t, ps=pslot: E.tensor_copy(
                h_sb[:, 24576 + t * 128:24576 + (t + 1) * 128], ps), inc="dve")
            marks[t] = (p.mark("pe"), p.mark("dve"))
        p.barrier(("tensor", "scalar", "gpsimd"), ["dve"])

    def collective(cin, cout, ccname):
        p.barrier(("gpsimd",), ["qS", "qG", "dve", "act", "po", "pe"])
        p.op("gpsimd", lambda E, i=cin, o=cout: E.collective_compute(
            "AllGather", ALU.bypass, replica_groups=[[0, 1, 2, 3], [4, 5, 6, 7]],
            ins=[i.opt()], outs=[o.opt()]), inc=ccname, n=1)
        for e in ENGINES:
            p.wait(e, ccname, 1)

    def head_norm(colbase, gaincol):
        for blk in range(16):
            c0 = colbase + blk * 512
            p.wait("scalar", "pe")
            p.op("scalar", lambda E, c0=c0: E.activation(
                sq_sb[:], h_sb[:, c0:c0 + 512], AF.Square), inc="act")
            p.wait("tensor", "act")
            p.op("tensor", lambda E: E.matmul(
                psA[0:1, 0:512], onesC_sb[:], sq_sb[:], start=True, stop=True), inc="pe")
            p.wait("scalar", "pe")
            p.op("scalar", lambda E: E.activation(row_sb[0:1, 0:512], psA[0:1, 0:512],
                                                  AF.Sqrt, bias=gb_sb[0:1, 13:14], scale=1.0 / 128.0), inc="act")
            p.wait("vector", "act")
            p.op("vector", lambda E: E.reciprocal(row_sb[0:1, 0:512], row_sb[0:1, 0:512]), inc="dve")
            p.wait("tensor", "dve")
            p.op("tensor", lambda E: E.matmul(
                psA[:, 1024:1536], onesR_sb[:], row_sb[0:1, 0:512], start=True, stop=True), inc="pe")
            p.wait("vector", "pe")
            if gaincol is not None:
                p.op("vector", lambda E, c0=c0, gc=gaincol: E.scalar_tensor_tensor(
                    h_sb[:, c0:c0 + 512], h_sb[:, c0:c0 + 512], gb_sb[:, gc:gc + 1],
                    psA[:, 1024:1536], ALU.mult, ALU.mult), inc="dve")
            else:
                p.op("vector", lambda E, c0=c0: E.tensor_mul(
                    h_sb[:, c0:c0 + 512], h_sb[:, c0:c0 + 512], psA[:, 1024:1536]), inc="dve")
            p.wait("tensor", "dve")
            p.wait("scalar", "dve")

    def head_attn(is_b, j, g2in):
        qh = h_sb[:, 0:8192]
        kh = h_sb[:, 8192:16384]
        vhf = h_sb[:, 16384:24576]
        vh = h_sb[:, 24576:32768]
        if not is_b:
            for g in range(16):
                p.op("vector", lambda E, g=g: E.reduce_sum(
                    cv_sb[:, g:g + 1], vhf[:, g * 512:(g + 1) * 512], axis=mybir.AxisListType.X),
                    inc="dve")
                p.wait("vector", "dve")
            p.wait("tensor", "dve")
            p.op("tensor", lambda E: E.transpose(psA[0:16, 0:128], cv_sb[:], ident_sb[:]), inc="pe")
            p.wait("vector", "pe")
            p.op("vector", lambda E: E.tensor_copy(cvT_sb[:], psA[0:16, 0:128]), inc="dve")
            p.wait("tensor", "dve")
            p.op("tensor", lambda E: E.matmul(psA[:, 0:16], cvT_sb[:], UT_sb[:],
                                              start=True, stop=True), inc="pe")
            p.wait("vector", "pe")
            p.op("vector", lambda E: E.tensor_copy(sv_sb[:], psA[:, 0:16]), inc="dve")
            p.wait("tensor", "dve")

        avs = [psB[:, 0:512], psB[:, 512:1024]]
        zs = [psB[0:1, 1024:1536], psB[0:1, 1536:2048]]
        bcs = [psB[:, 1024:1536], psB[:, 1536:2048]]
        rows = [row_sb[0:1, 0:512], row_sb[0:1, 512:1024]]
        exp_m = {}; av_m = {}; s2_m = {}
        zmm_m = {}; recip_m = {}; rzcopy_m = {}; norm_m = {}; store_m = {}
        av_last = {}
        gp_box = [0]

        def tail1(q):
            sl = q % 2
            if not is_b:
                p.wait("vector", "pe", av_last[q])
                p.op("vector", lambda E, sl=sl, q=q: E.tensor_scalar_add(
                    avs[sl], avs[sl], sv_sb[:, q:q + 1]), inc="dve")
            if q >= 2:
                p.wait("tensor", "act", rzcopy_m[q - 2])
            p.wait("tensor", "dve")
            for g in range(2):
                p.op("tensor", lambda E, g=g, sl=sl: E.matmul(
                    zs[sl], onesC_sb[:], S2s[sl][:, g * 512:(g + 1) * 512],
                    start=(g == 0), stop=(g == 1)), inc=("pe" if g == 1 else None))
            zmm_m[q] = p.mark("pe")
            p.wait("vector", "pe", zmm_m[q])
            if not is_b and q < 15:
                p.op("vector", lambda E, sl=sl, q=q: E.tensor_scalar_add(
                    zs[sl], zs[sl], float(128 * (60 - 4 * q))), inc="dve")
                p.wait("vector", "dve")
            p.op("vector", lambda E, sl=sl: E.reciprocal(rows[sl], zs[sl]), inc="dve")
            recip_m[q] = p.mark("dve")

        def tail2(q):
            sl = q % 2
            if q >= 2:
                p.wait("tensor", "act", rzcopy_m[q - 2])
            p.wait("tensor", "dve", recip_m[q])
            p.op("tensor", lambda E, sl=sl: E.matmul(
                bcs[sl], onesR_sb[:], rows[sl], start=True, stop=True), inc="pe")
            bc_m = p.mark("pe")
            p.wait("scalar", "pe", bc_m)
            if q >= 2:
                p.wait("scalar", "dve", norm_m[q - 2])
            p.op("scalar", lambda E, sl=sl: E.copy(rzVs[sl][:, 0:512], bcs[sl]), inc="act")
            rzcopy_m[q] = p.mark("act")
            p.wait("vector", "act", rzcopy_m[q])
            if q >= 2:
                p.wait("vector", "qS", store_m[q - 2])
            p.op("vector", lambda E, sl=sl: E.tensor_mul(
                stg[sl][:, 0:512], avs[sl], rzVs[sl][:, 0:512]), inc="dve")
            norm_m[q] = p.mark("dve")
            p.wait("sync", "dve", norm_m[q])
            p.dma("sync", g2in[j * 128:(j + 1) * 128, q * QT:(q + 1) * QT], stg[sl][:, 0:512], "qS")
            store_m[q] = p.mark("qS")

        for q in range(NQT):
            npair = 2 * (q + 1)
            sl = q % 2
            if q >= 2:
                p.wait("vector", "pe", zmm_m[q - 2])
            p.op("vector", lambda E, sl=sl: E.memset(S2s[sl][:], 0.0), inc="dve")
            for cp in range(npair):
                gp = gp_box[0]
                slot = gp % 2
                at = attnT2[slot]
                scp = psA[:, slot * 1024:(slot + 1) * 1024]
                if gp >= 2:
                    p.wait("tensor", "act", exp_m[gp - 2])
                for ci in range(2):
                    c = 2 * cp + ci
                    p.op("tensor", lambda E, c=c, q=q, sc=scp, ci=ci: E.matmul(
                        sc[:, ci * 512:(ci + 1) * 512], kh[:, c * 128:(c + 1) * 128],
                        qh[:, q * QT:(q + 1) * QT], start=True, stop=True),
                        inc=("pe" if ci == 1 else None))
                pe_m = p.mark("pe")
                dve_need = False
                for ci in range(2):
                    c = 2 * cp + ci
                    d_ = c - 4 * q
                    if 0 <= d_ <= 3:
                        if not dve_need:
                            p.wait("vector", "pe", pe_m)
                            dve_need = True
                        op_ = "tensor_add" if is_b else "tensor_mul"
                        p.op("vector", lambda E, sc=scp, ci=ci, d_=d_, op_=op_: getattr(E, op_)(
                            sc[:, ci * 512:(ci + 1) * 512], sc[:, ci * 512:(ci + 1) * 512],
                            mask_sb[:, d_ * QT:(d_ + 1) * QT]), inc="dve")
                if dve_need:
                    p.wait("scalar", "dve")
                else:
                    p.wait("scalar", "pe", pe_m)
                if gp >= 2:
                    p.wait("scalar", "pe", av_m[gp - 2])
                    p.wait("scalar", "dve", s2_m[gp - 2])
                if is_b:
                    p.op("scalar", lambda E, sc=scp, at=at, j=j: E.activation(
                        at, sc, AF.Exp, bias=gb_sb[:, 8 + j:9 + j], scale=SCALE_B), inc="act")
                else:
                    p.op("scalar", lambda E, sc=scp, at=at: E.activation(
                        at, sc, AF.Exp, scale=SCALE_A), inc="act")
                exp_m[gp] = p.mark("act")
                p.wait("vector", "act", exp_m[gp])
                p.op("vector", lambda E, at=at, sl=sl: E.tensor_add(S2s[sl][:], S2s[sl][:], at), inc="dve")
                s2_m[gp] = p.mark("dve")
                p.wait("tensor", "act", exp_m[gp])
                if cp == 0 and q >= 2:
                    p.wait("tensor", "dve", norm_m[q - 2])
                for ci in range(2):
                    c = 2 * cp + ci
                    p.op("tensor", lambda E, c=c, at=at, ci=ci, sl=sl,
                         s=(cp == 0 and ci == 0), st=(cp == npair - 1 and ci == 1): E.matmul(
                        avs[sl], vh[:, c * 128:(c + 1) * 128], at[:, ci * 512:(ci + 1) * 512],
                        start=s, stop=st), inc=("pe" if ci == 1 else None))
                av_m[gp] = p.mark("pe")
                gp_box[0] = gp + 1
                if q >= 1 and cp == 1:
                    tail1(q - 1)
                if q >= 1 and cp == min(4, npair - 1):
                    tail2(q - 1)
            av_last[q] = av_m[gp_box[0] - 1]
        tail1(NQT - 1)
        tail2(NQT - 1)

    def o_proj(g2out, Wot, lora, scale_col, aocol, wmark):
        for hdc in range(NCK):
            p.dmad("gpsimd", lambda E, dyn, hd=hdc, ic=aocol + hdc: E.indirect_dma_start(
                h_sb[:, hd * TOK:(hd + 1) * TOK], None,
                g2out.rearrange("r (b x) -> (r b) x", b=4),
                IOA(gidx_sb[:, ic:ic + 1], 0)), "qG")
        p.barrier(("tensor", "scalar", "vector", "gpsimd"), ["qG"])
        if lora:
            load_AB(1536, 4096, 2048)
            emit_u()
        pe_done = {}
        cons = {}
        qc_m = {}
        p.wait("scalar", "qG", wmark)
        p.dma("scalar", wbuf[0][:, 0:2048], Wot[:, 0:2048], "qC")
        qc_m[0] = p.mark("qC")
        for och in range(NCK):
            wb = wbuf[och % 2]
            if och + 1 < NCK:
                if och >= 1:
                    p.wait("scalar", "pe", pe_done[och - 1])
                p.dma("scalar", wbuf[(och + 1) % 2][:, 0:2048],
                      Wot[:, (och + 1) * 2048:(och + 2) * 2048], "qC")
                qc_m[och + 1] = p.mark("qC")
            p.wait("tensor", "qC", qc_m[och])
            for th in range(2):
                u = och * 2 + th
                ps = psA if u % 2 == 0 else psB
                c0 = th * 1024
                if u >= 2:
                    p.wait("tensor", *cons[u - 2])
                for dc in range(NCK):
                    for g in range(2):
                        p.op("tensor", lambda E, d=dc, g=g, ps=ps, c0=c0, wb=wb: E.matmul(
                            ps[:, g * 512:(g + 1) * 512], wb[:, d * 128:(d + 1) * 128],
                            h_sb[:, d * TOK + c0 + g * 512:d * TOK + c0 + g * 512 + 512],
                            start=(d == 0), stop=(d == NCK - 1)))
                if lora:
                    for g in range(2):
                        p.op("tensor", lambda E, g=g, ps=ps, c0=c0, och=och: E.matmul(
                            ps[:, 1024 + g * 512:1024 + (g + 1) * 512],
                            BV[:, och * 128:(och + 1) * 128],
                            uV[:, c0 + g * 512:c0 + g * 512 + 512],
                            start=True, stop=True), inc=("pe" if g == 1 else None))
                else:
                    p.q["tensor"][-1] = (p.q["tensor"][-1][0], "pe", 1, None)
                    p.cnt["pe"] += 1
                if th == 1:
                    pe_done[och] = p.mark("pe")
                p.wait("scalar", "pe")
                st_ = stg[u % 2]
                if u >= 2:
                    p.wait("scalar", *cons[u - 2])
                if scale_col is not None:
                    p.op("scalar", lambda E, ps=ps, st=st_, sc=scale_col: E.activation(
                        st[:], ps[:, 0:1024], AF.Copy, scale=gb_sb[:, sc:sc + 1]), inc="act")
                else:
                    p.op("scalar", lambda E, ps=ps, st=st_: E.copy(st[:], ps[:, 0:1024]), inc="act")
                p.wait("vector", "act")
                if lora:
                    p.op("vector", lambda E, ps=ps, st=st_: E.tensor_add(st[:], st[:], ps[:, 1024:2048]))
                p.op("vector", lambda E, st=st_, och=och, c0=c0: E.tensor_add(
                    x_sb[:, och * TOK + c0:och * TOK + c0 + 1024],
                    x_sb[:, och * TOK + c0:och * TOK + c0 + 1024], st[:]), inc="dve")
                cons[u] = ("dve", p.mark("dve"))
        p.barrier()

    def attn_sublayer(is_b):
        emit_ln()
        if DBG and not is_b and not hasattr(attn_sublayer, "_h_dumped"):
            attn_sublayer._h_dumped = True
            p.barrier()
            p.dma("sync", dbg_h, h_sb[:], "qS")
            p.barrier()
        if not is_b:
            p.dma("sync", mask_sb[:], maskAi, "qS")
            fm_proj(WqaB, 16, g1a_in, 0, wcast["Wqa"])
            fm_proj(WkaB, 16, g1a_in, 2048, wcast["Wka"])
            fm_proj(WvaB, 16, g1a_in, 4096, wcast["Wva"])
            collective(g1a_in, g1a_out, "cc0")
            for j in range(4):
                for s in range(4):
                    gather_rows(s * 2048, g1a_out, 12 * j + s)
                    gather_rows(8192 + s * 2048, g1a_out, 12 * j + 4 + s)
                    gather_rows(16384 + s * 2048, g1a_out, 12 * j + 8 + s)
                p.barrier(("tensor", "scalar", "vector", "gpsimd"), ["qG"])
                v_transpose()
                head_attn(False, j, g2a_in)
                p.barrier()
            collective(g2a_in, g2a_out, "cc1")
            o_proj(g2a_out, WoaB, False, None, 80, wcast["Woa"])
        else:
            p.dma("sync", mask_sb[:], maskBi, "qS")
            load_AB(0, 0, 2048)
            emit_u()
            fm_proj(WqbB, 16, g1b_in, 0, wcast["Wqb"], lora=True, scale_col=4, rope=True)
            load_AB(512, 2048, 1024)
            emit_u()
            fm_proj(WkbB, 8, g1b_in, 2048, wcast["Wkb"], lora=True, scale_col=5, rope=True)
            load_AB(1024, 3072, 1024)
            emit_u()
            fm_proj(WvbB, 8, g1b_in, 3072, wcast["Wvb"], lora=True, scale_col=6)
            collective(g1b_in, g1b_out, "cc2")
            for jj in range(2):
                for s in range(4):
                    gather_rows(8192 + s * 2048, g1b_out, 64 + 4 * jj + s)
                    gather_rows(16384 + s * 2048, g1b_out, 72 + 4 * jj + s)
                p.barrier(("tensor", "scalar", "vector", "gpsimd"), ["qG"])
                v_transpose()
                head_norm(8192, None)
                for j2 in range(2):
                    j = jj * 2 + j2
                    for s in range(4):
                        gather_rows(s * 2048, g1b_out, 48 + 4 * j + s)
                    p.barrier(("tensor", "scalar", "vector", "gpsimd"), ["qG"])
                    head_norm(0, j)
                    head_attn(True, j, g2b_in)
                    p.barrier()
            collective(g2b_in, g2b_out, "cc3")
            o_proj(g2b_out, WobB, True, 7, 96, wcast["Wob"])

    def emit_router():
        for dc in range(NCK):
            for g in range(4):
                p.op("tensor", lambda E, d=dc, g=g: E.matmul(
                    psB[0:8, g * 512:(g + 1) * 512], rW_sb[:, d * 8:(d + 1) * 8],
                    h_sb[:, d * TOK + g * 512:d * TOK + g * 512 + 512],
                    start=(d == 0), stop=(d == NCK - 1)),
                    inc=("pe" if (dc == NCK - 1 and g == 3) else None))
        p.wait("scalar", "pe")
        p.op("scalar", lambda E: E.activation(sfmV[:, 0:2048], psB[0:8, 0:2048], AF.Sigmoid), inc="act")
        for tc in range(NCK):
            p.wait("tensor", "act")
            if tc >= 1:
                p.wait("tensor", "dve")
            p.op("tensor", lambda E, tc=tc: E.transpose(
                psA[:, 1536:1544], sfmV[0:8, tc * 128:(tc + 1) * 128], ident_sb[0:8, 0:8]), inc="pe")
            p.wait("vector", "pe")
            m1 = tp_sb[:, 0:1]; m2 = tp_sb[:, 1:2]; r_ = tp_sb[:, 3:4]
            chain = [
                lambda E: E.tensor_copy(stm_sb[:], psA[:, 1536:1544]),
                lambda E: E.reduce_max(m1, stm_sb[:], axis=mybir.AxisListType.X),
                lambda E: E.tensor_scalar(msk1_sb[:], stm_sb[:], m1, None, ALU.subtract),
                lambda E: E.tensor_single_scalar(msk1_sb[:], msk1_sb[:], 0.0, ALU.is_equal),
                lambda E: E.scalar_tensor_tensor(msk1_sb[:], msk1_sb[:], NEG, stm_sb[:], ALU.mult, ALU.add),
                lambda E: E.reduce_max(m2, msk1_sb[:], axis=mybir.AxisListType.X),
                lambda E: E.tensor_add(tp_sb[:, 2:3], m1, m2),
                lambda E: E.reciprocal(r_, tp_sb[:, 2:3]),
                lambda E: E.tensor_single_scalar(msk1_sb[:], stm_sb[:], m2, ALU.is_ge),
                lambda E: E.scalar_tensor_tensor(
                    ctm_holder[0][:], stm_sb[:], r_, msk1_sb[:], ALU.mult, ALU.mult),
            ]
            for fn_ in chain:
                p.op("vector", fn_, inc="dve")
                p.wait("vector", "dve")
            if DBG and tc == 0 and not getattr(emit_router, "_td", False):
                emit_router._td = True
                p.barrier()
                p.dma("gpsimd", dbg_t2[:, 0:8], stm_sb[:], "qG", allow_slow_non_contiguous=True)
                p.dma("gpsimd", dbg_t2[:, 8:16], msk1_sb[:], "qG", allow_slow_non_contiguous=True)
                p.dma("gpsimd", dbg_t2[:, 16:32], tp_sb[:], "qG", allow_slow_non_contiguous=True)
                p.dma("gpsimd", dbg_t2[:, 32:40], ctm_holder[0][:], "qG", allow_slow_non_contiguous=True)
                p.barrier()
            p.wait("tensor", "dve")
            p.op("tensor", lambda E: E.transpose(psA[0:8, 1536:1664], ctm_holder[0][:], ident_sb[:]), inc="pe")
            p.wait("vector", "pe")
            p.op("vector", lambda E, tc=tc: E.tensor_copy(c_fm[:, tc * 128:(tc + 1) * 128], psA[0:8, 1536:1664]), inc="dve")
        p.barrier(("tensor", "scalar", "gpsimd"), ["dve"])
        if DBG and not getattr(emit_router, "_rd", False):
            emit_router._rd = True
            p.barrier()
            p.dma("gpsimd", dbg_rt[0:8, 0:2048], sfmV[:, 0:2048], "qG")
            p.dma("gpsimd", dbg_rt[0:8, 2048:4096], c_fm[:], "qG")
            p.barrier()

    def emit_ffn(final):
        emit_ln()
        emit_router()
        quarters = [(0, 3), (3, 6), (6, 9), (9, 11)]
        for Ei in range(10):
            routed = Ei < 8
            if routed:
                p.wait("tensor", "dve")
                for g in range(4):
                    p.op("tensor", lambda E, g=g, Ei=Ei: E.matmul(
                        psA[:, g * 512:(g + 1) * 512], sel_sb[:, Ei * 128:(Ei + 1) * 128],
                        c_fm[:, g * 512:(g + 1) * 512], start=True, stop=True),
                        inc=("pe" if g == 3 else None))
                p.wait("scalar", "pe")
                p.op("scalar", lambda E: E.copy(cb_sb[:], psA[:, 0:2048]), inc="act")
                p.wait("vector", "act")
                p.wait("tensor", "act")
            pe_fc = {}
            qc_fc = {}
            p.wait("scalar", "qG", wcast[f"w13_{Ei}"])
            p.wait("scalar", "qG", wcast[f"w2_{Ei}"])
            p.dma("scalar", wbuf[0][:], w13B[Ei, 0], "qC")
            qc_fc[0] = p.mark("qC")
            for (f0, f1) in quarters:
                nf = f1 - f0
                p.wait("scalar", "pe")
                p.dma("scalar", w2V[:, 0:nf * 2048], w2sB[Ei][:, f0 * 2048:f1 * 2048], "qC")
                w2_m = p.mark("qC")
                he_marks = []
                for fi in range(nf):
                    fc = f0 + fi
                    wb = wbuf[fc % 2]
                    if fc + 1 < NFC:
                        if fc - 1 in pe_fc:
                            p.wait("scalar", "pe", pe_fc[fc - 1])
                        p.dma("scalar", wbuf[(fc + 1) % 2][:], w13B[Ei, fc + 1], "qC")
                        qc_fc[fc + 1] = p.mark("qC")
                    p.wait("tensor", "qC", qc_fc[fc])
                    for th in range(2):
                        u = fi * 2 + th
                        ps = psA if u % 2 == 0 else psB
                        c0 = th * 1024
                        if u >= 2:
                            p.wait("tensor", "dve", he_marks[u - 2])
                        for dc in range(NCK):
                            for g in range(2):
                                p.op("tensor", lambda E, d=dc, g=g, ps=ps, c0=c0, wb=wb: E.matmul(
                                    ps[:, g * 512:(g + 1) * 512], wb[:, d * 128:(d + 1) * 128],
                                    h_sb[:, d * TOK + c0 + g * 512:d * TOK + c0 + g * 512 + 512],
                                    start=(d == 0), stop=(d == NCK - 1)))
                        for dc in range(NCK):
                            for g in range(2):
                                p.op("tensor", lambda E, d=dc, g=g, ps=ps, c0=c0, wb=wb: E.matmul(
                                    ps[:, 1024 + g * 512:1024 + (g + 1) * 512],
                                    wb[:, 2048 + d * 128:2048 + (d + 1) * 128],
                                    h_sb[:, d * TOK + c0 + g * 512:d * TOK + c0 + g * 512 + 512],
                                    start=(d == 0), stop=(d == NCK - 1)),
                                    inc=("pe" if (dc == NCK - 1 and g == 1) else None))
                        if th == 1:
                            pe_fc[fc] = p.mark("pe")
                        p.wait("scalar", "pe")
                        p.wait("scalar", "dve")
                        p.op("scalar", lambda E, ps=ps: E.activation(
                            silu_sb[:], ps[:, 0:1024], AF.Silu), inc="act")
                        p.wait("vector", "act")
                        he_sl = heV[:, fi * 2048 + c0:fi * 2048 + c0 + 1024]
                        p.op("vector", lambda E, ps=ps, he=he_sl: E.tensor_mul(he, silu_sb[:], ps[:, 1024:2048]))
                        if routed:
                            p.op("vector", lambda E, he=he_sl, c0=c0: E.tensor_mul(
                                he, he, cb_sb[:, c0:c0 + 1024]), inc="dve")
                        else:
                            p.q["vector"][-1] = (p.q["vector"][-1][0], "dve", 1, None)
                            p.cnt["dve"] += 1
                        he_marks.append(p.mark("dve"))
                if DBG and Ei == 0 and f0 == 0 and not getattr(emit_ffn, "_fd", False):
                    emit_ffn._fd = True
                    p.barrier()
                    p.dma("gpsimd", dbg_ffn[:, 0:6144], heV, "qG")
                    p.dma("gpsimd", dbg_ffn[:, 6144:8192], cb_sb[:], "qG")
                    p.barrier()
                p.wait("tensor", "qC", w2_m)
                p.wait("tensor", "dve")
                for th in range(2):
                    c0 = th * 1024
                    for op_ in range(8):
                        u = th * 8 + op_
                        ps = psA if u % 2 == 0 else psB
                        if u >= 2:
                            p.wait("tensor", "dve")
                        for fi in range(nf):
                            for oc in range(2):
                                och = op_ * 2 + oc
                                for g in range(2):
                                    p.op("tensor", lambda E, fi=fi, oc=oc, och=och, g=g, ps=ps, c0=c0, s=(fi == 0), st=(fi == nf - 1): E.matmul(
                                        ps[:, oc * 1024 + g * 512:oc * 1024 + g * 512 + 512],
                                        w2V[:, fi * 2048 + och * 128:fi * 2048 + (och + 1) * 128],
                                        heV[:, fi * 2048 + c0 + g * 512:fi * 2048 + c0 + g * 512 + 512],
                                        start=s, stop=st),
                                        inc=("pe" if (fi == nf - 1 and oc == 1 and g == 1) else None))
                        p.wait("vector", "pe")
                        for oc in range(2):
                            och = op_ * 2 + oc
                            if final:
                                p.op("vector", lambda E, ps=ps, oc=oc, och=och, c0=c0: E.tensor_add(
                                    x_sb[:, och * TOK + c0:och * TOK + c0 + 1024],
                                    x_sb[:, och * TOK + c0:och * TOK + c0 + 1024],
                                    ps[:, oc * 1024:(oc + 1) * 1024]),
                                    inc=("dve" if oc == 1 else None))
                            else:
                                p.op("vector", lambda E, ps=ps, oc=oc, och=och, c0=c0: E.tensor_add(
                                    x_sb[:, och * TOK + c0:och * TOK + c0 + 1024],
                                    x_sb[:, och * TOK + c0:och * TOK + c0 + 1024],
                                    ps[:, oc * 1024:(oc + 1) * 1024]),
                                    inc=("dve" if oc == 1 else None))
        p.barrier()

    ctm_holder = [ctx.enter_context(nc.sbuf_tensor("ctm_sb", [128, 8], F32))]


    def dump(dst, src):
        if DBG:
            p.barrier()
            p.dma("gpsimd", dst, src, "qG")
            p.barrier()

    def dump_sb(dst, src):
        if DBG:
            p.barrier()
            p.dma("sync", dst, src[:], "qS")
            p.barrier()

    if DBG:
        emit_ln._dump_h = dbg_h

    attn_sublayer(False)
    if DBG:
        dump(dbg_qkv, g1a_in)
        dump(dbg_att, g2a_in)
        dump_sb(dbg_x1, x_sb)
    emit_ffn(False)
    if DBG:
        dump_sb(dbg_x2, x_sb)
    attn_sublayer(True)
    if DBG:
        dump(dbg_qkvb, g1b_in)
        dump(dbg_attb, g2b_in)
        dump_sb(dbg_x3, x_sb)
    emit_ffn(True)

    for och in range(NCK):
        for th in range(2):
            u = och * 2 + th
            if u >= 1:
                p.wait("scalar", "qS", out_m[u - 1] if u >= 1 else 0)
            p.op("scalar", lambda E, och=och, th=th: E.copy(
                outV[:, 0:1024], x_sb[:, och * TOK + th * 1024:och * TOK + th * 1024 + 1024]), inc="act")
            p.wait("sync", "act")
            if u == 0:
                out_m = {}
            p.dma("sync", out_d[:, och * TOK + th * 1024:och * TOK + th * 1024 + 1024], outV[:, 0:1024], "qS")
            out_m[u] = p.mark("qS")
    p.wait("sync", "qS")
    p.wait("sync", "qG")
    p.wait("sync", "qC")

    with nc.Block() as block:
        def make_runner(name):
            def runner(engine):
                dyn = {}
                for fn, inc, n, dfn in p.q[name]:
                    if dfn is not None:
                        ins = dfn(engine, dyn)
                        ins.then_inc(sems[inc], n)
                    elif isinstance(fn, tuple) and fn[0] == "W":
                        engine.wait_ge(sems[fn[1]], fn[2])
                    else:
                        ins = fn(engine)
                        if inc is not None:
                            ins.then_inc(sems[inc], n)
            return runner

        r_sync = make_runner("sync"); r_scalar = make_runner("scalar")
        r_vector = make_runner("vector"); r_tensor = make_runner("tensor")
        r_gpsimd = make_runner("gpsimd")

        @block.sync
        def _(sync):
            r_sync(sync)

        @block.scalar
        def _(scalar):
            r_scalar(scalar)

        @block.vector
        def _(vector):
            r_vector(vector)

        @block.tensor
        def _(tensor):
            r_tensor(tensor)

        @block.gpsimd
        def _(gpsimd):
            r_gpsimd(gpsimd)

    ctx.close()
    return nc



def tile_stationary(W):
    d, o = W.shape
    ndc, noc = d // 128, o // 128
    return np.ascontiguousarray(
        W.reshape(ndc, 128, noc, 128).transpose(1, 2, 0, 3).reshape(128, noc * ndc * 128))


def tile_moving(W):
    d, o = W.shape
    ndc = d // 128
    return np.ascontiguousarray(W.reshape(ndc, 128, o).transpose(1, 0, 2).reshape(128, ndc * o))


def tile_a(A):
    At = np.ascontiguousarray(A.T)
    ndc = At.shape[0] // 128
    return np.ascontiguousarray(At.reshape(ndc, 128, 32).transpose(1, 0, 2).reshape(128, ndc * 32))


def tile_w13(w1, w3):
    nfc = w1.shape[1] // 128
    out = np.empty((nfc, 128, 4096), np.float32)
    for which, w in ((0, w1), (1, w3)):
        t = w.reshape(16, 128, nfc, 128).transpose(2, 1, 0, 3).reshape(nfc, 128, 2048)
        out[:, :, which * 2048:(which + 1) * 2048] = t
    return out


def tile_w2(w2):
    nfc = w2.shape[0] // 128
    return np.ascontiguousarray(
        w2.reshape(nfc, 128, 16, 128).transpose(1, 0, 2, 3).reshape(128, nfc * 2048))


_CACHE = {}


def kernel(**inputs):
    ip = {k: np.asarray(v, dtype=np.float32) for k, v in inputs.items()}
    x = ip["L0_attn_a_x"]

    shared = {}
    shared["Wqa"] = tile_stationary(ip["L0_attn_a_Wq"])
    shared["Wka"] = tile_stationary(ip["L0_attn_a_Wk"])
    shared["Wva"] = tile_stationary(ip["L0_attn_a_Wv"])
    shared["Woa"] = tile_stationary(ip["L0_attn_a_Wo"])
    shared["Wqb"] = tile_stationary(np.ascontiguousarray(ip["L0_attn_b_W_q"].T))
    shared["Wkb"] = tile_stationary(np.ascontiguousarray(ip["L0_attn_b_W_k"].T))
    shared["Wvb"] = tile_stationary(np.ascontiguousarray(ip["L0_attn_b_W_v"].T))
    shared["Wob"] = tile_stationary(np.ascontiguousarray(ip["L0_attn_b_W_o"].T))
    shared["Aall"] = np.concatenate([tile_a(ip["L0_attn_b_A_q"]), tile_a(ip["L0_attn_b_A_k"]),
                                     tile_a(ip["L0_attn_b_A_v"]), tile_a(ip["L0_attn_b_A_o"])], axis=1)
    shared["Ball"] = np.concatenate([np.ascontiguousarray(ip["L0_attn_b_B_q"].T),
                                     np.ascontiguousarray(ip["L0_attn_b_B_k"].T),
                                     np.ascontiguousarray(ip["L0_attn_b_B_v"].T),
                                     np.ascontiguousarray(ip["L0_attn_b_B_o"].T)], axis=1)
    rt = ip["L0_ffn_w_router"]
    shared["rW"] = np.ascontiguousarray(
        rt.reshape(16, 128, 8).transpose(1, 0, 2).reshape(128, 128))
    w13l = [tile_w13(ip["L0_ffn_w1"][e], ip["L0_ffn_w3"][e]) for e in range(8)]
    ws1, ws3, ws2 = ip["L0_ffn_ws1"], ip["L0_ffn_ws3"], ip["L0_ffn_ws2"]
    w13l.append(tile_w13(ws1[:, 0:1408], ws3[:, 0:1408]))
    w13l.append(tile_w13(ws1[:, 1408:2816], ws3[:, 1408:2816]))
    shared["w13"] = np.stack(w13l)
    w2l = [tile_w2(ip["L0_ffn_w2"][e]) for e in range(8)]
    w2l.append(tile_w2(ws2[0:1408]))
    w2l.append(tile_w2(ws2[1408:2816]))
    shared["w2s"] = np.stack(w2l)

    tqi = np.arange(QT)[None, :]
    tsi = np.arange(128)[:, None]
    mA = np.zeros((128, 4 * QT), ml_dtypes.bfloat16)
    mB = np.zeros((128, 4 * QT), ml_dtypes.bfloat16)
    for d_ in range(4):
        keep = (tqi - d_ * 128) >= tsi
        mA[:, d_ * QT:(d_ + 1) * QT] = keep.astype(ml_dtypes.bfloat16)
        mB[:, d_ * QT:(d_ + 1) * QT] = np.where(keep, 0.0, NEG).astype(ml_dtypes.bfloat16)
    shared["maskAi"] = mA
    shared["maskBi"] = mB
    shared["identF"] = np.eye(128, dtype=np.float32)
    shared["onesC"] = np.ones((128, 1), np.float32)
    shared["onesCb"] = np.ones((128, 1), ml_dtypes.bfloat16)
    shared["onesR"] = np.ones((1, 128), np.float32)
    g_ = np.arange(16)
    shared["UTc"] = (g_[:, None] > g_[None, :]).astype(np.float32)
    sel = np.zeros((8, 8 * 128), ml_dtypes.bfloat16)
    for e in range(8):
        sel[e, e * 128:(e + 1) * 128] = 1
    shared["sel8"] = sel
    shared["svec"] = np.array([[ip["L0_attn_b_s_q"][0], ip["L0_attn_b_s_k"][0],
                                ip["L0_attn_b_s_v"][0], ip["L0_attn_b_s_o"][0]]], np.float32)
    shared["epsv"] = np.array([[LN_EPS, RMS_EPS]], np.float32)

    inv = 1.0 / (10000.0 ** (np.arange(0, 64, 2, dtype=np.float32) / 64.0))
    freqs = np.outer(np.arange(T, dtype=np.float32), inv)
    cosf = np.cos(freqs).T
    sinf = np.sin(freqs).T

    in_maps = []
    for c in range(8):
        b, r = c // 4, c % 4
        xs = np.ascontiguousarray(x[b, r * TOK:(r + 1) * TOK, :].T)
        m = dict(shared)
        m["x_in"] = np.ascontiguousarray(
            xs.reshape(16, 128, TOK).transpose(1, 0, 2).reshape(128, 16 * TOK))
        cs = np.empty((64, 2 * TOK), ml_dtypes.bfloat16)
        cs[0:32, 0:TOK] = cosf[:, r * TOK:(r + 1) * TOK].astype(ml_dtypes.bfloat16)
        cs[32:64, 0:TOK] = sinf[:, r * TOK:(r + 1) * TOK].astype(ml_dtypes.bfloat16)
        cs[0:32, TOK:] = sinf[:, r * TOK:(r + 1) * TOK].astype(ml_dtypes.bfloat16)
        cs[32:64, TOK:] = cosf[:, r * TOK:(r + 1) * TOK].astype(ml_dtypes.bfloat16)
        m["csO"] = cs
        m["gain4"] = np.ascontiguousarray(ip["L0_attn_b_q_gain"][4 * r:4 * r + 4][None, :])
        pp = np.arange(128, dtype=np.uint32)
        gi = np.zeros((128, 112), np.uint32)
        for j in range(4):
            for s_ in range(4):
                gi[:, 12 * j + s_] = 512 * r + s_ * 6144 + j * 128 + pp
                gi[:, 12 * j + 4 + s_] = 512 * r + s_ * 6144 + 2048 + j * 128 + pp
                gi[:, 12 * j + 8 + s_] = 512 * r + s_ * 6144 + 4096 + j * 128 + pp
        for j in range(4):
            for s_ in range(4):
                gi[:, 48 + 4 * j + s_] = 512 * r + s_ * 4096 + j * 128 + pp
        for jj in range(2):
            for s_ in range(4):
                gi[:, 64 + 4 * jj + s_] = 256 * r + s_ * 4096 + 2048 + jj * 128 + pp
                gi[:, 72 + 4 * jj + s_] = 256 * r + s_ * 4096 + 3072 + jj * 128 + pp
        for hdc in range(16):
            gi[:, 80 + hdc] = (hdc * 128 + pp) * 4 + r
            gi[:, 96 + hdc] = (hdc * 128 + pp) * 4 + r
        m["gidx"] = gi
        in_maps.append(m)

    if "nc" not in _CACHE:
        _CACHE["nc"] = build_program()
    nc = _CACHE["nc"]
    res = run_bass_kernel_spmd(nc, in_maps, core_ids=list(range(8)))

    out = np.empty((2, T, D), np.float32)
    for c in range(8):
        b, r = c // 4, c % 4
        o = res.results[c]["out_d"]
        out[b, r * TOK:(r + 1) * TOK, :] = o.reshape(128, 16, TOK).transpose(1, 0, 2).reshape(D, TOK).T
    return out


# version 07
# speedup vs baseline: 1.0308x; 1.0304x over previous
import numpy as np
import ml_dtypes
from collections import defaultdict
from contextlib import ExitStack

import concourse.bass as bass
import concourse.mybir as mybir
from concourse.expressions import RuntimeValue
from concourse.bass_utils import run_bass_kernel_spmd

F32 = mybir.dt.float32
BF16 = mybir.dt.bfloat16
AF = mybir.ActivationFunctionType
ALU = mybir.AluOpType
ds = bass.ds

D = 2048
T = 8192
TOK = 2048
NCK = 16
QT = 512
NQT = T // QT
NFC = 11
SCALE_A = float(1.0 / np.float64(np.sqrt(np.float32(T))))
SCALE_B = float(1.0 / np.float64(np.sqrt(np.float32(128))))
BOUND_B = float(np.sqrt(np.float64(128.0)))
LN_EPS = 1e-5
RMS_EPS = 1.1920929e-07
NEG = -1.0e9

ENGINES = ("sync", "scalar", "vector", "tensor", "gpsimd")
DBG = False


class Rec:

    def __init__(self):
        self.q = {e: [] for e in ENGINES}
        self.cnt = defaultdict(int)
        self.dynm = {}
        self.dyn = None

    def op(self, eng, fn, inc=None, n=1):
        self.q[eng].append((fn, inc, n, None))
        if inc:
            self.cnt[inc] += n
        return self.cnt[inc] if inc else None

    def dma(self, eng, out, in_, inc="q" + "X", **kw):
        return self.op(eng, lambda E, o=out, i=in_, k=kw: E.dma_start(out=o, in_=i, **k), inc=inc, n=16)

    def dmad(self, eng, fn, inc):
        self.q[eng].append((None, inc, 16, fn))
        self.cnt[inc] += 16
        return self.cnt[inc]

    def wait(self, eng, sem, val=None):
        v = self.cnt[sem] if val is None else val
        assert v <= self.cnt[sem], (sem, v, self.cnt[sem])
        self.q[eng].append((("W", sem, v), None, 0, None))

    def mark(self, sem):
        return self.cnt[sem]

    def barrier(self, engines=ENGINES, sems=None):
        if sems is None:
            sems = [s for s in self.cnt if not s.startswith("cc")]
        for e in engines:
            for s in sems:
                if self.cnt[s] > 0:
                    self.wait(e, s, self.cnt[s])


def build_program():
    nc = bass.Bass("TRN2", target_bir_lowering=False, detect_race_conditions=False)
    dr = lambda name, shape, dt, kind="ExternalInput": nc.dram_tensor(name, shape, dt, kind=kind).ap()

    x_in = dr("x_in", [128, NCK * TOK], F32)
    Wqa = dr("Wqa", [128, 32768], F32); Wka = dr("Wka", [128, 32768], F32)
    Wva = dr("Wva", [128, 32768], F32); Woa = dr("Woa", [128, 32768], F32)
    Wqb = dr("Wqb", [128, 32768], F32); Wkb = dr("Wkb", [128, 16384], F32)
    Wvb = dr("Wvb", [128, 16384], F32); Wob = dr("Wob", [128, 32768], F32)
    Aall = dr("Aall", [128, 4 * 512], F32)
    Ball = dr("Ball", [32, 6144], F32)
    rW = dr("rW", [128, 128], F32)
    w13 = dr("w13", [10, NFC, 128, 4096], F32)
    w2s = dr("w2s", [10, 128, NFC * 2048], F32)
    csO = dr("csO", [64, 2 * TOK], BF16)
    maskAi = dr("maskAi", [128, 4 * QT], BF16)
    maskBi = dr("maskBi", [128, 4 * QT], BF16)
    identF = dr("identF", [128, 128], F32)
    onesC = dr("onesC", [128, 1], F32)
    onesCb = dr("onesCb", [128, 1], BF16)
    onesR = dr("onesR", [1, 128], F32)
    UTc = dr("UTc", [16, 16], F32)
    sel8 = dr("sel8", [8, 8 * 128], BF16)
    gain4 = dr("gain4", [1, 4], F32)
    gidx = dr("gidx", [128, 112], mybir.dt.uint32)
    svec = dr("svec", [1, 4], F32)
    epsv = dr("epsv", [1, 2], F32)
    out_d = dr("out_d", [128, NCK * TOK], F32, kind="ExternalOutput")
    if DBG:
        dbg_h = dr("dbg_h", [128, NCK * TOK], BF16, kind="ExternalOutput")
        dbg_qkv = dr("dbg_qkv", [6144, 2048], BF16, kind="ExternalOutput")
        dbg_att = dr("dbg_att", [512, T], BF16, kind="ExternalOutput")
        dbg_x1 = dr("dbg_x1", [128, NCK * TOK], BF16, kind="ExternalOutput")
        dbg_x2 = dr("dbg_x2", [128, NCK * TOK], BF16, kind="ExternalOutput")
        dbg_x3 = dr("dbg_x3", [128, NCK * TOK], BF16, kind="ExternalOutput")
        dbg_qkvb = dr("dbg_qkvb", [4096, 2048], BF16, kind="ExternalOutput")
        dbg_attb = dr("dbg_attb", [512, T], BF16, kind="ExternalOutput")
        dbg_fine = dr("dbg_fine", [128, 6144], F32, kind="ExternalOutput")
        dbg_ffn = dr("dbg_ffn", [128, 8192], F32, kind="ExternalOutput")
        dbg_rt = dr("dbg_rt", [8, 4096], F32, kind="ExternalOutput")
        dbg_t2 = dr("dbg_t2", [128, 64], F32, kind="ExternalOutput")

    WqaB = dr("WqaB", [128, 32768], BF16, kind="Internal")
    WkaB = dr("WkaB", [128, 32768], BF16, kind="Internal")
    WvaB = dr("WvaB", [128, 32768], BF16, kind="Internal")
    WoaB = dr("WoaB", [128, 32768], BF16, kind="Internal")
    WqbB = dr("WqbB", [128, 32768], BF16, kind="Internal")
    WkbB = dr("WkbB", [128, 16384], BF16, kind="Internal")
    WvbB = dr("WvbB", [128, 16384], BF16, kind="Internal")
    WobB = dr("WobB", [128, 32768], BF16, kind="Internal")
    w13B = dr("w13B", [10, NFC, 128, 4096], BF16, kind="Internal")
    w2sB = dr("w2sB", [10, 128, NFC * 2048], BF16, kind="Internal")
    g1a_in = dr("g1a_in", [6144, 2048], BF16, kind="Internal")
    g1a_out = dr("g1a_out", [24576, 2048], BF16, kind="Internal")
    g1b_in = dr("g1b_in", [4096, 2048], BF16, kind="Internal")
    g1b_out = dr("g1b_out", [16384, 2048], BF16, kind="Internal")
    g2a_in = dr("g2a_in", [512, T], BF16, kind="Internal")
    g2a_out = dr("g2a_out", [2048, T], BF16, kind="Internal")
    g2b_in = dr("g2b_in", [512, T], BF16, kind="Internal")
    g2b_out = dr("g2b_out", [2048, T], BF16, kind="Internal")

    ctx = ExitStack()
    sb = lambda name, shape, dt: ctx.enter_context(nc.sbuf_tensor(name, shape, dt))
    x_sb = sb("x_sb", [128, NCK * TOK], BF16)
    h_sb = sb("h_sb", [128, NCK * TOK], BF16)
    wbuf = [sb(f"wbuf{i}", [128, 4096], BF16) for i in range(2)]
    stg = [sb(f"stg{i}", [128, 1024], BF16) for i in range(2)]
    row_sb = sb("row_sb", [1, 1024], F32)
    sq_sb = sb("sq_sb", [128, 512], F32)
    silu_sb = sb("silu_sb", [128, 1024], BF16)
    cb_sb = sb("cb_sb", [128, 2048], BF16)
    c_fm = sb("c_fm", [8, 2048], BF16)
    cs_sb = sb("cs_sb", [64, 2 * TOK], BF16)
    rt_sb = sb("rt_sb", [64, 1024], BF16)
    mask_sb = sb("mask_sb", [128, 4 * QT], BF16)
    ident_sb = sb("ident_sb", [128, 128], F32)
    sel_sb = sb("sel_sb", [8, 8 * 128], BF16)
    onesC_sb = sb("onesC_sb", [128, 1], F32)
    onesCb_sb = sb("onesCb_sb", [128, 1], BF16)
    onesR_sb = sb("onesR_sb", [1, 128], F32)
    UT_sb = sb("UT_sb", [16, 16], F32)
    rW_sb = sb("rW_sb", [128, 128], BF16)
    gb_sb = sb("gb_sb", [128, 16], F32)
    gidx_sb = sb("gidx_sb", [128, 112], mybir.dt.uint32)
    tp_sb = sb("tp_sb", [128, 16], F32)
    stm_sb = sb("stm_sb", [128, 8], F32)
    msk1_sb = sb("msk1_sb", [128, 8], F32)
    cv_sb = sb("cv_sb", [128, 16], F32)
    cvT_sb = sb("cvT_sb", [16, 128], F32)
    sv_sb = sb("sv_sb", [128, 16], F32)
    AR = sb("AR", [128, 12800], BF16)

    psA = ctx.enter_context(nc.psum_tensor("psA", [128, 2048], F32))
    psB = ctx.enter_context(nc.psum_tensor("psB", [128, 2048], F32))

    sem_names = ["qS", "qC", "qG", "pe", "act", "dve", "po",
                 "cc0", "cc1", "cc2", "cc3"]
    sems = {n: ctx.enter_context(nc.semaphore(n)) for n in sem_names}
    p = Rec()

    heV = AR[:, 0:6144]
    w2V = AR[:, 6144:12288]
    vslabV = AR[:, 0:8192]
    uV = AR[0:32, 8192:10240]
    BV = AR[0:32, 10240:12288]
    AVt = AR[:, 12288:12800]
    attnT2 = [AR[:, 0:2048], AR[:, 2048:4096]]
    S2s = [AR[:, 4096:8192].bitcast(F32), AR[:, 8192:12288].bitcast(F32)]
    sfmV = AR[0:8, 4608:8704].bitcast(F32)
    outV = AR[:, 0:2048].bitcast(F32)

    p.dma("sync", ident_sb[:], identF, "qS")
    p.dma("sync", onesC_sb[:], onesC, "qS")
    p.dma("sync", onesCb_sb[:], onesCb, "qS")
    p.dma("sync", onesR_sb[:], onesR, "qS")
    p.dma("sync", UT_sb[:], UTc, "qS")
    p.dma("sync", sel_sb[:], sel8, "qS")
    p.dma("sync", cs_sb[:], csO, "qS")
    p.dma("sync", row_sb[0:1, 0:4], gain4, "qS")
    p.dma("sync", row_sb[0:1, 4:8], svec, "qS")
    p.dma("sync", gb_sb[0:1, 12:14], epsv, "qS", allow_slow_non_contiguous=True)
    p.dma("gpsimd", rW_sb[:], rW, "qG")
    p.dma("gpsimd", x_sb[:], x_in, "qG")
    p.dma("sync", gidx_sb[:], gidx, "qS")
    p.barrier()
    p.op("tensor", lambda E: E.matmul(psA[:, 0:8], onesR_sb[:], row_sb[0:1, 0:8],
                                      start=True, stop=True), inc="pe")
    p.wait("vector", "pe")
    p.op("vector", lambda E: E.tensor_copy(gb_sb[:, 0:8], psA[:, 0:8]), inc="dve")
    p.wait("scalar", "dve")
    p.op("scalar", lambda E: E.activation(gb_sb[:, 8:12], gb_sb[:, 0:4], AF.Abs), inc="act")
    p.wait("vector", "act")
    p.op("vector", lambda E: E.tensor_scalar_mul(gb_sb[:, 8:12], gb_sb[:, 8:12], -BOUND_B), inc="dve")
    p.barrier()

    wcast = {}
    for nm, dst, src in [("Wqa", WqaB, Wqa), ("Wka", WkaB, Wka), ("Wva", WvaB, Wva),
                         ("Woa", WoaB, Woa)]:
        p.dma("gpsimd", dst, src, "qG")
        wcast[nm] = p.mark("qG")
    for e in range(10):
        p.dma("gpsimd", w13B[e], w13[e], "qG")
        wcast[f"w13_{e}"] = p.mark("qG")
        p.dma("gpsimd", w2sB[e], w2s[e], "qG")
        wcast[f"w2_{e}"] = p.mark("qG")
    for nm, dst, src in [("Wqb", WqbB, Wqb), ("Wkb", WkbB, Wkb), ("Wvb", WvbB, Wvb),
                         ("Wob", WobB, Wob)]:
        p.dma("gpsimd", dst, src, "qG")
        wcast[nm] = p.mark("qG")

    def emit_ln():
        for tq4 in range(4):
            c0 = tq4 * 512
            for dc in range(NCK):
                p.wait("scalar", "pe")
                p.op("scalar", lambda E, d=dc, c0=c0: E.activation(
                    sq_sb[:], x_sb[:, d * TOK + c0:d * TOK + c0 + 512], AF.Square), inc="act")
                p.wait("tensor", "act")
                p.op("tensor", lambda E, d=dc, c0=c0: E.matmul(
                    psA[0:1, 0:512], onesCb_sb[:], x_sb[:, d * TOK + c0:d * TOK + c0 + 512],
                    start=(d == 0), stop=(d == NCK - 1)))
                p.op("tensor", lambda E, d=dc: E.matmul(
                    psA[0:1, 512:1024], onesC_sb[:], sq_sb[:, 0:512],
                    start=(d == 0), stop=(d == NCK - 1)), inc="pe")
            p.wait("vector", "pe")
            mu = row_sb[0:1, 0:512]
            wk = row_sb[0:1, 512:1024]
            p.op("vector", lambda E: E.tensor_scalar_mul(mu, psA[0:1, 0:512], 1.0 / D))
            p.op("vector", lambda E: E.tensor_mul(wk, mu, mu))
            p.op("vector", lambda E: E.scalar_tensor_tensor(
                wk, psA[0:1, 512:1024], 1.0 / D, wk, ALU.mult, ALU.subtract), inc="dve")
            p.wait("scalar", "dve")
            p.op("scalar", lambda E: E.activation(wk, wk, AF.Sqrt, bias=gb_sb[0:1, 12:13]), inc="act")
            p.wait("vector", "act")
            p.op("vector", lambda E: E.reciprocal(wk, wk))
            p.op("vector", lambda E: E.scalar_tensor_tensor(mu, mu, -1.0, wk, ALU.mult, ALU.mult), inc="dve")
            p.wait("tensor", "dve")
            p.op("tensor", lambda E: E.matmul(psB[:, 0:512], onesR_sb[:], wk, start=True, stop=True))
            p.op("tensor", lambda E: E.matmul(psB[:, 512:1024], onesR_sb[:], mu, start=True, stop=True), inc="pe")
            p.wait("vector", "pe")
            for dc in range(NCK):
                p.op("vector", lambda E, d=dc, c0=c0: E.tensor_mul(
                    h_sb[:, d * TOK + c0:d * TOK + c0 + 512],
                    x_sb[:, d * TOK + c0:d * TOK + c0 + 512], psB[:, 0:512]))
                p.op("vector", lambda E, d=dc, c0=c0: E.tensor_add(
                    h_sb[:, d * TOK + c0:d * TOK + c0 + 512],
                    h_sb[:, d * TOK + c0:d * TOK + c0 + 512], psB[:, 512:1024]),
                    inc=("dve" if dc == NCK - 1 else None))
            p.barrier(("tensor", "scalar"), ["dve"])
        p.barrier()

    def load_AB(acol, bcol, blen):
        p.dma("gpsimd", AVt, Aall[:, acol:acol + 512], "qG")
        p.dma("gpsimd", BV[:, 0:blen], Ball[:, bcol:bcol + blen], "qG")

    def emit_u():
        p.barrier(("tensor",), ["qG"])
        for dc in range(NCK):
            for g in range(4):
                p.op("tensor", lambda E, d=dc, g=g: E.matmul(
                    psA[0:32, g * 512:(g + 1) * 512], AVt[:, d * 32:(d + 1) * 32],
                    h_sb[:, d * TOK + g * 512:d * TOK + g * 512 + 512],
                    start=(d == 0), stop=(d == NCK - 1)),
                    inc=("pe" if (dc == NCK - 1 and g == 3) else None))
        p.wait("scalar", "pe")
        p.op("scalar", lambda E: E.copy(uV[:, 0:2048], psA[0:32, 0:2048]), inc="act")
        p.wait("tensor", "act")

    def fm_proj(Wt, noc, dst, dst_rows, wmark, lora=False, scale_col=None, rope=False):
        pe_done = {}
        cons = {}
        store_done = {}
        qc_m = {}
        p.wait("scalar", "qG", wmark)
        p.dma("scalar", wbuf[0][:, 0:2048], Wt[:, 0:2048], "qC")
        qc_m[0] = p.mark("qC")
        for och in range(noc):
            wb = wbuf[och % 2]
            if och + 1 < noc:
                if och >= 1:
                    p.wait("scalar", "pe", pe_done[och - 1])
                p.dma("scalar", wbuf[(och + 1) % 2][:, 0:2048],
                      Wt[:, (och + 1) * 2048:(och + 2) * 2048], "qC")
                qc_m[och + 1] = p.mark("qC")
            p.wait("tensor", "qC", qc_m[och])
            for th in range(2):
                u = och * 2 + th
                ps = psA if u % 2 == 0 else psB
                c0 = th * 1024
                if u >= 2:
                    p.wait("tensor", *cons[u - 2])
                for dc in range(NCK):
                    for g in range(2):
                        p.op("tensor", lambda E, d=dc, g=g, ps=ps, c0=c0, wb=wb: E.matmul(
                            ps[:, g * 512:(g + 1) * 512], wb[:, d * 128:(d + 1) * 128],
                            h_sb[:, d * TOK + c0 + g * 512:d * TOK + c0 + g * 512 + 512],
                            start=(d == 0), stop=(d == NCK - 1)))
                if lora:
                    for g in range(2):
                        p.op("tensor", lambda E, g=g, ps=ps, c0=c0, och=och: E.matmul(
                            ps[:, 1024 + g * 512:1024 + (g + 1) * 512],
                            BV[:, och * 128:(och + 1) * 128],
                            uV[:, c0 + g * 512:c0 + g * 512 + 512],
                            start=True, stop=True),
                            inc=("pe" if g == 1 else None))
                else:
                    p.q["tensor"][-1] = (p.q["tensor"][-1][0], "pe", 1, None)
                    p.cnt["pe"] += 1
                if th == 1:
                    pe_done[och] = p.mark("pe")
                p.wait("scalar", "pe")
                st_ = stg[u % 2]
                if u >= 2:
                    p.wait("scalar", "qS", store_done[u - 2])
                if scale_col is not None:
                    p.op("scalar", lambda E, ps=ps, st=st_, sc=scale_col: E.activation(
                        st[:], ps[:, 0:1024], AF.Copy, scale=gb_sb[:, sc:sc + 1]), inc="act")
                else:
                    p.op("scalar", lambda E, ps=ps, st=st_: E.copy(st[:], ps[:, 0:1024]), inc="act")
                if lora:
                    p.wait("vector", "act")
                    p.op("vector", lambda E, ps=ps, st=st_: E.tensor_add(st[:], st[:], ps[:, 1024:2048]), inc="dve")
                    cons[u] = ("dve", p.mark("dve"))
                else:
                    cons[u] = ("act", p.mark("act"))
                if rope:
                    p.wait("gpsimd", *cons[u])
                    t_b0 = rt_sb[0:32, 0:1024]
                    t_b32 = rt_sb[32:64, 0:1024]
                    co_b0 = cs_sb[0:32, c0:c0 + 1024]
                    si_b32 = cs_sb[32:64, c0:c0 + 1024]
                    si_b0 = cs_sb[0:32, TOK + c0:TOK + c0 + 1024]
                    co_b32 = cs_sb[32:64, TOK + c0:TOK + c0 + 1024]
                    q1 = st_[0:32, :]
                    q2 = st_[32:64, :]
                    p.op("gpsimd", lambda E, q1=q1, si=si_b0, t=t_b32: E.tensor_mul(t, q1, si))
                    p.op("gpsimd", lambda E, q1=q1, co=co_b0: E.tensor_mul(q1, q1, co))
                    p.op("gpsimd", lambda E, q2=q2, si=si_b32, t=t_b0: E.tensor_mul(t, q2, si))
                    p.op("gpsimd", lambda E, q1=q1, t=t_b0: E.tensor_add(q1, q1, t))
                    p.op("gpsimd", lambda E, q2=q2, co=co_b32: E.tensor_mul(q2, q2, co))
                    p.op("gpsimd", lambda E, q2=q2, t=t_b32: E.tensor_sub(q2, q2, t), inc="po")
                    cons[u] = ("po", p.mark("po"))
                p.wait("sync", *cons[u])
                p.dma("sync", dst[dst_rows + och * 128:dst_rows + (och + 1) * 128, c0:c0 + 1024], st_[:], "qS")
                store_done[u] = p.mark("qS")
        p.barrier(("gpsimd", "vector", "scalar"), ["qS"])

    IOA = bass.IndirectOffsetOnAxis

    def gather_rows(dst_cols, src, idxcol):
        p.dmad("gpsimd", lambda E, dyn, dc=dst_cols, src=src, ic=idxcol: E.indirect_dma_start(
            h_sb[:, dc:dc + 2048], None, src, IOA(gidx_sb[:, ic:ic + 1], 0)), "qG")

    def v_transpose(n_tiles=64):
        marks = {}
        for t in range(n_tiles):
            stf = sq_sb[:, (t % 2) * 128:(t % 2) * 128 + 128]
            pslot = psA[:, 1024 + (t % 2) * 512:1024 + (t % 2) * 512 + 128]
            if t >= 2:
                p.wait("scalar", "pe", marks[t - 2][0])
            p.op("scalar", lambda E, t=t, stf=stf: E.copy(
                stf, h_sb[:, 16384 + t * 128:16384 + (t + 1) * 128]), inc="act")
            p.wait("tensor", "act")
            if t >= 2:
                p.wait("tensor", "dve", marks[t - 2][1])
            p.op("tensor", lambda E, stf=stf, ps=pslot: E.transpose(ps, stf, ident_sb[:]), inc="pe")
            p.wait("vector", "pe")
            p.op("vector", lambda E, t=t, ps=pslot: E.tensor_copy(
                h_sb[:, 24576 + t * 128:24576 + (t + 1) * 128], ps), inc="dve")
            marks[t] = (p.mark("pe"), p.mark("dve"))
        p.barrier(("tensor", "scalar", "gpsimd"), ["dve"])

    def collective(cin, cout, ccname):
        p.barrier(("gpsimd",), ["qS", "qG", "dve", "act", "po", "pe"])
        p.op("gpsimd", lambda E, i=cin, o=cout: E.collective_compute(
            "AllGather", ALU.bypass, replica_groups=[[0, 1, 2, 3], [4, 5, 6, 7]],
            ins=[i.opt()], outs=[o.opt()]), inc=ccname, n=1)
        for e in ENGINES:
            p.wait(e, ccname, 1)

    def head_norm(colbase, gaincol):
        for blk in range(16):
            c0 = colbase + blk * 512
            p.wait("scalar", "pe")
            p.op("scalar", lambda E, c0=c0: E.activation(
                sq_sb[:], h_sb[:, c0:c0 + 512], AF.Square), inc="act")
            p.wait("tensor", "act")
            p.op("tensor", lambda E: E.matmul(
                psA[0:1, 0:512], onesC_sb[:], sq_sb[:], start=True, stop=True), inc="pe")
            p.wait("scalar", "pe")
            p.op("scalar", lambda E: E.activation(row_sb[0:1, 0:512], psA[0:1, 0:512],
                                                  AF.Sqrt, bias=gb_sb[0:1, 13:14], scale=1.0 / 128.0), inc="act")
            p.wait("vector", "act")
            p.op("vector", lambda E: E.reciprocal(row_sb[0:1, 0:512], row_sb[0:1, 0:512]), inc="dve")
            p.wait("tensor", "dve")
            p.op("tensor", lambda E: E.matmul(
                psA[:, 1024:1536], onesR_sb[:], row_sb[0:1, 0:512], start=True, stop=True), inc="pe")
            p.wait("vector", "pe")
            if gaincol is not None:
                p.op("vector", lambda E, c0=c0, gc=gaincol: E.scalar_tensor_tensor(
                    h_sb[:, c0:c0 + 512], h_sb[:, c0:c0 + 512], gb_sb[:, gc:gc + 1],
                    psA[:, 1024:1536], ALU.mult, ALU.mult), inc="dve")
            else:
                p.op("vector", lambda E, c0=c0: E.tensor_mul(
                    h_sb[:, c0:c0 + 512], h_sb[:, c0:c0 + 512], psA[:, 1024:1536]), inc="dve")
            p.wait("tensor", "dve")
            p.wait("scalar", "dve")

    def head_attn(is_b, j, g2in):
        qh = h_sb[:, 0:8192]
        kh = h_sb[:, 8192:16384]
        vhf = h_sb[:, 16384:24576]
        vh = h_sb[:, 24576:32768]
        if not is_b:
            for g in range(16):
                p.op("vector", lambda E, g=g: E.reduce_sum(
                    cv_sb[:, g:g + 1], vhf[:, g * 512:(g + 1) * 512], axis=mybir.AxisListType.X),
                    inc="dve")
                p.wait("vector", "dve")
            p.wait("tensor", "dve")
            p.op("tensor", lambda E: E.transpose(psA[0:16, 0:128], cv_sb[:], ident_sb[:]), inc="pe")
            p.wait("vector", "pe")
            p.op("vector", lambda E: E.tensor_copy(cvT_sb[:], psA[0:16, 0:128]), inc="dve")
            p.wait("tensor", "dve")
            p.op("tensor", lambda E: E.matmul(psA[:, 0:16], cvT_sb[:], UT_sb[:],
                                              start=True, stop=True), inc="pe")
            p.wait("vector", "pe")
            p.op("vector", lambda E: E.tensor_copy(sv_sb[:], psA[:, 0:16]), inc="dve")
            p.wait("tensor", "dve")

        avs = [psB[:, 0:512], psB[:, 512:1024]]
        zs = [psB[0:1, 1024:1536], psB[0:1, 1536:2048]]
        bcs = [psB[:, 1024:1536], psB[:, 1536:2048]]
        rows = [row_sb[0:1, 0:512], row_sb[0:1, 512:1024]]
        rzVs = [sq_sb.bitcast(BF16)[:, 0:512], sq_sb.bitcast(BF16)[:, 512:1024]]
        exp_m = {}; av_m = {}; s2_m = {}
        zmm_m = {}; recip_m = {}; rzcopy_m = {}; norm_m = {}; store_m = {}
        av_last = {}
        gq = [0]

        def tail1(q):
            sl = q % 2
            if not is_b:
                p.wait("vector", "pe", av_last[q])
                p.op("vector", lambda E, sl=sl, q=q: E.tensor_scalar_add(
                    avs[sl], avs[sl], sv_sb[:, q:q + 1]), inc="dve")
            if q >= 2:
                p.wait("tensor", "act", rzcopy_m[q - 2])
            p.wait("tensor", "dve")
            for g in range(4):
                p.op("tensor", lambda E, g=g, sl=sl: E.matmul(
                    zs[sl], onesC_sb[:], S2s[sl][:, g * 512:(g + 1) * 512],
                    start=(g == 0), stop=(g == 3)), inc=("pe" if g == 3 else None))
            zmm_m[q] = p.mark("pe")
            p.wait("vector", "pe", zmm_m[q])
            if not is_b and q < 15:
                p.op("vector", lambda E, sl=sl, q=q: E.tensor_scalar_add(
                    zs[sl], zs[sl], float(128 * (60 - 4 * q))), inc="dve")
                p.wait("vector", "dve")
            p.op("vector", lambda E, sl=sl: E.reciprocal(rows[sl], zs[sl]), inc="dve")
            recip_m[q] = p.mark("dve")

        def tail2(q):
            sl = q % 2
            if q >= 2:
                p.wait("tensor", "act", rzcopy_m[q - 2])
            p.wait("tensor", "dve", recip_m[q])
            p.op("tensor", lambda E, sl=sl: E.matmul(
                bcs[sl], onesR_sb[:], rows[sl], start=True, stop=True), inc="pe")
            bc_m = p.mark("pe")
            p.wait("scalar", "pe", bc_m)
            if q >= 2:
                p.wait("scalar", "dve", norm_m[q - 2])
            p.op("scalar", lambda E, sl=sl: E.copy(rzVs[sl], bcs[sl]), inc="act")
            rzcopy_m[q] = p.mark("act")
            p.wait("vector", "act", rzcopy_m[q])
            if q >= 2:
                p.wait("vector", "qS", store_m[q - 2])
            p.op("vector", lambda E, sl=sl: E.tensor_mul(
                stg[sl][:, 0:512], avs[sl], rzVs[sl]), inc="dve")
            norm_m[q] = p.mark("dve")
            p.wait("sync", "dve", norm_m[q])
            p.dma("sync", g2in[j * 128:(j + 1) * 128, q * QT:(q + 1) * QT], stg[sl][:, 0:512], "qS")
            store_m[q] = p.mark("qS")

        for q in range(NQT):
            nq = q + 1
            sl = q % 2
            if q >= 2:
                p.wait("vector", "pe", zmm_m[q - 2])
            p.op("vector", lambda E, sl=sl: E.memset(S2s[sl][:], 0.0), inc="dve")
            for qd in range(nq):
                g_ = gq[0]
                at = attnT2[g_ % 2]
                diag = (qd == nq - 1)
                if g_ >= 1:
                    p.wait("tensor", "act", exp_m[g_ - 1])
                for ci in range(4):
                    c = 4 * qd + ci
                    p.op("tensor", lambda E, c=c, q=q, ci=ci: E.matmul(
                        psA[:, ci * 512:(ci + 1) * 512], kh[:, c * 128:(c + 1) * 128],
                        qh[:, q * QT:(q + 1) * QT], start=True, stop=True),
                        inc=("pe" if ci == 3 else None))
                pe_m = p.mark("pe")
                if diag:
                    p.wait("vector", "pe", pe_m)
                    op_ = "tensor_add" if is_b else "tensor_mul"
                    for ci in range(4):
                        p.op("vector", lambda E, ci=ci, op_=op_: getattr(E, op_)(
                            psA[:, ci * 512:(ci + 1) * 512], psA[:, ci * 512:(ci + 1) * 512],
                            mask_sb[:, ci * QT:(ci + 1) * QT]),
                            inc=("dve" if ci == 3 else None))
                    p.wait("scalar", "dve")
                else:
                    p.wait("scalar", "pe", pe_m)
                if g_ >= 2:
                    p.wait("scalar", "pe", av_m[g_ - 2])
                    p.wait("scalar", "dve", s2_m[g_ - 2])
                if is_b:
                    p.op("scalar", lambda E, at=at, j=j: E.activation(
                        at, psA[:], AF.Exp, bias=gb_sb[:, 8 + j:9 + j], scale=SCALE_B), inc="act")
                else:
                    p.op("scalar", lambda E, at=at: E.activation(
                        at, psA[:], AF.Exp, scale=SCALE_A), inc="act")
                exp_m[g_] = p.mark("act")
                p.wait("vector", "act", exp_m[g_])
                p.op("vector", lambda E, at=at, sl=sl: E.tensor_add(S2s[sl][:], S2s[sl][:], at), inc="dve")
                s2_m[g_] = p.mark("dve")
                p.wait("tensor", "act", exp_m[g_])
                if qd == 0 and q >= 2:
                    p.wait("tensor", "dve", norm_m[q - 2])
                for ci in range(4):
                    c = 4 * qd + ci
                    p.op("tensor", lambda E, c=c, at=at, ci=ci, sl=sl,
                         s=(qd == 0 and ci == 0), st=(qd == nq - 1 and ci == 3): E.matmul(
                        avs[sl], vh[:, c * 128:(c + 1) * 128], at[:, ci * 512:(ci + 1) * 512],
                        start=s, stop=st), inc=("pe" if ci == 3 else None))
                av_m[g_] = p.mark("pe")
                gq[0] = g_ + 1
                if q >= 1 and qd == 0:
                    tail1(q - 1)
                if q >= 1 and qd == min(2, nq - 1):
                    tail2(q - 1)
            av_last[q] = av_m[gq[0] - 1]
        tail1(NQT - 1)
        tail2(NQT - 1)

    def o_proj(g2out, Wot, lora, scale_col, aocol, wmark):
        for hdc in range(NCK):
            p.dmad("gpsimd", lambda E, dyn, hd=hdc, ic=aocol + hdc: E.indirect_dma_start(
                h_sb[:, hd * TOK:(hd + 1) * TOK], None,
                g2out.rearrange("r (b x) -> (r b) x", b=4),
                IOA(gidx_sb[:, ic:ic + 1], 0)), "qG")
        p.barrier(("tensor", "scalar", "vector", "gpsimd"), ["qG"])
        if lora:
            load_AB(1536, 4096, 2048)
            emit_u()
        pe_done = {}
        cons = {}
        qc_m = {}
        p.wait("scalar", "qG", wmark)
        p.dma("scalar", wbuf[0][:, 0:2048], Wot[:, 0:2048], "qC")
        qc_m[0] = p.mark("qC")
        for och in range(NCK):
            wb = wbuf[och % 2]
            if och + 1 < NCK:
                if och >= 1:
                    p.wait("scalar", "pe", pe_done[och - 1])
                p.dma("scalar", wbuf[(och + 1) % 2][:, 0:2048],
                      Wot[:, (och + 1) * 2048:(och + 2) * 2048], "qC")
                qc_m[och + 1] = p.mark("qC")
            p.wait("tensor", "qC", qc_m[och])
            for th in range(2):
                u = och * 2 + th
                ps = psA if u % 2 == 0 else psB
                c0 = th * 1024
                if u >= 2:
                    p.wait("tensor", *cons[u - 2])
                for dc in range(NCK):
                    for g in range(2):
                        p.op("tensor", lambda E, d=dc, g=g, ps=ps, c0=c0, wb=wb: E.matmul(
                            ps[:, g * 512:(g + 1) * 512], wb[:, d * 128:(d + 1) * 128],
                            h_sb[:, d * TOK + c0 + g * 512:d * TOK + c0 + g * 512 + 512],
                            start=(d == 0), stop=(d == NCK - 1)))
                if lora:
                    for g in range(2):
                        p.op("tensor", lambda E, g=g, ps=ps, c0=c0, och=och: E.matmul(
                            ps[:, 1024 + g * 512:1024 + (g + 1) * 512],
                            BV[:, och * 128:(och + 1) * 128],
                            uV[:, c0 + g * 512:c0 + g * 512 + 512],
                            start=True, stop=True), inc=("pe" if g == 1 else None))
                else:
                    p.q["tensor"][-1] = (p.q["tensor"][-1][0], "pe", 1, None)
                    p.cnt["pe"] += 1
                if th == 1:
                    pe_done[och] = p.mark("pe")
                p.wait("scalar", "pe")
                st_ = stg[u % 2]
                if u >= 2:
                    p.wait("scalar", *cons[u - 2])
                if scale_col is not None:
                    p.op("scalar", lambda E, ps=ps, st=st_, sc=scale_col: E.activation(
                        st[:], ps[:, 0:1024], AF.Copy, scale=gb_sb[:, sc:sc + 1]), inc="act")
                else:
                    p.op("scalar", lambda E, ps=ps, st=st_: E.copy(st[:], ps[:, 0:1024]), inc="act")
                p.wait("vector", "act")
                if lora:
                    p.op("vector", lambda E, ps=ps, st=st_: E.tensor_add(st[:], st[:], ps[:, 1024:2048]))
                p.op("vector", lambda E, st=st_, och=och, c0=c0: E.tensor_add(
                    x_sb[:, och * TOK + c0:och * TOK + c0 + 1024],
                    x_sb[:, och * TOK + c0:och * TOK + c0 + 1024], st[:]), inc="dve")
                cons[u] = ("dve", p.mark("dve"))
        p.barrier()

    def attn_sublayer(is_b):
        emit_ln()
        if DBG and not is_b and not hasattr(attn_sublayer, "_h_dumped"):
            attn_sublayer._h_dumped = True
            p.barrier()
            p.dma("sync", dbg_h, h_sb[:], "qS")
            p.barrier()
        if not is_b:
            p.dma("sync", mask_sb[:], maskAi, "qS")
            fm_proj(WqaB, 16, g1a_in, 0, wcast["Wqa"])
            fm_proj(WkaB, 16, g1a_in, 2048, wcast["Wka"])
            fm_proj(WvaB, 16, g1a_in, 4096, wcast["Wva"])
            collective(g1a_in, g1a_out, "cc0")
            for j in range(4):
                for s in range(4):
                    gather_rows(s * 2048, g1a_out, 12 * j + s)
                    gather_rows(8192 + s * 2048, g1a_out, 12 * j + 4 + s)
                    gather_rows(16384 + s * 2048, g1a_out, 12 * j + 8 + s)
                p.barrier(("tensor", "scalar", "vector", "gpsimd"), ["qG"])
                v_transpose()
                head_attn(False, j, g2a_in)
                p.barrier()
            collective(g2a_in, g2a_out, "cc1")
            o_proj(g2a_out, WoaB, False, None, 80, wcast["Woa"])
        else:
            p.dma("sync", mask_sb[:], maskBi, "qS")
            load_AB(0, 0, 2048)
            emit_u()
            fm_proj(WqbB, 16, g1b_in, 0, wcast["Wqb"], lora=True, scale_col=4, rope=True)
            load_AB(512, 2048, 1024)
            emit_u()
            fm_proj(WkbB, 8, g1b_in, 2048, wcast["Wkb"], lora=True, scale_col=5, rope=True)
            load_AB(1024, 3072, 1024)
            emit_u()
            fm_proj(WvbB, 8, g1b_in, 3072, wcast["Wvb"], lora=True, scale_col=6)
            collective(g1b_in, g1b_out, "cc2")
            for jj in range(2):
                for s in range(4):
                    gather_rows(8192 + s * 2048, g1b_out, 64 + 4 * jj + s)
                    gather_rows(16384 + s * 2048, g1b_out, 72 + 4 * jj + s)
                p.barrier(("tensor", "scalar", "vector", "gpsimd"), ["qG"])
                v_transpose()
                head_norm(8192, None)
                for j2 in range(2):
                    j = jj * 2 + j2
                    for s in range(4):
                        gather_rows(s * 2048, g1b_out, 48 + 4 * j + s)
                    p.barrier(("tensor", "scalar", "vector", "gpsimd"), ["qG"])
                    head_norm(0, j)
                    head_attn(True, j, g2b_in)
                    p.barrier()
            collective(g2b_in, g2b_out, "cc3")
            o_proj(g2b_out, WobB, True, 7, 96, wcast["Wob"])

    def emit_router():
        for dc in range(NCK):
            for g in range(4):
                p.op("tensor", lambda E, d=dc, g=g: E.matmul(
                    psB[0:8, g * 512:(g + 1) * 512], rW_sb[:, d * 8:(d + 1) * 8],
                    h_sb[:, d * TOK + g * 512:d * TOK + g * 512 + 512],
                    start=(d == 0), stop=(d == NCK - 1)),
                    inc=("pe" if (dc == NCK - 1 and g == 3) else None))
        p.wait("scalar", "pe")
        p.op("scalar", lambda E: E.activation(sfmV[:, 0:2048], psB[0:8, 0:2048], AF.Sigmoid), inc="act")
        for tc in range(NCK):
            p.wait("tensor", "act")
            if tc >= 1:
                p.wait("tensor", "dve")
            p.op("tensor", lambda E, tc=tc: E.transpose(
                psA[:, 1536:1544], sfmV[0:8, tc * 128:(tc + 1) * 128], ident_sb[0:8, 0:8]), inc="pe")
            p.wait("vector", "pe")
            m1 = tp_sb[:, 0:1]; m2 = tp_sb[:, 1:2]; r_ = tp_sb[:, 3:4]
            chain = [
                lambda E: E.tensor_copy(stm_sb[:], psA[:, 1536:1544]),
                lambda E: E.reduce_max(m1, stm_sb[:], axis=mybir.AxisListType.X),
                lambda E: E.tensor_scalar(msk1_sb[:], stm_sb[:], m1, None, ALU.subtract),
                lambda E: E.tensor_single_scalar(msk1_sb[:], msk1_sb[:], 0.0, ALU.is_equal),
                lambda E: E.scalar_tensor_tensor(msk1_sb[:], msk1_sb[:], NEG, stm_sb[:], ALU.mult, ALU.add),
                lambda E: E.reduce_max(m2, msk1_sb[:], axis=mybir.AxisListType.X),
                lambda E: E.tensor_add(tp_sb[:, 2:3], m1, m2),
                lambda E: E.reciprocal(r_, tp_sb[:, 2:3]),
                lambda E: E.tensor_single_scalar(msk1_sb[:], stm_sb[:], m2, ALU.is_ge),
                lambda E: E.scalar_tensor_tensor(
                    ctm_holder[0][:], stm_sb[:], r_, msk1_sb[:], ALU.mult, ALU.mult),
            ]
            for fn_ in chain:
                p.op("vector", fn_, inc="dve")
                p.wait("vector", "dve")
            if DBG and tc == 0 and not getattr(emit_router, "_td", False):
                emit_router._td = True
                p.barrier()
                p.dma("gpsimd", dbg_t2[:, 0:8], stm_sb[:], "qG", allow_slow_non_contiguous=True)
                p.dma("gpsimd", dbg_t2[:, 8:16], msk1_sb[:], "qG", allow_slow_non_contiguous=True)
                p.dma("gpsimd", dbg_t2[:, 16:32], tp_sb[:], "qG", allow_slow_non_contiguous=True)
                p.dma("gpsimd", dbg_t2[:, 32:40], ctm_holder[0][:], "qG", allow_slow_non_contiguous=True)
                p.barrier()
            p.wait("tensor", "dve")
            p.op("tensor", lambda E: E.transpose(psA[0:8, 1536:1664], ctm_holder[0][:], ident_sb[:]), inc="pe")
            p.wait("vector", "pe")
            p.op("vector", lambda E, tc=tc: E.tensor_copy(c_fm[:, tc * 128:(tc + 1) * 128], psA[0:8, 1536:1664]), inc="dve")
        p.barrier(("tensor", "scalar", "gpsimd"), ["dve"])
        if DBG and not getattr(emit_router, "_rd", False):
            emit_router._rd = True
            p.barrier()
            p.dma("gpsimd", dbg_rt[0:8, 0:2048], sfmV[:, 0:2048], "qG")
            p.dma("gpsimd", dbg_rt[0:8, 2048:4096], c_fm[:], "qG")
            p.barrier()

    def emit_ffn(final):
        emit_ln()
        emit_router()
        quarters = [(0, 3), (3, 6), (6, 9), (9, 11)]
        for Ei in range(10):
            routed = Ei < 8
            if routed:
                p.wait("tensor", "dve")
                for g in range(4):
                    p.op("tensor", lambda E, g=g, Ei=Ei: E.matmul(
                        psA[:, g * 512:(g + 1) * 512], sel_sb[:, Ei * 128:(Ei + 1) * 128],
                        c_fm[:, g * 512:(g + 1) * 512], start=True, stop=True),
                        inc=("pe" if g == 3 else None))
                p.wait("scalar", "pe")
                p.op("scalar", lambda E: E.copy(cb_sb[:], psA[:, 0:2048]), inc="act")
                p.wait("vector", "act")
                p.wait("tensor", "act")
            pe_fc = {}
            qc_fc = {}
            p.wait("scalar", "qG", wcast[f"w13_{Ei}"])
            p.wait("scalar", "qG", wcast[f"w2_{Ei}"])
            p.dma("scalar", wbuf[0][:], w13B[Ei, 0], "qC")
            qc_fc[0] = p.mark("qC")
            for (f0, f1) in quarters:
                nf = f1 - f0
                p.wait("scalar", "pe")
                p.dma("scalar", w2V[:, 0:nf * 2048], w2sB[Ei][:, f0 * 2048:f1 * 2048], "qC")
                w2_m = p.mark("qC")
                he_marks = []
                for fi in range(nf):
                    fc = f0 + fi
                    wb = wbuf[fc % 2]
                    if fc + 1 < NFC:
                        if fc - 1 in pe_fc:
                            p.wait("scalar", "pe", pe_fc[fc - 1])
                        p.dma("scalar", wbuf[(fc + 1) % 2][:], w13B[Ei, fc + 1], "qC")
                        qc_fc[fc + 1] = p.mark("qC")
                    p.wait("tensor", "qC", qc_fc[fc])
                    for th in range(2):
                        u = fi * 2 + th
                        ps = psA if u % 2 == 0 else psB
                        c0 = th * 1024
                        if u >= 2:
                            p.wait("tensor", "dve", he_marks[u - 2])
                        for dc in range(NCK):
                            for g in range(2):
                                p.op("tensor", lambda E, d=dc, g=g, ps=ps, c0=c0, wb=wb: E.matmul(
                                    ps[:, g * 512:(g + 1) * 512], wb[:, d * 128:(d + 1) * 128],
                                    h_sb[:, d * TOK + c0 + g * 512:d * TOK + c0 + g * 512 + 512],
                                    start=(d == 0), stop=(d == NCK - 1)))
                        for dc in range(NCK):
                            for g in range(2):
                                p.op("tensor", lambda E, d=dc, g=g, ps=ps, c0=c0, wb=wb: E.matmul(
                                    ps[:, 1024 + g * 512:1024 + (g + 1) * 512],
                                    wb[:, 2048 + d * 128:2048 + (d + 1) * 128],
                                    h_sb[:, d * TOK + c0 + g * 512:d * TOK + c0 + g * 512 + 512],
                                    start=(d == 0), stop=(d == NCK - 1)),
                                    inc=("pe" if (dc == NCK - 1 and g == 1) else None))
                        if th == 1:
                            pe_fc[fc] = p.mark("pe")
                        p.wait("scalar", "pe")
                        p.wait("scalar", "dve")
                        p.op("scalar", lambda E, ps=ps: E.activation(
                            silu_sb[:], ps[:, 0:1024], AF.Silu), inc="act")
                        p.wait("vector", "act")
                        he_sl = heV[:, fi * 2048 + c0:fi * 2048 + c0 + 1024]
                        p.op("vector", lambda E, ps=ps, he=he_sl: E.tensor_mul(he, silu_sb[:], ps[:, 1024:2048]))
                        if routed:
                            p.op("vector", lambda E, he=he_sl, c0=c0: E.tensor_mul(
                                he, he, cb_sb[:, c0:c0 + 1024]), inc="dve")
                        else:
                            p.q["vector"][-1] = (p.q["vector"][-1][0], "dve", 1, None)
                            p.cnt["dve"] += 1
                        he_marks.append(p.mark("dve"))
                if DBG and Ei == 0 and f0 == 0 and not getattr(emit_ffn, "_fd", False):
                    emit_ffn._fd = True
                    p.barrier()
                    p.dma("gpsimd", dbg_ffn[:, 0:6144], heV, "qG")
                    p.dma("gpsimd", dbg_ffn[:, 6144:8192], cb_sb[:], "qG")
                    p.barrier()
                p.wait("tensor", "qC", w2_m)
                p.wait("tensor", "dve")
                for th in range(2):
                    c0 = th * 1024
                    for op_ in range(8):
                        u = th * 8 + op_
                        ps = psA if u % 2 == 0 else psB
                        if u >= 2:
                            p.wait("tensor", "dve")
                        for fi in range(nf):
                            for oc in range(2):
                                och = op_ * 2 + oc
                                for g in range(2):
                                    p.op("tensor", lambda E, fi=fi, oc=oc, och=och, g=g, ps=ps, c0=c0, s=(fi == 0), st=(fi == nf - 1): E.matmul(
                                        ps[:, oc * 1024 + g * 512:oc * 1024 + g * 512 + 512],
                                        w2V[:, fi * 2048 + och * 128:fi * 2048 + (och + 1) * 128],
                                        heV[:, fi * 2048 + c0 + g * 512:fi * 2048 + c0 + g * 512 + 512],
                                        start=s, stop=st),
                                        inc=("pe" if (fi == nf - 1 and oc == 1 and g == 1) else None))
                        p.wait("vector", "pe")
                        for oc in range(2):
                            och = op_ * 2 + oc
                            if final:
                                p.op("vector", lambda E, ps=ps, oc=oc, och=och, c0=c0: E.tensor_add(
                                    x_sb[:, och * TOK + c0:och * TOK + c0 + 1024],
                                    x_sb[:, och * TOK + c0:och * TOK + c0 + 1024],
                                    ps[:, oc * 1024:(oc + 1) * 1024]),
                                    inc=("dve" if oc == 1 else None))
                            else:
                                p.op("vector", lambda E, ps=ps, oc=oc, och=och, c0=c0: E.tensor_add(
                                    x_sb[:, och * TOK + c0:och * TOK + c0 + 1024],
                                    x_sb[:, och * TOK + c0:och * TOK + c0 + 1024],
                                    ps[:, oc * 1024:(oc + 1) * 1024]),
                                    inc=("dve" if oc == 1 else None))
        p.barrier()

    ctm_holder = [ctx.enter_context(nc.sbuf_tensor("ctm_sb", [128, 8], F32))]


    def dump(dst, src):
        if DBG:
            p.barrier()
            p.dma("gpsimd", dst, src, "qG")
            p.barrier()

    def dump_sb(dst, src):
        if DBG:
            p.barrier()
            p.dma("sync", dst, src[:], "qS")
            p.barrier()

    if DBG:
        emit_ln._dump_h = dbg_h

    attn_sublayer(False)
    if DBG:
        dump(dbg_qkv, g1a_in)
        dump(dbg_att, g2a_in)
        dump_sb(dbg_x1, x_sb)
    emit_ffn(False)
    if DBG:
        dump_sb(dbg_x2, x_sb)
    attn_sublayer(True)
    if DBG:
        dump(dbg_qkvb, g1b_in)
        dump(dbg_attb, g2b_in)
        dump_sb(dbg_x3, x_sb)
    emit_ffn(True)

    for och in range(NCK):
        for th in range(2):
            u = och * 2 + th
            if u >= 1:
                p.wait("scalar", "qS", out_m[u - 1] if u >= 1 else 0)
            p.op("scalar", lambda E, och=och, th=th: E.copy(
                outV[:, 0:1024], x_sb[:, och * TOK + th * 1024:och * TOK + th * 1024 + 1024]), inc="act")
            p.wait("sync", "act")
            if u == 0:
                out_m = {}
            p.dma("sync", out_d[:, och * TOK + th * 1024:och * TOK + th * 1024 + 1024], outV[:, 0:1024], "qS")
            out_m[u] = p.mark("qS")
    p.wait("sync", "qS")
    p.wait("sync", "qG")
    p.wait("sync", "qC")

    with nc.Block() as block:
        def make_runner(name):
            def runner(engine):
                dyn = {}
                for fn, inc, n, dfn in p.q[name]:
                    if dfn is not None:
                        ins = dfn(engine, dyn)
                        ins.then_inc(sems[inc], n)
                    elif isinstance(fn, tuple) and fn[0] == "W":
                        engine.wait_ge(sems[fn[1]], fn[2])
                    else:
                        ins = fn(engine)
                        if inc is not None:
                            ins.then_inc(sems[inc], n)
            return runner

        r_sync = make_runner("sync"); r_scalar = make_runner("scalar")
        r_vector = make_runner("vector"); r_tensor = make_runner("tensor")
        r_gpsimd = make_runner("gpsimd")

        @block.sync
        def _(sync):
            r_sync(sync)

        @block.scalar
        def _(scalar):
            r_scalar(scalar)

        @block.vector
        def _(vector):
            r_vector(vector)

        @block.tensor
        def _(tensor):
            r_tensor(tensor)

        @block.gpsimd
        def _(gpsimd):
            r_gpsimd(gpsimd)

    ctx.close()
    return nc



def tile_stationary(W):
    d, o = W.shape
    ndc, noc = d // 128, o // 128
    return np.ascontiguousarray(
        W.reshape(ndc, 128, noc, 128).transpose(1, 2, 0, 3).reshape(128, noc * ndc * 128))


def tile_moving(W):
    d, o = W.shape
    ndc = d // 128
    return np.ascontiguousarray(W.reshape(ndc, 128, o).transpose(1, 0, 2).reshape(128, ndc * o))


def tile_a(A):
    At = np.ascontiguousarray(A.T)
    ndc = At.shape[0] // 128
    return np.ascontiguousarray(At.reshape(ndc, 128, 32).transpose(1, 0, 2).reshape(128, ndc * 32))


def tile_w13(w1, w3):
    nfc = w1.shape[1] // 128
    out = np.empty((nfc, 128, 4096), np.float32)
    for which, w in ((0, w1), (1, w3)):
        t = w.reshape(16, 128, nfc, 128).transpose(2, 1, 0, 3).reshape(nfc, 128, 2048)
        out[:, :, which * 2048:(which + 1) * 2048] = t
    return out


def tile_w2(w2):
    nfc = w2.shape[0] // 128
    return np.ascontiguousarray(
        w2.reshape(nfc, 128, 16, 128).transpose(1, 0, 2, 3).reshape(128, nfc * 2048))


_CACHE = {}


def kernel(**inputs):
    ip = {k: np.asarray(v, dtype=np.float32) for k, v in inputs.items()}
    x = ip["L0_attn_a_x"]

    shared = {}
    shared["Wqa"] = tile_stationary(ip["L0_attn_a_Wq"])
    shared["Wka"] = tile_stationary(ip["L0_attn_a_Wk"])
    shared["Wva"] = tile_stationary(ip["L0_attn_a_Wv"])
    shared["Woa"] = tile_stationary(ip["L0_attn_a_Wo"])
    shared["Wqb"] = tile_stationary(np.ascontiguousarray(ip["L0_attn_b_W_q"].T))
    shared["Wkb"] = tile_stationary(np.ascontiguousarray(ip["L0_attn_b_W_k"].T))
    shared["Wvb"] = tile_stationary(np.ascontiguousarray(ip["L0_attn_b_W_v"].T))
    shared["Wob"] = tile_stationary(np.ascontiguousarray(ip["L0_attn_b_W_o"].T))
    shared["Aall"] = np.concatenate([tile_a(ip["L0_attn_b_A_q"]), tile_a(ip["L0_attn_b_A_k"]),
                                     tile_a(ip["L0_attn_b_A_v"]), tile_a(ip["L0_attn_b_A_o"])], axis=1)
    shared["Ball"] = np.concatenate([np.ascontiguousarray(ip["L0_attn_b_B_q"].T),
                                     np.ascontiguousarray(ip["L0_attn_b_B_k"].T),
                                     np.ascontiguousarray(ip["L0_attn_b_B_v"].T),
                                     np.ascontiguousarray(ip["L0_attn_b_B_o"].T)], axis=1)
    rt = ip["L0_ffn_w_router"]
    shared["rW"] = np.ascontiguousarray(
        rt.reshape(16, 128, 8).transpose(1, 0, 2).reshape(128, 128))
    w13l = [tile_w13(ip["L0_ffn_w1"][e], ip["L0_ffn_w3"][e]) for e in range(8)]
    ws1, ws3, ws2 = ip["L0_ffn_ws1"], ip["L0_ffn_ws3"], ip["L0_ffn_ws2"]
    w13l.append(tile_w13(ws1[:, 0:1408], ws3[:, 0:1408]))
    w13l.append(tile_w13(ws1[:, 1408:2816], ws3[:, 1408:2816]))
    shared["w13"] = np.stack(w13l)
    w2l = [tile_w2(ip["L0_ffn_w2"][e]) for e in range(8)]
    w2l.append(tile_w2(ws2[0:1408]))
    w2l.append(tile_w2(ws2[1408:2816]))
    shared["w2s"] = np.stack(w2l)

    tqi = np.arange(QT)[None, :]
    tsi = np.arange(128)[:, None]
    mA = np.zeros((128, 4 * QT), ml_dtypes.bfloat16)
    mB = np.zeros((128, 4 * QT), ml_dtypes.bfloat16)
    for d_ in range(4):
        keep = (tqi - d_ * 128) >= tsi
        mA[:, d_ * QT:(d_ + 1) * QT] = keep.astype(ml_dtypes.bfloat16)
        mB[:, d_ * QT:(d_ + 1) * QT] = np.where(keep, 0.0, NEG).astype(ml_dtypes.bfloat16)
    shared["maskAi"] = mA
    shared["maskBi"] = mB
    shared["identF"] = np.eye(128, dtype=np.float32)
    shared["onesC"] = np.ones((128, 1), np.float32)
    shared["onesCb"] = np.ones((128, 1), ml_dtypes.bfloat16)
    shared["onesR"] = np.ones((1, 128), np.float32)
    g_ = np.arange(16)
    shared["UTc"] = (g_[:, None] > g_[None, :]).astype(np.float32)
    sel = np.zeros((8, 8 * 128), ml_dtypes.bfloat16)
    for e in range(8):
        sel[e, e * 128:(e + 1) * 128] = 1
    shared["sel8"] = sel
    shared["svec"] = np.array([[ip["L0_attn_b_s_q"][0], ip["L0_attn_b_s_k"][0],
                                ip["L0_attn_b_s_v"][0], ip["L0_attn_b_s_o"][0]]], np.float32)
    shared["epsv"] = np.array([[LN_EPS, RMS_EPS]], np.float32)

    inv = 1.0 / (10000.0 ** (np.arange(0, 64, 2, dtype=np.float32) / 64.0))
    freqs = np.outer(np.arange(T, dtype=np.float32), inv)
    cosf = np.cos(freqs).T
    sinf = np.sin(freqs).T

    in_maps = []
    for c in range(8):
        b, r = c // 4, c % 4
        xs = np.ascontiguousarray(x[b, r * TOK:(r + 1) * TOK, :].T)
        m = dict(shared)
        m["x_in"] = np.ascontiguousarray(
            xs.reshape(16, 128, TOK).transpose(1, 0, 2).reshape(128, 16 * TOK))
        cs = np.empty((64, 2 * TOK), ml_dtypes.bfloat16)
        cs[0:32, 0:TOK] = cosf[:, r * TOK:(r + 1) * TOK].astype(ml_dtypes.bfloat16)
        cs[32:64, 0:TOK] = sinf[:, r * TOK:(r + 1) * TOK].astype(ml_dtypes.bfloat16)
        cs[0:32, TOK:] = sinf[:, r * TOK:(r + 1) * TOK].astype(ml_dtypes.bfloat16)
        cs[32:64, TOK:] = cosf[:, r * TOK:(r + 1) * TOK].astype(ml_dtypes.bfloat16)
        m["csO"] = cs
        m["gain4"] = np.ascontiguousarray(ip["L0_attn_b_q_gain"][4 * r:4 * r + 4][None, :])
        pp = np.arange(128, dtype=np.uint32)
        gi = np.zeros((128, 112), np.uint32)
        for j in range(4):
            for s_ in range(4):
                gi[:, 12 * j + s_] = 512 * r + s_ * 6144 + j * 128 + pp
                gi[:, 12 * j + 4 + s_] = 512 * r + s_ * 6144 + 2048 + j * 128 + pp
                gi[:, 12 * j + 8 + s_] = 512 * r + s_ * 6144 + 4096 + j * 128 + pp
        for j in range(4):
            for s_ in range(4):
                gi[:, 48 + 4 * j + s_] = 512 * r + s_ * 4096 + j * 128 + pp
        for jj in range(2):
            for s_ in range(4):
                gi[:, 64 + 4 * jj + s_] = 256 * r + s_ * 4096 + 2048 + jj * 128 + pp
                gi[:, 72 + 4 * jj + s_] = 256 * r + s_ * 4096 + 3072 + jj * 128 + pp
        for hdc in range(16):
            gi[:, 80 + hdc] = (hdc * 128 + pp) * 4 + r
            gi[:, 96 + hdc] = (hdc * 128 + pp) * 4 + r
        m["gidx"] = gi
        in_maps.append(m)

    if "nc" not in _CACHE:
        _CACHE["nc"] = build_program()
    nc = _CACHE["nc"]
    res = run_bass_kernel_spmd(nc, in_maps, core_ids=list(range(8)))

    out = np.empty((2, T, D), np.float32)
    for c in range(8):
        b, r = c // 4, c % 4
        o = res.results[c]["out_d"]
        out[b, r * TOK:(r + 1) * TOK, :] = o.reshape(128, 16, TOK).transpose(1, 0, 2).reshape(D, TOK).T
    return out


# version 10
# speedup vs baseline: 1.0760x; 1.0438x over previous
import numpy as np
import ml_dtypes
from collections import defaultdict
from contextlib import ExitStack

import concourse.bass as bass
import concourse.mybir as mybir
from concourse.expressions import RuntimeValue
from concourse.bass_utils import run_bass_kernel_spmd

F32 = mybir.dt.float32
BF16 = mybir.dt.bfloat16
AF = mybir.ActivationFunctionType
ALU = mybir.AluOpType
ds = bass.ds

D = 2048
T = 8192
TOK = 2048
NCK = 16
QT = 512
NQT = T // QT
NFC = 11
SCALE_A = float(1.0 / np.float64(np.sqrt(np.float32(T))))
SCALE_B = float(1.0 / np.float64(np.sqrt(np.float32(128))))
BOUND_B = float(np.sqrt(np.float64(128.0)))
LN_EPS = 1e-5
RMS_EPS = 1.1920929e-07
NEG = -1.0e9

ENGINES = ("sync", "scalar", "vector", "tensor", "gpsimd")
DBG = False


class Rec:

    def __init__(self):
        self.q = {e: [] for e in ENGINES}
        self.cnt = defaultdict(int)
        self.dynm = {}
        self.dyn = None

    def op(self, eng, fn, inc=None, n=1):
        self.q[eng].append((fn, inc, n, None))
        if inc:
            self.cnt[inc] += n
        return self.cnt[inc] if inc else None

    def dma(self, eng, out, in_, inc="q" + "X", **kw):
        return self.op(eng, lambda E, o=out, i=in_, k=kw: E.dma_start(out=o, in_=i, **k), inc=inc, n=16)

    def dmad(self, eng, fn, inc):
        self.q[eng].append((None, inc, 16, fn))
        self.cnt[inc] += 16
        return self.cnt[inc]

    def wait(self, eng, sem, val=None):
        v = self.cnt[sem] if val is None else val
        assert v <= self.cnt[sem], (sem, v, self.cnt[sem])
        self.q[eng].append((("W", sem, v), None, 0, None))

    def mark(self, sem):
        return self.cnt[sem]

    def barrier(self, engines=ENGINES, sems=None):
        if sems is None:
            sems = [s for s in self.cnt if not s.startswith("cc")]
        for e in engines:
            for s in sems:
                if self.cnt[s] > 0:
                    self.wait(e, s, self.cnt[s])


def build_program():
    nc = bass.Bass("TRN2", target_bir_lowering=False, detect_race_conditions=False)
    dr = lambda name, shape, dt, kind="ExternalInput": nc.dram_tensor(name, shape, dt, kind=kind).ap()

    x_in = dr("x_in", [128, NCK * TOK], F32)
    Wqa = dr("Wqa", [128, 32768], F32); Wka = dr("Wka", [128, 32768], F32)
    Wva = dr("Wva", [128, 32768], F32); Woa = dr("Woa", [128, 32768], F32)
    Wqb = dr("Wqb", [128, 32768], F32); Wkb = dr("Wkb", [128, 16384], F32)
    Wvb = dr("Wvb", [128, 16384], F32); Wob = dr("Wob", [128, 32768], F32)
    Aall = dr("Aall", [128, 4 * 512], F32)
    Ball = dr("Ball", [32, 6144], F32)
    rW = dr("rW", [128, 128], F32)
    w13 = dr("w13", [10, NFC, 128, 4096], F32)
    w2s = dr("w2s", [10, 128, NFC * 2048], F32)
    csO = dr("csO", [64, 2 * TOK], BF16)
    maskAi = dr("maskAi", [128, 4 * QT], BF16)
    maskBi = dr("maskBi", [128, 4 * QT], BF16)
    identF = dr("identF", [128, 128], F32)
    onesC = dr("onesC", [128, 1], F32)
    onesCb = dr("onesCb", [128, 1], BF16)
    onesR = dr("onesR", [1, 128], F32)
    UTc = dr("UTc", [16, 16], F32)
    sel8 = dr("sel8", [8, 8 * 128], BF16)
    gain4 = dr("gain4", [1, 4], F32)
    gidx = dr("gidx", [128, 112], mybir.dt.uint32)
    svec = dr("svec", [1, 4], F32)
    epsv = dr("epsv", [1, 2], F32)
    out_d = dr("out_d", [128, NCK * TOK], F32, kind="ExternalOutput")
    if DBG:
        dbg_h = dr("dbg_h", [128, NCK * TOK], BF16, kind="ExternalOutput")
        dbg_qkv = dr("dbg_qkv", [6144, 2048], BF16, kind="ExternalOutput")
        dbg_att = dr("dbg_att", [512, T], BF16, kind="ExternalOutput")
        dbg_x1 = dr("dbg_x1", [128, NCK * TOK], BF16, kind="ExternalOutput")
        dbg_x2 = dr("dbg_x2", [128, NCK * TOK], BF16, kind="ExternalOutput")
        dbg_x3 = dr("dbg_x3", [128, NCK * TOK], BF16, kind="ExternalOutput")
        dbg_qkvb = dr("dbg_qkvb", [4096, 2048], BF16, kind="ExternalOutput")
        dbg_attb = dr("dbg_attb", [512, T], BF16, kind="ExternalOutput")
        dbg_fine = dr("dbg_fine", [128, 6144], F32, kind="ExternalOutput")
        dbg_ffn = dr("dbg_ffn", [128, 8192], F32, kind="ExternalOutput")
        dbg_rt = dr("dbg_rt", [8, 4096], F32, kind="ExternalOutput")
        dbg_t2 = dr("dbg_t2", [128, 64], F32, kind="ExternalOutput")

    WqaB = dr("WqaB", [128, 32768], BF16, kind="Internal")
    WkaB = dr("WkaB", [128, 32768], BF16, kind="Internal")
    WvaB = dr("WvaB", [128, 32768], BF16, kind="Internal")
    WoaB = dr("WoaB", [128, 32768], BF16, kind="Internal")
    WqbB = dr("WqbB", [128, 32768], BF16, kind="Internal")
    WkbB = dr("WkbB", [128, 16384], BF16, kind="Internal")
    WvbB = dr("WvbB", [128, 16384], BF16, kind="Internal")
    WobB = dr("WobB", [128, 32768], BF16, kind="Internal")
    w13B = dr("w13B", [10, NFC, 128, 4096], BF16, kind="Internal")
    w2sB = dr("w2sB", [10, 128, NFC * 2048], BF16, kind="Internal")
    g1a_in = dr("g1a_in", [6144, 2048], BF16, kind="Internal")
    g1a_out = dr("g1a_out", [24576, 2048], BF16, kind="Internal")
    g1b_in = dr("g1b_in", [4096, 2048], BF16, kind="Internal")
    g1b_out = dr("g1b_out", [16384, 2048], BF16, kind="Internal")
    g2a_in = dr("g2a_in", [512, T], BF16, kind="Internal")
    g2a_out = dr("g2a_out", [2048, T], BF16, kind="Internal")
    g2b_in = dr("g2b_in", [512, T], BF16, kind="Internal")
    g2b_out = dr("g2b_out", [2048, T], BF16, kind="Internal")

    ctx = ExitStack()
    sb = lambda name, shape, dt: ctx.enter_context(nc.sbuf_tensor(name, shape, dt))
    x_sb = sb("x_sb", [128, NCK * TOK], BF16)
    h_sb = sb("h_sb", [128, NCK * TOK], BF16)
    wbuf = [sb(f"wbuf{i}", [128, 4096], BF16) for i in range(2)]
    stg = [sb(f"stg{i}", [128, 1024], BF16) for i in range(2)]
    row_sb = sb("row_sb", [1, 1024], F32)
    sq_sb = sb("sq_sb", [128, 512], F32)
    silu_sb = sb("silu_sb", [128, 1024], BF16)
    cb2 = [sb(f"cb2_{i}", [128, 2048], BF16) for i in range(2)]
    c_fm = sb("c_fm", [8, 2048], BF16)

    rt_sb = sb("rt_sb", [64, 1024], BF16)
    mask_sb = sb("mask_sb", [128, 4 * QT], BF16)
    ident_sb = sb("ident_sb", [128, 128], F32)
    sel_sb = sb("sel_sb", [8, 8 * 128], BF16)
    onesC_sb = sb("onesC_sb", [128, 1], F32)
    onesCb_sb = sb("onesCb_sb", [128, 1], BF16)
    onesR_sb = sb("onesR_sb", [1, 128], F32)
    UT_sb = sb("UT_sb", [16, 16], F32)
    rW_sb = sb("rW_sb", [128, 128], BF16)
    gb_sb = sb("gb_sb", [128, 16], F32)
    gidx_sb = sb("gidx_sb", [128, 112], mybir.dt.uint32)
    tp_sb = sb("tp_sb", [128, 16], F32)
    stm_sb = sb("stm_sb", [128, 8], F32)
    msk1_sb = sb("msk1_sb", [128, 8], F32)
    cv_sb = sb("cv_sb", [128, 16], F32)
    cvT_sb = sb("cvT_sb", [16, 128], F32)
    sv_sb = sb("sv_sb", [128, 16], F32)
    AR = sb("AR", [128, 12288], BF16)

    psA = ctx.enter_context(nc.psum_tensor("psA", [128, 2048], F32))
    psB = ctx.enter_context(nc.psum_tensor("psB", [128, 2048], F32))

    sem_names = ["qS", "qC", "qG", "pe", "act", "dve", "po",
                 "cc0", "cc1", "cc2", "cc3"]
    sems = {n: ctx.enter_context(nc.semaphore(n)) for n in sem_names}
    p = Rec()

    heV = AR[:, 0:6144]
    w2V = AR[:, 6144:12288]
    csV = AR[0:64, 0:4096]
    uV = AR[0:32, 4096:6144]
    BV = AR[0:32, 6144:8192]
    AVt = AR[:, 8192:8704]
    attnT2 = [AR[:, 0:2048], AR[:, 2048:4096]]
    S2s = [AR[:, 4096:8192].bitcast(F32), AR[:, 8192:12288].bitcast(F32)]
    sfmV = AR[0:8, 4608:8704].bitcast(F32)
    outV = AR[:, 0:2048].bitcast(F32)

    p.dma("sync", ident_sb[:], identF, "qS")
    p.dma("sync", onesC_sb[:], onesC, "qS")
    p.dma("sync", onesCb_sb[:], onesCb, "qS")
    p.dma("sync", onesR_sb[:], onesR, "qS")
    p.dma("sync", UT_sb[:], UTc, "qS")
    p.dma("sync", sel_sb[:], sel8, "qS")
    p.dma("sync", row_sb[0:1, 0:4], gain4, "qS")
    p.dma("sync", row_sb[0:1, 4:8], svec, "qS")
    p.dma("sync", gb_sb[0:1, 12:14], epsv, "qS", allow_slow_non_contiguous=True)
    p.dma("gpsimd", rW_sb[:], rW, "qG")
    p.dma("gpsimd", x_sb[:], x_in, "qG")
    p.dma("sync", gidx_sb[:], gidx, "qS")
    p.barrier()
    p.op("tensor", lambda E: E.matmul(psA[:, 0:8], onesR_sb[:], row_sb[0:1, 0:8],
                                      start=True, stop=True), inc="pe")
    p.wait("vector", "pe")
    p.op("vector", lambda E: E.tensor_copy(gb_sb[:, 0:8], psA[:, 0:8]), inc="dve")
    p.wait("scalar", "dve")
    p.op("scalar", lambda E: E.activation(gb_sb[:, 8:12], gb_sb[:, 0:4], AF.Abs), inc="act")
    p.wait("vector", "act")
    p.op("vector", lambda E: E.tensor_scalar_mul(gb_sb[:, 8:12], gb_sb[:, 8:12], -BOUND_B), inc="dve")
    p.barrier()

    wcast = {}
    cast_pending = [("Wqa", WqaB, Wqa), ("Wka", WkaB, Wka), ("Wva", WvaB, Wva),
                    ("Woa", WoaB, Woa), ("w13_0", w13B[0], w13[0]), ("w2_0", w2sB[0], w2s[0])]
    for e in range(1, 10):
        cast_pending.append((f"w13_{e}", w13B[e], w13[e]))
        cast_pending.append((f"w2_{e}", w2sB[e], w2s[e]))
    for nm, dst, src in [("Wqb", WqbB, Wqb), ("Wkb", WkbB, Wkb), ("Wvb", WvbB, Wvb),
                         ("Wob", WobB, Wob)]:
        cast_pending.append((nm, dst, src))

    def drain_casts(n):
        for _ in range(n):
            if not cast_pending:
                return
            nm, dst, src = cast_pending.pop(0)
            p.dma("gpsimd", dst, src, "qG")
            wcast[nm] = p.mark("qG")

    drain_casts(6)

    def emit_ln():
        for tq4 in range(4):
            c0 = tq4 * 512
            for dc in range(NCK):
                p.wait("scalar", "pe")
                p.op("scalar", lambda E, d=dc, c0=c0: E.activation(
                    sq_sb[:], x_sb[:, d * TOK + c0:d * TOK + c0 + 512], AF.Square), inc="act")
                p.wait("tensor", "act")
                p.op("tensor", lambda E, d=dc, c0=c0: E.matmul(
                    psA[0:1, 0:512], onesCb_sb[:], x_sb[:, d * TOK + c0:d * TOK + c0 + 512],
                    start=(d == 0), stop=(d == NCK - 1)))
                p.op("tensor", lambda E, d=dc: E.matmul(
                    psA[0:1, 512:1024], onesC_sb[:], sq_sb[:, 0:512],
                    start=(d == 0), stop=(d == NCK - 1)), inc="pe")
            p.wait("vector", "pe")
            mu = row_sb[0:1, 0:512]
            wk = row_sb[0:1, 512:1024]
            p.op("vector", lambda E: E.tensor_scalar_mul(mu, psA[0:1, 0:512], 1.0 / D))
            p.op("vector", lambda E: E.tensor_mul(wk, mu, mu))
            p.op("vector", lambda E: E.scalar_tensor_tensor(
                wk, psA[0:1, 512:1024], 1.0 / D, wk, ALU.mult, ALU.subtract), inc="dve")
            p.wait("scalar", "dve")
            p.op("scalar", lambda E: E.activation(wk, wk, AF.Sqrt, bias=gb_sb[0:1, 12:13]), inc="act")
            p.wait("vector", "act")
            p.op("vector", lambda E: E.reciprocal(wk, wk))
            p.op("vector", lambda E: E.scalar_tensor_tensor(mu, mu, -1.0, wk, ALU.mult, ALU.mult), inc="dve")
            p.wait("tensor", "dve")
            p.op("tensor", lambda E: E.matmul(psB[:, 0:512], onesR_sb[:], wk, start=True, stop=True))
            p.op("tensor", lambda E: E.matmul(psB[:, 512:1024], onesR_sb[:], mu, start=True, stop=True), inc="pe")
            p.wait("vector", "pe")
            for dc in range(NCK):
                p.op("vector", lambda E, d=dc, c0=c0: E.tensor_mul(
                    h_sb[:, d * TOK + c0:d * TOK + c0 + 512],
                    x_sb[:, d * TOK + c0:d * TOK + c0 + 512], psB[:, 0:512]))
                p.op("vector", lambda E, d=dc, c0=c0: E.tensor_add(
                    h_sb[:, d * TOK + c0:d * TOK + c0 + 512],
                    h_sb[:, d * TOK + c0:d * TOK + c0 + 512], psB[:, 512:1024]),
                    inc=("dve" if dc == NCK - 1 else None))
            p.barrier(("tensor", "scalar"), ["dve"])
        p.barrier()

    def load_AB(acol, bcol, blen):
        p.dma("gpsimd", AVt, Aall[:, acol:acol + 512], "qG")
        p.dma("gpsimd", BV[:, 0:blen], Ball[:, bcol:bcol + blen], "qG")

    def emit_u():
        p.barrier(("tensor",), ["qG"])
        for dc in range(NCK):
            for g in range(4):
                p.op("tensor", lambda E, d=dc, g=g: E.matmul(
                    psA[0:32, g * 512:(g + 1) * 512], AVt[:, d * 32:(d + 1) * 32],
                    h_sb[:, d * TOK + g * 512:d * TOK + g * 512 + 512],
                    start=(d == 0), stop=(d == NCK - 1)),
                    inc=("pe" if (dc == NCK - 1 and g == 3) else None))
        p.wait("scalar", "pe")
        p.op("scalar", lambda E: E.copy(uV[:, 0:2048], psA[0:32, 0:2048]), inc="act")
        p.wait("tensor", "act")

    def need_cast(nm):
        while nm not in wcast and cast_pending:
            drain_casts(1)
        return wcast[nm]

    def fm_proj(Wt, noc, dst, dst_rows, wmark, lora=False, scale_col=None, rope=False):
        pe_done = {}
        cons = {}
        store_done = {}
        qc_m = {}
        p.wait("scalar", "qG", wmark)
        p.dma("scalar", wbuf[0][:, 0:2048], Wt[:, 0:2048], "qC")
        qc_m[0] = p.mark("qC")
        for och in range(noc):
            wb = wbuf[och % 2]
            if och + 1 < noc:
                if och >= 1:
                    p.wait("scalar", "pe", pe_done[och - 1])
                p.dma("scalar", wbuf[(och + 1) % 2][:, 0:2048],
                      Wt[:, (och + 1) * 2048:(och + 2) * 2048], "qC")
                qc_m[och + 1] = p.mark("qC")
            p.wait("tensor", "qC", qc_m[och])
            for th in range(2):
                u = och * 2 + th
                ps = psA if u % 2 == 0 else psB
                c0 = th * 1024
                if u >= 2:
                    p.wait("tensor", *cons[u - 2])
                for dc in range(NCK):
                    for g in range(2):
                        p.op("tensor", lambda E, d=dc, g=g, ps=ps, c0=c0, wb=wb: E.matmul(
                            ps[:, g * 512:(g + 1) * 512], wb[:, d * 128:(d + 1) * 128],
                            h_sb[:, d * TOK + c0 + g * 512:d * TOK + c0 + g * 512 + 512],
                            start=(d == 0), stop=(d == NCK - 1)))
                if lora:
                    for g in range(2):
                        p.op("tensor", lambda E, g=g, ps=ps, c0=c0, och=och: E.matmul(
                            ps[:, 1024 + g * 512:1024 + (g + 1) * 512],
                            BV[:, och * 128:(och + 1) * 128],
                            uV[:, c0 + g * 512:c0 + g * 512 + 512],
                            start=True, stop=True),
                            inc=("pe" if g == 1 else None))
                else:
                    p.q["tensor"][-1] = (p.q["tensor"][-1][0], "pe", 1, None)
                    p.cnt["pe"] += 1
                if th == 1:
                    pe_done[och] = p.mark("pe")
                p.wait("scalar", "pe")
                st_ = stg[u % 2]
                if u >= 2:
                    p.wait("scalar", "qS", store_done[u - 2])
                if scale_col is not None:
                    p.op("scalar", lambda E, ps=ps, st=st_, sc=scale_col: E.activation(
                        st[:], ps[:, 0:1024], AF.Copy, scale=gb_sb[:, sc:sc + 1]), inc="act")
                else:
                    p.op("scalar", lambda E, ps=ps, st=st_: E.copy(st[:], ps[:, 0:1024]), inc="act")
                if lora:
                    p.wait("vector", "act")
                    p.op("vector", lambda E, ps=ps, st=st_: E.tensor_add(st[:], st[:], ps[:, 1024:2048]), inc="dve")
                    cons[u] = ("dve", p.mark("dve"))
                else:
                    cons[u] = ("act", p.mark("act"))
                if rope:
                    p.wait("gpsimd", *cons[u])
                    t_b0 = rt_sb[0:32, 0:1024]
                    t_b32 = rt_sb[32:64, 0:1024]
                    co_b0 = csV[0:32, c0:c0 + 1024]
                    si_b32 = csV[32:64, c0:c0 + 1024]
                    si_b0 = csV[0:32, TOK + c0:TOK + c0 + 1024]
                    co_b32 = csV[32:64, TOK + c0:TOK + c0 + 1024]
                    q1 = st_[0:32, :]
                    q2 = st_[32:64, :]
                    p.op("gpsimd", lambda E, q1=q1, si=si_b0, t=t_b32: E.tensor_mul(t, q1, si))
                    p.op("gpsimd", lambda E, q1=q1, co=co_b0: E.tensor_mul(q1, q1, co))
                    p.op("gpsimd", lambda E, q2=q2, si=si_b32, t=t_b0: E.tensor_mul(t, q2, si))
                    p.op("gpsimd", lambda E, q1=q1, t=t_b0: E.tensor_add(q1, q1, t))
                    p.op("gpsimd", lambda E, q2=q2, co=co_b32: E.tensor_mul(q2, q2, co))
                    p.op("gpsimd", lambda E, q2=q2, t=t_b32: E.tensor_sub(q2, q2, t), inc="po")
                    cons[u] = ("po", p.mark("po"))
                p.wait("sync", *cons[u])
                p.dma("sync", dst[dst_rows + och * 128:dst_rows + (och + 1) * 128, c0:c0 + 1024], st_[:], "qS")
                store_done[u] = p.mark("qS")
        p.barrier(("gpsimd", "vector", "scalar"), ["qS"])

    IOA = bass.IndirectOffsetOnAxis

    def gather_rows(dst_cols, src, idxcol):
        p.dmad("gpsimd", lambda E, dyn, dc=dst_cols, src=src, ic=idxcol: E.indirect_dma_start(
            h_sb[:, dc:dc + 2048], None, src, IOA(gidx_sb[:, ic:ic + 1], 0)), "qG")

    def v_transpose(n_tiles=64):
        marks = {}
        for t in range(n_tiles):
            stf = sq_sb[:, (t % 2) * 128:(t % 2) * 128 + 128]
            pslot = psA[:, 1024 + (t % 2) * 512:1024 + (t % 2) * 512 + 128]
            if t >= 2:
                p.wait("scalar", "pe", marks[t - 2][0])
            p.op("scalar", lambda E, t=t, stf=stf: E.copy(
                stf, h_sb[:, 16384 + t * 128:16384 + (t + 1) * 128]), inc="act")
            p.wait("tensor", "act")
            if t >= 2:
                p.wait("tensor", "dve", marks[t - 2][1])
            p.op("tensor", lambda E, stf=stf, ps=pslot: E.transpose(ps, stf, ident_sb[:]), inc="pe")
            p.wait("vector", "pe")
            p.op("vector", lambda E, t=t, ps=pslot: E.tensor_copy(
                h_sb[:, 24576 + t * 128:24576 + (t + 1) * 128], ps), inc="dve")
            marks[t] = (p.mark("pe"), p.mark("dve"))
        p.barrier(("tensor", "scalar", "gpsimd"), ["dve"])

    def collective(cin, cout, ccname):
        p.barrier(("gpsimd",), ["qS", "qG", "dve", "act", "po", "pe"])
        p.op("gpsimd", lambda E, i=cin, o=cout: E.collective_compute(
            "AllGather", ALU.bypass, replica_groups=[[0, 1, 2, 3], [4, 5, 6, 7]],
            ins=[i.opt()], outs=[o.opt()]), inc=ccname, n=1)
        for e in ENGINES:
            p.wait(e, ccname, 1)

    def head_norm(colbase, gaincol):
        for blk in range(16):
            c0 = colbase + blk * 512
            p.wait("scalar", "pe")
            p.op("scalar", lambda E, c0=c0: E.activation(
                sq_sb[:], h_sb[:, c0:c0 + 512], AF.Square), inc="act")
            p.wait("tensor", "act")
            p.op("tensor", lambda E: E.matmul(
                psA[0:1, 0:512], onesC_sb[:], sq_sb[:], start=True, stop=True), inc="pe")
            p.wait("scalar", "pe")
            p.op("scalar", lambda E: E.activation(row_sb[0:1, 0:512], psA[0:1, 0:512],
                                                  AF.Sqrt, bias=gb_sb[0:1, 13:14], scale=1.0 / 128.0), inc="act")
            p.wait("vector", "act")
            p.op("vector", lambda E: E.reciprocal(row_sb[0:1, 0:512], row_sb[0:1, 0:512]), inc="dve")
            p.wait("tensor", "dve")
            p.op("tensor", lambda E: E.matmul(
                psA[:, 1024:1536], onesR_sb[:], row_sb[0:1, 0:512], start=True, stop=True), inc="pe")
            p.wait("vector", "pe")
            if gaincol is not None:
                p.op("vector", lambda E, c0=c0, gc=gaincol: E.scalar_tensor_tensor(
                    h_sb[:, c0:c0 + 512], h_sb[:, c0:c0 + 512], gb_sb[:, gc:gc + 1],
                    psA[:, 1024:1536], ALU.mult, ALU.mult), inc="dve")
            else:
                p.op("vector", lambda E, c0=c0: E.tensor_mul(
                    h_sb[:, c0:c0 + 512], h_sb[:, c0:c0 + 512], psA[:, 1024:1536]), inc="dve")
            p.wait("tensor", "dve")
            p.wait("scalar", "dve")

    def head_attn(is_b, j, g2in):
        qh = h_sb[:, 0:8192]
        kh = h_sb[:, 8192:16384]
        vhf = h_sb[:, 16384:24576]
        vh = h_sb[:, 24576:32768]
        if not is_b:
            for g in range(16):
                p.op("vector", lambda E, g=g: E.reduce_sum(
                    cv_sb[:, g:g + 1], vhf[:, g * 512:(g + 1) * 512], axis=mybir.AxisListType.X),
                    inc="dve")
                p.wait("vector", "dve")
            p.wait("tensor", "dve")
            p.op("tensor", lambda E: E.transpose(psA[0:16, 0:128], cv_sb[:], ident_sb[:]), inc="pe")
            p.wait("vector", "pe")
            p.op("vector", lambda E: E.tensor_copy(cvT_sb[:], psA[0:16, 0:128]), inc="dve")
            p.wait("tensor", "dve")
            p.op("tensor", lambda E: E.matmul(psA[:, 0:16], cvT_sb[:], UT_sb[:],
                                              start=True, stop=True), inc="pe")
            p.wait("vector", "pe")
            p.op("vector", lambda E: E.tensor_copy(sv_sb[:], psA[:, 0:16]), inc="dve")
            p.wait("tensor", "dve")

        avs = [psB[:, 0:512], psB[:, 512:1024]]
        zs = [psB[0:1, 1024:1536], psB[0:1, 1536:2048]]
        bcs = [psB[:, 1024:1536], psB[:, 1536:2048]]
        rows = [row_sb[0:1, 0:512], row_sb[0:1, 512:1024]]
        rzVs = [sq_sb.bitcast(BF16)[:, 0:512], sq_sb.bitcast(BF16)[:, 512:1024]]
        exp_m = {}; av_m = {}; s2_m = {}
        zmm_m = {}; recip_m = {}; rzcopy_m = {}; norm_m = {}; store_m = {}
        av_last = {}
        gq = [0]

        def tail1(q):
            sl = q % 2
            if not is_b:
                p.wait("vector", "pe", av_last[q])
                p.op("vector", lambda E, sl=sl, q=q: E.tensor_scalar_add(
                    avs[sl], avs[sl], sv_sb[:, q:q + 1]), inc="dve")
            if q >= 2:
                p.wait("tensor", "act", rzcopy_m[q - 2])
            p.wait("tensor", "dve")
            for g in range(4):
                p.op("tensor", lambda E, g=g, sl=sl: E.matmul(
                    zs[sl], onesC_sb[:], S2s[sl][:, g * 512:(g + 1) * 512],
                    start=(g == 0), stop=(g == 3)), inc=("pe" if g == 3 else None))
            zmm_m[q] = p.mark("pe")
            p.wait("vector", "pe", zmm_m[q])
            if not is_b and q < 15:
                p.op("vector", lambda E, sl=sl, q=q: E.tensor_scalar_add(
                    zs[sl], zs[sl], float(128 * (60 - 4 * q))), inc="dve")
                p.wait("vector", "dve")
            p.op("vector", lambda E, sl=sl: E.reciprocal(rows[sl], zs[sl]), inc="dve")
            recip_m[q] = p.mark("dve")

        def tail2(q):
            sl = q % 2
            if q >= 2:
                p.wait("tensor", "act", rzcopy_m[q - 2])
            p.wait("tensor", "dve", recip_m[q])
            p.op("tensor", lambda E, sl=sl: E.matmul(
                bcs[sl], onesR_sb[:], rows[sl], start=True, stop=True), inc="pe")
            bc_m = p.mark("pe")
            p.wait("scalar", "pe", bc_m)
            if q >= 2:
                p.wait("scalar", "dve", norm_m[q - 2])
            p.op("scalar", lambda E, sl=sl: E.copy(rzVs[sl], bcs[sl]), inc="act")
            rzcopy_m[q] = p.mark("act")
            p.wait("vector", "act", rzcopy_m[q])
            if q >= 2:
                p.wait("vector", "qS", store_m[q - 2])
            p.op("vector", lambda E, sl=sl: E.tensor_mul(
                stg[sl][:, 0:512], avs[sl], rzVs[sl]), inc="dve")
            norm_m[q] = p.mark("dve")
            p.wait("sync", "dve", norm_m[q])
            p.dma("sync", g2in[j * 128:(j + 1) * 128, q * QT:(q + 1) * QT], stg[sl][:, 0:512], "qS")
            store_m[q] = p.mark("qS")

        for q in range(NQT):
            nq = q + 1
            sl = q % 2
            if q >= 2:
                p.wait("vector", "pe", zmm_m[q - 2])
            p.op("vector", lambda E, sl=sl: E.memset(S2s[sl][:], 0.0), inc="dve")
            for qd in range(nq):
                g_ = gq[0]
                at = attnT2[g_ % 2]
                diag = (qd == nq - 1)
                if g_ >= 1:
                    p.wait("tensor", "act", exp_m[g_ - 1])
                for ci in range(4):
                    c = 4 * qd + ci
                    p.op("tensor", lambda E, c=c, q=q, ci=ci: E.matmul(
                        psA[:, ci * 512:(ci + 1) * 512], kh[:, c * 128:(c + 1) * 128],
                        qh[:, q * QT:(q + 1) * QT], start=True, stop=True),
                        inc=("pe" if ci == 3 else None))
                pe_m = p.mark("pe")
                if diag:
                    p.wait("vector", "pe", pe_m)
                    op_ = "tensor_add" if is_b else "tensor_mul"
                    for ci in range(4):
                        p.op("vector", lambda E, ci=ci, op_=op_: getattr(E, op_)(
                            psA[:, ci * 512:(ci + 1) * 512], psA[:, ci * 512:(ci + 1) * 512],
                            mask_sb[:, ci * QT:(ci + 1) * QT]),
                            inc=("dve" if ci == 3 else None))
                    p.wait("scalar", "dve")
                else:
                    p.wait("scalar", "pe", pe_m)
                if g_ >= 2:
                    p.wait("scalar", "pe", av_m[g_ - 2])
                    p.wait("scalar", "dve", s2_m[g_ - 2])
                if is_b:
                    p.op("scalar", lambda E, at=at, j=j: E.activation(
                        at, psA[:], AF.Exp, bias=gb_sb[:, 8 + j:9 + j], scale=SCALE_B), inc="act")
                else:
                    p.op("scalar", lambda E, at=at: E.activation(
                        at, psA[:], AF.Exp, scale=SCALE_A), inc="act")
                exp_m[g_] = p.mark("act")
                p.wait("vector", "act", exp_m[g_])
                p.op("vector", lambda E, at=at, sl=sl: E.tensor_add(S2s[sl][:], S2s[sl][:], at), inc="dve")
                s2_m[g_] = p.mark("dve")
                p.wait("tensor", "act", exp_m[g_])
                if qd == 0 and q >= 2:
                    p.wait("tensor", "dve", norm_m[q - 2])
                for ci in range(4):
                    c = 4 * qd + ci
                    p.op("tensor", lambda E, c=c, at=at, ci=ci, sl=sl,
                         s=(qd == 0 and ci == 0), st=(qd == nq - 1 and ci == 3): E.matmul(
                        avs[sl], vh[:, c * 128:(c + 1) * 128], at[:, ci * 512:(ci + 1) * 512],
                        start=s, stop=st), inc=("pe" if ci == 3 else None))
                av_m[g_] = p.mark("pe")
                gq[0] = g_ + 1
                if q >= 1 and qd == 0:
                    tail1(q - 1)
                if q >= 1 and qd == min(2, nq - 1):
                    tail2(q - 1)
            av_last[q] = av_m[gq[0] - 1]
        tail1(NQT - 1)
        tail2(NQT - 1)

    def o_proj(g2out, Wot, lora, scale_col, aocol, wmark):
        for hdc in range(NCK):
            p.dmad("gpsimd", lambda E, dyn, hd=hdc, ic=aocol + hdc: E.indirect_dma_start(
                h_sb[:, hd * TOK:(hd + 1) * TOK], None,
                g2out.rearrange("r (b x) -> (r b) x", b=4),
                IOA(gidx_sb[:, ic:ic + 1], 0)), "qG")
        p.barrier(("tensor", "scalar", "vector", "gpsimd"), ["qG"])
        if lora:
            load_AB(1536, 4096, 2048)
            emit_u()
        pe_done = {}
        cons = {}
        qc_m = {}
        p.wait("scalar", "qG", wmark)
        p.dma("scalar", wbuf[0][:, 0:2048], Wot[:, 0:2048], "qC")
        qc_m[0] = p.mark("qC")
        for och in range(NCK):
            wb = wbuf[och % 2]
            if och + 1 < NCK:
                if och >= 1:
                    p.wait("scalar", "pe", pe_done[och - 1])
                p.dma("scalar", wbuf[(och + 1) % 2][:, 0:2048],
                      Wot[:, (och + 1) * 2048:(och + 2) * 2048], "qC")
                qc_m[och + 1] = p.mark("qC")
            p.wait("tensor", "qC", qc_m[och])
            for th in range(2):
                u = och * 2 + th
                ps = psA if u % 2 == 0 else psB
                c0 = th * 1024
                if u >= 2:
                    p.wait("tensor", *cons[u - 2])
                for dc in range(NCK):
                    for g in range(2):
                        p.op("tensor", lambda E, d=dc, g=g, ps=ps, c0=c0, wb=wb: E.matmul(
                            ps[:, g * 512:(g + 1) * 512], wb[:, d * 128:(d + 1) * 128],
                            h_sb[:, d * TOK + c0 + g * 512:d * TOK + c0 + g * 512 + 512],
                            start=(d == 0), stop=(d == NCK - 1)))
                if lora:
                    for g in range(2):
                        p.op("tensor", lambda E, g=g, ps=ps, c0=c0, och=och: E.matmul(
                            ps[:, 1024 + g * 512:1024 + (g + 1) * 512],
                            BV[:, och * 128:(och + 1) * 128],
                            uV[:, c0 + g * 512:c0 + g * 512 + 512],
                            start=True, stop=True), inc=("pe" if g == 1 else None))
                else:
                    p.q["tensor"][-1] = (p.q["tensor"][-1][0], "pe", 1, None)
                    p.cnt["pe"] += 1
                if th == 1:
                    pe_done[och] = p.mark("pe")
                p.wait("scalar", "pe")
                st_ = stg[u % 2]
                if u >= 2:
                    p.wait("scalar", *cons[u - 2])
                if scale_col is not None:
                    p.op("scalar", lambda E, ps=ps, st=st_, sc=scale_col: E.activation(
                        st[:], ps[:, 0:1024], AF.Copy, scale=gb_sb[:, sc:sc + 1]), inc="act")
                else:
                    p.op("scalar", lambda E, ps=ps, st=st_: E.copy(st[:], ps[:, 0:1024]), inc="act")
                p.wait("vector", "act")
                if lora:
                    p.op("vector", lambda E, ps=ps, st=st_: E.tensor_add(st[:], st[:], ps[:, 1024:2048]))
                p.op("vector", lambda E, st=st_, och=och, c0=c0: E.tensor_add(
                    x_sb[:, och * TOK + c0:och * TOK + c0 + 1024],
                    x_sb[:, och * TOK + c0:och * TOK + c0 + 1024], st[:]), inc="dve")
                cons[u] = ("dve", p.mark("dve"))
        p.barrier()

    def attn_sublayer(is_b):
        emit_ln()
        if DBG and not is_b and not hasattr(attn_sublayer, "_h_dumped"):
            attn_sublayer._h_dumped = True
            p.barrier()
            p.dma("sync", dbg_h, h_sb[:], "qS")
            p.barrier()
        if not is_b:
            p.dma("sync", mask_sb[:], maskAi, "qS")
            fm_proj(WqaB, 16, g1a_in, 0, need_cast("Wqa"))
            fm_proj(WkaB, 16, g1a_in, 2048, need_cast("Wka"))
            fm_proj(WvaB, 16, g1a_in, 4096, need_cast("Wva"))
            collective(g1a_in, g1a_out, "cc0")
            for j in range(4):
                for s in range(4):
                    gather_rows(s * 2048, g1a_out, 12 * j + s)
                    gather_rows(8192 + s * 2048, g1a_out, 12 * j + 4 + s)
                    gather_rows(16384 + s * 2048, g1a_out, 12 * j + 8 + s)
                p.barrier(("tensor", "scalar", "vector", "gpsimd"), ["qG"])
                drain_casts(3)
                v_transpose()
                head_attn(False, j, g2a_in)
                p.barrier()
            collective(g2a_in, g2a_out, "cc1")
            o_proj(g2a_out, WoaB, False, None, 80, need_cast("Woa"))
        else:
            p.dma("sync", mask_sb[:], maskBi, "qS")
            p.dma("sync", csV, csO, "qS")
            load_AB(0, 0, 2048)
            emit_u()
            fm_proj(WqbB, 16, g1b_in, 0, need_cast("Wqb"), lora=True, scale_col=4, rope=True)
            load_AB(512, 2048, 1024)
            emit_u()
            fm_proj(WkbB, 8, g1b_in, 2048, need_cast("Wkb"), lora=True, scale_col=5, rope=True)
            load_AB(1024, 3072, 1024)
            emit_u()
            fm_proj(WvbB, 8, g1b_in, 3072, need_cast("Wvb"), lora=True, scale_col=6)
            collective(g1b_in, g1b_out, "cc2")
            for jj in range(2):
                for s in range(4):
                    gather_rows(8192 + s * 2048, g1b_out, 64 + 4 * jj + s)
                    gather_rows(16384 + s * 2048, g1b_out, 72 + 4 * jj + s)
                p.barrier(("tensor", "scalar", "vector", "gpsimd"), ["qG"])
                drain_casts(2)
                v_transpose()
                head_norm(8192, None)
                for j2 in range(2):
                    j = jj * 2 + j2
                    for s in range(4):
                        gather_rows(s * 2048, g1b_out, 48 + 4 * j + s)
                    p.barrier(("tensor", "scalar", "vector", "gpsimd"), ["qG"])
                    head_norm(0, j)
                    head_attn(True, j, g2b_in)
                    p.barrier()
            collective(g2b_in, g2b_out, "cc3")
            o_proj(g2b_out, WobB, True, 7, 96, need_cast("Wob"))

    def emit_router():
        for dc in range(NCK):
            for g in range(4):
                p.op("tensor", lambda E, d=dc, g=g: E.matmul(
                    psB[0:8, g * 512:(g + 1) * 512], rW_sb[:, d * 8:(d + 1) * 8],
                    h_sb[:, d * TOK + g * 512:d * TOK + g * 512 + 512],
                    start=(d == 0), stop=(d == NCK - 1)),
                    inc=("pe" if (dc == NCK - 1 and g == 3) else None))
        p.wait("scalar", "pe")
        p.op("scalar", lambda E: E.activation(sfmV[:, 0:2048], psB[0:8, 0:2048], AF.Sigmoid), inc="act")
        for tc in range(NCK):
            p.wait("tensor", "act")
            if tc >= 1:
                p.wait("tensor", "dve")
            p.op("tensor", lambda E, tc=tc: E.transpose(
                psA[:, 1536:1544], sfmV[0:8, tc * 128:(tc + 1) * 128], ident_sb[0:8, 0:8]), inc="pe")
            p.wait("vector", "pe")
            m1 = tp_sb[:, 0:1]; m2 = tp_sb[:, 1:2]; r_ = tp_sb[:, 3:4]
            chain = [
                lambda E: E.tensor_copy(stm_sb[:], psA[:, 1536:1544]),
                lambda E: E.reduce_max(m1, stm_sb[:], axis=mybir.AxisListType.X),
                lambda E: E.tensor_scalar(msk1_sb[:], stm_sb[:], m1, None, ALU.subtract),
                lambda E: E.tensor_single_scalar(msk1_sb[:], msk1_sb[:], 0.0, ALU.is_equal),
                lambda E: E.scalar_tensor_tensor(msk1_sb[:], msk1_sb[:], NEG, stm_sb[:], ALU.mult, ALU.add),
                lambda E: E.reduce_max(m2, msk1_sb[:], axis=mybir.AxisListType.X),
                lambda E: E.tensor_add(tp_sb[:, 2:3], m1, m2),
                lambda E: E.reciprocal(r_, tp_sb[:, 2:3]),
                lambda E: E.tensor_single_scalar(msk1_sb[:], stm_sb[:], m2, ALU.is_ge),
                lambda E: E.scalar_tensor_tensor(
                    ctm_holder[0][:], stm_sb[:], r_, msk1_sb[:], ALU.mult, ALU.mult),
            ]
            for fn_ in chain:
                p.op("vector", fn_, inc="dve")
                p.wait("vector", "dve")
            if DBG and tc == 0 and not getattr(emit_router, "_td", False):
                emit_router._td = True
                p.barrier()
                p.dma("gpsimd", dbg_t2[:, 0:8], stm_sb[:], "qG", allow_slow_non_contiguous=True)
                p.dma("gpsimd", dbg_t2[:, 8:16], msk1_sb[:], "qG", allow_slow_non_contiguous=True)
                p.dma("gpsimd", dbg_t2[:, 16:32], tp_sb[:], "qG", allow_slow_non_contiguous=True)
                p.dma("gpsimd", dbg_t2[:, 32:40], ctm_holder[0][:], "qG", allow_slow_non_contiguous=True)
                p.barrier()
            p.wait("tensor", "dve")
            p.op("tensor", lambda E: E.transpose(psA[0:8, 1536:1664], ctm_holder[0][:], ident_sb[:]), inc="pe")
            p.wait("vector", "pe")
            p.op("vector", lambda E, tc=tc: E.tensor_copy(c_fm[:, tc * 128:(tc + 1) * 128], psA[0:8, 1536:1664]), inc="dve")
        p.barrier(("tensor", "scalar", "gpsimd"), ["dve"])
        if DBG and not getattr(emit_router, "_rd", False):
            emit_router._rd = True
            p.barrier()
            p.dma("gpsimd", dbg_rt[0:8, 0:2048], sfmV[:, 0:2048], "qG")
            p.dma("gpsimd", dbg_rt[0:8, 2048:4096], c_fm[:], "qG")
            p.barrier()

    def emit_ffn(final):
        emit_ln()
        emit_router()
        quarters = [(0, 3), (3, 6), (6, 9), (9, 11)]
        def emit_cb(e, buf):
            p.wait("tensor", "dve")
            for g in range(4):
                p.op("tensor", lambda E, g=g, e=e: E.matmul(
                    psA[:, g * 512:(g + 1) * 512], sel_sb[:, e * 128:(e + 1) * 128],
                    c_fm[:, g * 512:(g + 1) * 512], start=True, stop=True),
                    inc=("pe" if g == 3 else None))
            p.wait("scalar", "pe")
            p.op("scalar", lambda E, buf=buf: E.copy(cb2[buf][:], psA[:, 0:2048]), inc="act")
            p.wait("vector", "act")
            p.wait("tensor", "act")

        for Ei in range(10):
            drain_casts(2)
            routed = Ei < 8
            cb_sb = cb2[Ei % 2]
            if routed:
                emit_cb(Ei, Ei % 2)
            pe_fc = {}
            qc_fc = {}
            p.wait("scalar", "qG", need_cast(f"w13_{Ei}"))
            p.wait("scalar", "qG", need_cast(f"w2_{Ei}"))
            p.dma("scalar", wbuf[0][:], w13B[Ei, 0], "qC")
            qc_fc[0] = p.mark("qC")
            for (f0, f1) in quarters:
                nf = f1 - f0
                p.wait("scalar", "pe")
                p.dma("scalar", w2V[:, 0:nf * 2048], w2sB[Ei][:, f0 * 2048:f1 * 2048], "qC")
                w2_m = p.mark("qC")
                he_marks = []
                for fi in range(nf):
                    fc = f0 + fi
                    wb = wbuf[fc % 2]
                    if fc + 1 < NFC:
                        if fc - 1 in pe_fc:
                            p.wait("scalar", "pe", pe_fc[fc - 1])
                        p.dma("scalar", wbuf[(fc + 1) % 2][:], w13B[Ei, fc + 1], "qC")
                        qc_fc[fc + 1] = p.mark("qC")
                    p.wait("tensor", "qC", qc_fc[fc])
                    for th in range(2):
                        u = fi * 2 + th
                        ps = psA if u % 2 == 0 else psB
                        c0 = th * 1024
                        if u >= 2:
                            p.wait("tensor", "dve", he_marks[u - 2])
                        for dc in range(NCK):
                            for g in range(2):
                                p.op("tensor", lambda E, d=dc, g=g, ps=ps, c0=c0, wb=wb: E.matmul(
                                    ps[:, g * 512:(g + 1) * 512], wb[:, d * 128:(d + 1) * 128],
                                    h_sb[:, d * TOK + c0 + g * 512:d * TOK + c0 + g * 512 + 512],
                                    start=(d == 0), stop=(d == NCK - 1)))
                        for dc in range(NCK):
                            for g in range(2):
                                p.op("tensor", lambda E, d=dc, g=g, ps=ps, c0=c0, wb=wb: E.matmul(
                                    ps[:, 1024 + g * 512:1024 + (g + 1) * 512],
                                    wb[:, 2048 + d * 128:2048 + (d + 1) * 128],
                                    h_sb[:, d * TOK + c0 + g * 512:d * TOK + c0 + g * 512 + 512],
                                    start=(d == 0), stop=(d == NCK - 1)),
                                    inc=("pe" if (dc == NCK - 1 and g == 1) else None))
                        if th == 1:
                            pe_fc[fc] = p.mark("pe")
                        p.wait("scalar", "pe")
                        p.wait("scalar", "dve")
                        p.op("scalar", lambda E, ps=ps: E.activation(
                            silu_sb[:], ps[:, 0:1024], AF.Silu), inc="act")
                        p.wait("vector", "act")
                        he_sl = heV[:, fi * 2048 + c0:fi * 2048 + c0 + 1024]
                        p.op("vector", lambda E, ps=ps, he=he_sl: E.tensor_mul(he, silu_sb[:], ps[:, 1024:2048]))
                        if routed:
                            p.op("vector", lambda E, he=he_sl, c0=c0, cb=cb_sb: E.tensor_mul(
                                he, he, cb[:, c0:c0 + 1024]), inc="dve")
                        else:
                            p.q["vector"][-1] = (p.q["vector"][-1][0], "dve", 1, None)
                            p.cnt["dve"] += 1
                        he_marks.append(p.mark("dve"))
                if DBG and Ei == 0 and f0 == 0 and not getattr(emit_ffn, "_fd", False):
                    emit_ffn._fd = True
                    p.barrier()
                    p.dma("gpsimd", dbg_ffn[:, 0:6144], heV, "qG")
                    p.dma("gpsimd", dbg_ffn[:, 6144:8192], cb_sb[:], "qG")
                    p.barrier()
                p.wait("tensor", "qC", w2_m)
                p.wait("tensor", "dve")
                for th in range(2):
                    c0 = th * 1024
                    for op_ in range(8):
                        u = th * 8 + op_
                        ps = psA if u % 2 == 0 else psB
                        if u >= 2:
                            p.wait("tensor", "dve")
                        for fi in range(nf):
                            for oc in range(2):
                                och = op_ * 2 + oc
                                for g in range(2):
                                    p.op("tensor", lambda E, fi=fi, oc=oc, och=och, g=g, ps=ps, c0=c0, s=(fi == 0), st=(fi == nf - 1): E.matmul(
                                        ps[:, oc * 1024 + g * 512:oc * 1024 + g * 512 + 512],
                                        w2V[:, fi * 2048 + och * 128:fi * 2048 + (och + 1) * 128],
                                        heV[:, fi * 2048 + c0 + g * 512:fi * 2048 + c0 + g * 512 + 512],
                                        start=s, stop=st),
                                        inc=("pe" if (fi == nf - 1 and oc == 1 and g == 1) else None))
                        p.wait("vector", "pe")
                        for oc in range(2):
                            och = op_ * 2 + oc
                            if final:
                                p.op("vector", lambda E, ps=ps, oc=oc, och=och, c0=c0: E.tensor_add(
                                    x_sb[:, och * TOK + c0:och * TOK + c0 + 1024],
                                    x_sb[:, och * TOK + c0:och * TOK + c0 + 1024],
                                    ps[:, oc * 1024:(oc + 1) * 1024]),
                                    inc=("dve" if oc == 1 else None))
                            else:
                                p.op("vector", lambda E, ps=ps, oc=oc, och=och, c0=c0: E.tensor_add(
                                    x_sb[:, och * TOK + c0:och * TOK + c0 + 1024],
                                    x_sb[:, och * TOK + c0:och * TOK + c0 + 1024],
                                    ps[:, oc * 1024:(oc + 1) * 1024]),
                                    inc=("dve" if oc == 1 else None))
        p.barrier()

    ctm_holder = [ctx.enter_context(nc.sbuf_tensor("ctm_sb", [128, 8], F32))]


    def dump(dst, src):
        if DBG:
            p.barrier()
            p.dma("gpsimd", dst, src, "qG")
            p.barrier()

    def dump_sb(dst, src):
        if DBG:
            p.barrier()
            p.dma("sync", dst, src[:], "qS")
            p.barrier()

    if DBG:
        emit_ln._dump_h = dbg_h

    attn_sublayer(False)
    if DBG:
        dump(dbg_qkv, g1a_in)
        dump(dbg_att, g2a_in)
        dump_sb(dbg_x1, x_sb)
    emit_ffn(False)
    if DBG:
        dump_sb(dbg_x2, x_sb)
    attn_sublayer(True)
    if DBG:
        dump(dbg_qkvb, g1b_in)
        dump(dbg_attb, g2b_in)
        dump_sb(dbg_x3, x_sb)
    emit_ffn(True)

    for och in range(NCK):
        for th in range(2):
            u = och * 2 + th
            if u >= 1:
                p.wait("scalar", "qS", out_m[u - 1] if u >= 1 else 0)
            p.op("scalar", lambda E, och=och, th=th: E.copy(
                outV[:, 0:1024], x_sb[:, och * TOK + th * 1024:och * TOK + th * 1024 + 1024]), inc="act")
            p.wait("sync", "act")
            if u == 0:
                out_m = {}
            p.dma("sync", out_d[:, och * TOK + th * 1024:och * TOK + th * 1024 + 1024], outV[:, 0:1024], "qS")
            out_m[u] = p.mark("qS")
    p.wait("sync", "qS")
    p.wait("sync", "qG")
    p.wait("sync", "qC")

    with nc.Block() as block:
        def make_runner(name):
            def runner(engine):
                dyn = {}
                for fn, inc, n, dfn in p.q[name]:
                    if dfn is not None:
                        ins = dfn(engine, dyn)
                        ins.then_inc(sems[inc], n)
                    elif isinstance(fn, tuple) and fn[0] == "W":
                        engine.wait_ge(sems[fn[1]], fn[2])
                    else:
                        ins = fn(engine)
                        if inc is not None:
                            ins.then_inc(sems[inc], n)
            return runner

        r_sync = make_runner("sync"); r_scalar = make_runner("scalar")
        r_vector = make_runner("vector"); r_tensor = make_runner("tensor")
        r_gpsimd = make_runner("gpsimd")

        @block.sync
        def _(sync):
            r_sync(sync)

        @block.scalar
        def _(scalar):
            r_scalar(scalar)

        @block.vector
        def _(vector):
            r_vector(vector)

        @block.tensor
        def _(tensor):
            r_tensor(tensor)

        @block.gpsimd
        def _(gpsimd):
            r_gpsimd(gpsimd)

    ctx.close()
    return nc



def tile_stationary(W):
    d, o = W.shape
    ndc, noc = d // 128, o // 128
    return np.ascontiguousarray(
        W.reshape(ndc, 128, noc, 128).transpose(1, 2, 0, 3).reshape(128, noc * ndc * 128))


def tile_moving(W):
    d, o = W.shape
    ndc = d // 128
    return np.ascontiguousarray(W.reshape(ndc, 128, o).transpose(1, 0, 2).reshape(128, ndc * o))


def tile_a(A):
    At = np.ascontiguousarray(A.T)
    ndc = At.shape[0] // 128
    return np.ascontiguousarray(At.reshape(ndc, 128, 32).transpose(1, 0, 2).reshape(128, ndc * 32))


def tile_w13(w1, w3):
    nfc = w1.shape[1] // 128
    out = np.empty((nfc, 128, 4096), np.float32)
    for which, w in ((0, w1), (1, w3)):
        t = w.reshape(16, 128, nfc, 128).transpose(2, 1, 0, 3).reshape(nfc, 128, 2048)
        out[:, :, which * 2048:(which + 1) * 2048] = t
    return out


def tile_w2(w2):
    nfc = w2.shape[0] // 128
    return np.ascontiguousarray(
        w2.reshape(nfc, 128, 16, 128).transpose(1, 0, 2, 3).reshape(128, nfc * 2048))


_CACHE = {}


def kernel(**inputs):
    ip = {k: np.asarray(v, dtype=np.float32) for k, v in inputs.items()}
    x = ip["L0_attn_a_x"]

    shared = {}
    shared["Wqa"] = tile_stationary(ip["L0_attn_a_Wq"])
    shared["Wka"] = tile_stationary(ip["L0_attn_a_Wk"])
    shared["Wva"] = tile_stationary(ip["L0_attn_a_Wv"])
    shared["Woa"] = tile_stationary(ip["L0_attn_a_Wo"])
    shared["Wqb"] = tile_stationary(np.ascontiguousarray(ip["L0_attn_b_W_q"].T))
    shared["Wkb"] = tile_stationary(np.ascontiguousarray(ip["L0_attn_b_W_k"].T))
    shared["Wvb"] = tile_stationary(np.ascontiguousarray(ip["L0_attn_b_W_v"].T))
    shared["Wob"] = tile_stationary(np.ascontiguousarray(ip["L0_attn_b_W_o"].T))
    shared["Aall"] = np.concatenate([tile_a(ip["L0_attn_b_A_q"]), tile_a(ip["L0_attn_b_A_k"]),
                                     tile_a(ip["L0_attn_b_A_v"]), tile_a(ip["L0_attn_b_A_o"])], axis=1)
    shared["Ball"] = np.concatenate([np.ascontiguousarray(ip["L0_attn_b_B_q"].T),
                                     np.ascontiguousarray(ip["L0_attn_b_B_k"].T),
                                     np.ascontiguousarray(ip["L0_attn_b_B_v"].T),
                                     np.ascontiguousarray(ip["L0_attn_b_B_o"].T)], axis=1)
    rt = ip["L0_ffn_w_router"]
    shared["rW"] = np.ascontiguousarray(
        rt.reshape(16, 128, 8).transpose(1, 0, 2).reshape(128, 128))
    w13l = [tile_w13(ip["L0_ffn_w1"][e], ip["L0_ffn_w3"][e]) for e in range(8)]
    ws1, ws3, ws2 = ip["L0_ffn_ws1"], ip["L0_ffn_ws3"], ip["L0_ffn_ws2"]
    w13l.append(tile_w13(ws1[:, 0:1408], ws3[:, 0:1408]))
    w13l.append(tile_w13(ws1[:, 1408:2816], ws3[:, 1408:2816]))
    shared["w13"] = np.stack(w13l)
    w2l = [tile_w2(ip["L0_ffn_w2"][e]) for e in range(8)]
    w2l.append(tile_w2(ws2[0:1408]))
    w2l.append(tile_w2(ws2[1408:2816]))
    shared["w2s"] = np.stack(w2l)

    tqi = np.arange(QT)[None, :]
    tsi = np.arange(128)[:, None]
    mA = np.zeros((128, 4 * QT), ml_dtypes.bfloat16)
    mB = np.zeros((128, 4 * QT), ml_dtypes.bfloat16)
    for d_ in range(4):
        keep = (tqi - d_ * 128) >= tsi
        mA[:, d_ * QT:(d_ + 1) * QT] = keep.astype(ml_dtypes.bfloat16)
        mB[:, d_ * QT:(d_ + 1) * QT] = np.where(keep, 0.0, NEG).astype(ml_dtypes.bfloat16)
    shared["maskAi"] = mA
    shared["maskBi"] = mB
    shared["identF"] = np.eye(128, dtype=np.float32)
    shared["onesC"] = np.ones((128, 1), np.float32)
    shared["onesCb"] = np.ones((128, 1), ml_dtypes.bfloat16)
    shared["onesR"] = np.ones((1, 128), np.float32)
    g_ = np.arange(16)
    shared["UTc"] = (g_[:, None] > g_[None, :]).astype(np.float32)
    sel = np.zeros((8, 8 * 128), ml_dtypes.bfloat16)
    for e in range(8):
        sel[e, e * 128:(e + 1) * 128] = 1
    shared["sel8"] = sel
    shared["svec"] = np.array([[ip["L0_attn_b_s_q"][0], ip["L0_attn_b_s_k"][0],
                                ip["L0_attn_b_s_v"][0], ip["L0_attn_b_s_o"][0]]], np.float32)
    shared["epsv"] = np.array([[LN_EPS, RMS_EPS]], np.float32)

    inv = 1.0 / (10000.0 ** (np.arange(0, 64, 2, dtype=np.float32) / 64.0))
    freqs = np.outer(np.arange(T, dtype=np.float32), inv)
    cosf = np.cos(freqs).T
    sinf = np.sin(freqs).T

    in_maps = []
    for c in range(8):
        b, r = c // 4, c % 4
        xs = np.ascontiguousarray(x[b, r * TOK:(r + 1) * TOK, :].T)
        m = dict(shared)
        m["x_in"] = np.ascontiguousarray(
            xs.reshape(16, 128, TOK).transpose(1, 0, 2).reshape(128, 16 * TOK))
        cs = np.empty((64, 2 * TOK), ml_dtypes.bfloat16)
        cs[0:32, 0:TOK] = cosf[:, r * TOK:(r + 1) * TOK].astype(ml_dtypes.bfloat16)
        cs[32:64, 0:TOK] = sinf[:, r * TOK:(r + 1) * TOK].astype(ml_dtypes.bfloat16)
        cs[0:32, TOK:] = sinf[:, r * TOK:(r + 1) * TOK].astype(ml_dtypes.bfloat16)
        cs[32:64, TOK:] = cosf[:, r * TOK:(r + 1) * TOK].astype(ml_dtypes.bfloat16)
        m["csO"] = cs
        m["gain4"] = np.ascontiguousarray(ip["L0_attn_b_q_gain"][4 * r:4 * r + 4][None, :])
        pp = np.arange(128, dtype=np.uint32)
        gi = np.zeros((128, 112), np.uint32)
        for j in range(4):
            for s_ in range(4):
                gi[:, 12 * j + s_] = 512 * r + s_ * 6144 + j * 128 + pp
                gi[:, 12 * j + 4 + s_] = 512 * r + s_ * 6144 + 2048 + j * 128 + pp
                gi[:, 12 * j + 8 + s_] = 512 * r + s_ * 6144 + 4096 + j * 128 + pp
        for j in range(4):
            for s_ in range(4):
                gi[:, 48 + 4 * j + s_] = 512 * r + s_ * 4096 + j * 128 + pp
        for jj in range(2):
            for s_ in range(4):
                gi[:, 64 + 4 * jj + s_] = 256 * r + s_ * 4096 + 2048 + jj * 128 + pp
                gi[:, 72 + 4 * jj + s_] = 256 * r + s_ * 4096 + 3072 + jj * 128 + pp
        for hdc in range(16):
            gi[:, 80 + hdc] = (hdc * 128 + pp) * 4 + r
            gi[:, 96 + hdc] = (hdc * 128 + pp) * 4 + r
        m["gidx"] = gi
        in_maps.append(m)

    if "nc" not in _CACHE:
        _CACHE["nc"] = build_program()
    nc = _CACHE["nc"]
    res = run_bass_kernel_spmd(nc, in_maps, core_ids=list(range(8)))

    out = np.empty((2, T, D), np.float32)
    for c in range(8):
        b, r = c // 4, c % 4
        o = res.results[c]["out_d"]
        out[b, r * TOK:(r + 1) * TOK, :] = o.reshape(128, 16, TOK).transpose(1, 0, 2).reshape(D, TOK).T
    return out


# version 11
# speedup vs baseline: 1.0925x; 1.0153x over previous
import numpy as np
import ml_dtypes
from collections import defaultdict
from contextlib import ExitStack

import concourse.bass as bass
import concourse.mybir as mybir
from concourse.expressions import RuntimeValue
from concourse.bass_utils import run_bass_kernel_spmd

F32 = mybir.dt.float32
BF16 = mybir.dt.bfloat16
AF = mybir.ActivationFunctionType
ALU = mybir.AluOpType
ds = bass.ds

D = 2048
T = 8192
TOK = 2048
NCK = 16
QT = 512
NQT = T // QT
NFC = 11
SCALE_A = float(1.0 / np.float64(np.sqrt(np.float32(T))))
SCALE_B = float(1.0 / np.float64(np.sqrt(np.float32(128))))
BOUND_B = float(np.sqrt(np.float64(128.0)))
LN_EPS = 1e-5
RMS_EPS = 1.1920929e-07
NEG = -1.0e9

ENGINES = ("sync", "scalar", "vector", "tensor", "gpsimd")
DBG = False


class Rec:

    def __init__(self):
        self.q = {e: [] for e in ENGINES}
        self.cnt = defaultdict(int)
        self.dynm = {}
        self.dyn = None

    def op(self, eng, fn, inc=None, n=1):
        self.q[eng].append((fn, inc, n, None))
        if inc:
            self.cnt[inc] += n
        return self.cnt[inc] if inc else None

    def dma(self, eng, out, in_, inc="q" + "X", **kw):
        return self.op(eng, lambda E, o=out, i=in_, k=kw: E.dma_start(out=o, in_=i, **k), inc=inc, n=16)

    def dmad(self, eng, fn, inc):
        self.q[eng].append((None, inc, 16, fn))
        self.cnt[inc] += 16
        return self.cnt[inc]

    def wait(self, eng, sem, val=None):
        v = self.cnt[sem] if val is None else val
        assert v <= self.cnt[sem], (sem, v, self.cnt[sem])
        self.q[eng].append((("W", sem, v), None, 0, None))

    def mark(self, sem):
        return self.cnt[sem]

    def barrier(self, engines=ENGINES, sems=None):
        if sems is None:
            sems = [s for s in self.cnt if not s.startswith("cc")]
        for e in engines:
            for s in sems:
                if self.cnt[s] > 0:
                    self.wait(e, s, self.cnt[s])


def build_program():
    nc = bass.Bass("TRN2", target_bir_lowering=False, detect_race_conditions=False)
    dr = lambda name, shape, dt, kind="ExternalInput": nc.dram_tensor(name, shape, dt, kind=kind).ap()

    x_in = dr("x_in", [128, NCK * TOK], F32)
    Wqa = dr("Wqa", [128, 32768], F32); Wka = dr("Wka", [128, 32768], F32)
    Wva = dr("Wva", [128, 32768], F32); Woa = dr("Woa", [128, 32768], F32)
    Wqb = dr("Wqb", [128, 32768], F32); Wkb = dr("Wkb", [128, 16384], F32)
    Wvb = dr("Wvb", [128, 16384], F32); Wob = dr("Wob", [128, 32768], F32)
    Aall = dr("Aall", [128, 4 * 512], F32)
    Ball = dr("Ball", [32, 6144], F32)
    rW = dr("rW", [128, 128], F32)
    w13 = dr("w13", [10, NFC, 128, 4096], F32)
    w2s = dr("w2s", [10, 128, NFC * 2048], F32)
    csO = dr("csO", [64, 2 * TOK], BF16)
    maskAi = dr("maskAi", [128, 4 * QT], BF16)
    maskBi = dr("maskBi", [128, 4 * QT], BF16)
    identF = dr("identF", [128, 128], F32)
    onesC = dr("onesC", [128, 1], F32)
    onesCb = dr("onesCb", [128, 1], BF16)
    onesR = dr("onesR", [1, 128], F32)
    UTc = dr("UTc", [16, 16], F32)
    sel8 = dr("sel8", [8, 8 * 128], BF16)
    gain4 = dr("gain4", [1, 4], F32)
    gidx = dr("gidx", [128, 112], mybir.dt.uint32)
    svec = dr("svec", [1, 4], F32)
    epsv = dr("epsv", [1, 2], F32)
    out_d = dr("out_d", [128, NCK * TOK], F32, kind="ExternalOutput")
    if DBG:
        dbg_h = dr("dbg_h", [128, NCK * TOK], BF16, kind="ExternalOutput")
        dbg_qkv = dr("dbg_qkv", [6144, 2048], BF16, kind="ExternalOutput")
        dbg_att = dr("dbg_att", [512, T], BF16, kind="ExternalOutput")
        dbg_x1 = dr("dbg_x1", [128, NCK * TOK], BF16, kind="ExternalOutput")
        dbg_x2 = dr("dbg_x2", [128, NCK * TOK], BF16, kind="ExternalOutput")
        dbg_x3 = dr("dbg_x3", [128, NCK * TOK], BF16, kind="ExternalOutput")
        dbg_qkvb = dr("dbg_qkvb", [4096, 2048], BF16, kind="ExternalOutput")
        dbg_attb = dr("dbg_attb", [512, T], BF16, kind="ExternalOutput")
        dbg_fine = dr("dbg_fine", [128, 6144], F32, kind="ExternalOutput")
        dbg_ffn = dr("dbg_ffn", [128, 8192], F32, kind="ExternalOutput")
        dbg_rt = dr("dbg_rt", [8, 4096], F32, kind="ExternalOutput")
        dbg_t2 = dr("dbg_t2", [128, 64], F32, kind="ExternalOutput")

    WqaB = dr("WqaB", [128, 32768], BF16, kind="Internal")
    WkaB = dr("WkaB", [128, 32768], BF16, kind="Internal")
    WvaB = dr("WvaB", [128, 32768], BF16, kind="Internal")
    WoaB = dr("WoaB", [128, 32768], BF16, kind="Internal")
    WqbB = dr("WqbB", [128, 32768], BF16, kind="Internal")
    WkbB = dr("WkbB", [128, 16384], BF16, kind="Internal")
    WvbB = dr("WvbB", [128, 16384], BF16, kind="Internal")
    WobB = dr("WobB", [128, 32768], BF16, kind="Internal")
    w13B = dr("w13B", [10, NFC, 128, 4096], BF16, kind="Internal")
    w2sB = dr("w2sB", [10, 128, NFC * 2048], BF16, kind="Internal")
    g1a_in = dr("g1a_in", [6144, 2048], BF16, kind="Internal")
    g1a_out = dr("g1a_out", [24576, 2048], BF16, kind="Internal")
    g1b_in = dr("g1b_in", [4096, 2048], BF16, kind="Internal")
    g1b_out = dr("g1b_out", [16384, 2048], BF16, kind="Internal")
    g2a_in = dr("g2a_in", [512, T], BF16, kind="Internal")
    g2a_out = dr("g2a_out", [2048, T], BF16, kind="Internal")
    g2b_in = dr("g2b_in", [512, T], BF16, kind="Internal")
    g2b_out = dr("g2b_out", [2048, T], BF16, kind="Internal")

    ctx = ExitStack()
    sb = lambda name, shape, dt: ctx.enter_context(nc.sbuf_tensor(name, shape, dt))
    x_sb = sb("x_sb", [128, NCK * TOK], BF16)
    h_sb = sb("h_sb", [128, NCK * TOK], BF16)
    wbuf = [sb(f"wbuf{i}", [128, 4096], BF16) for i in range(2)]
    stg = [sb(f"stg{i}", [128, 1024], BF16) for i in range(2)]
    row_sb = sb("row_sb", [1, 1024], F32)
    sq_sb = sb("sq_sb", [128, 512], F32)
    silu_sb = sb("silu_sb", [128, 1024], BF16)
    cb2 = [sb(f"cb2_{i}", [128, 2048], BF16) for i in range(2)]
    c_fm = sb("c_fm", [8, 2048], BF16)

    rt_sb = sb("rt_sb", [64, 1024], BF16)
    mask_sb = sb("mask_sb", [128, 4 * QT], BF16)
    ident_sb = sb("ident_sb", [128, 128], F32)
    sel_sb = sb("sel_sb", [8, 8 * 128], BF16)
    onesC_sb = sb("onesC_sb", [128, 1], F32)
    onesCb_sb = sb("onesCb_sb", [128, 1], BF16)
    onesR_sb = sb("onesR_sb", [1, 128], F32)
    UT_sb = sb("UT_sb", [16, 16], F32)
    rW_sb = sb("rW_sb", [128, 128], BF16)
    gb_sb = sb("gb_sb", [128, 16], F32)
    gidx_sb = sb("gidx_sb", [128, 112], mybir.dt.uint32)
    tp_sb = sb("tp_sb", [128, 16], F32)
    stm_sb = sb("stm_sb", [128, 8], F32)
    msk1_sb = sb("msk1_sb", [128, 8], F32)
    cv_sb = sb("cv_sb", [128, 16], F32)
    cvT_sb = sb("cvT_sb", [16, 128], F32)
    sv_sb = sb("sv_sb", [128, 16], F32)
    AR = sb("AR", [128, 12288], BF16)

    psA = ctx.enter_context(nc.psum_tensor("psA", [128, 2048], F32))
    psB = ctx.enter_context(nc.psum_tensor("psB", [128, 2048], F32))

    sem_names = ["qS", "qC", "qG", "pe", "act", "dve", "po",
                 "cc0", "cc1", "cc2", "cc3"]
    sems = {n: ctx.enter_context(nc.semaphore(n)) for n in sem_names}
    p = Rec()

    heV = AR[:, 0:6144]
    w2V = AR[:, 6144:12288]
    csV = AR[0:64, 0:4096]
    uV = AR[0:32, 4096:6144]
    BV = AR[0:32, 6144:8192]
    AVt = AR[:, 8192:8704]
    attnT2 = [AR[:, 0:2048], AR[:, 2048:4096]]
    S2s = [AR[:, 4096:6144], AR[:, 6144:8192]]
    sfmV = AR[0:8, 4608:8704].bitcast(F32)
    outV = AR[:, 0:2048].bitcast(F32)

    p.dma("sync", ident_sb[:], identF, "qS")
    p.dma("sync", onesC_sb[:], onesC, "qS")
    p.dma("sync", onesCb_sb[:], onesCb, "qS")
    p.dma("sync", onesR_sb[:], onesR, "qS")
    p.dma("sync", UT_sb[:], UTc, "qS")
    p.dma("sync", sel_sb[:], sel8, "qS")
    p.dma("sync", row_sb[0:1, 0:4], gain4, "qS")
    p.dma("sync", row_sb[0:1, 4:8], svec, "qS")
    p.dma("sync", gb_sb[0:1, 12:14], epsv, "qS", allow_slow_non_contiguous=True)
    p.dma("gpsimd", rW_sb[:], rW, "qG")
    p.dma("gpsimd", x_sb[:], x_in, "qG")
    p.dma("sync", gidx_sb[:], gidx, "qS")
    p.barrier()
    p.op("tensor", lambda E: E.matmul(psA[:, 0:8], onesR_sb[:], row_sb[0:1, 0:8],
                                      start=True, stop=True), inc="pe")
    p.wait("vector", "pe")
    p.op("vector", lambda E: E.tensor_copy(gb_sb[:, 0:8], psA[:, 0:8]), inc="dve")
    p.wait("scalar", "dve")
    p.op("scalar", lambda E: E.activation(gb_sb[:, 8:12], gb_sb[:, 0:4], AF.Abs), inc="act")
    p.wait("vector", "act")
    p.op("vector", lambda E: E.tensor_scalar_mul(gb_sb[:, 8:12], gb_sb[:, 8:12], -BOUND_B), inc="dve")
    p.barrier()

    wcast = {}
    cast_pending = [("Wqa", WqaB, Wqa), ("Wka", WkaB, Wka), ("Wva", WvaB, Wva),
                    ("Woa", WoaB, Woa), ("w13_0", w13B[0], w13[0]), ("w2_0", w2sB[0], w2s[0])]
    for e in range(1, 10):
        cast_pending.append((f"w13_{e}", w13B[e], w13[e]))
        cast_pending.append((f"w2_{e}", w2sB[e], w2s[e]))
    for nm, dst, src in [("Wqb", WqbB, Wqb), ("Wkb", WkbB, Wkb), ("Wvb", WvbB, Wvb),
                         ("Wob", WobB, Wob)]:
        cast_pending.append((nm, dst, src))

    def drain_casts(n):
        for _ in range(n):
            if not cast_pending:
                return
            nm, dst, src = cast_pending.pop(0)
            p.dma("gpsimd", dst, src, "qG")
            wcast[nm] = p.mark("qG")

    drain_casts(6)

    def emit_ln():
        for tq4 in range(4):
            c0 = tq4 * 512
            for dc in range(NCK):
                p.wait("scalar", "pe")
                p.op("scalar", lambda E, d=dc, c0=c0: E.activation(
                    sq_sb[:], x_sb[:, d * TOK + c0:d * TOK + c0 + 512], AF.Square), inc="act")
                p.wait("tensor", "act")
                p.op("tensor", lambda E, d=dc, c0=c0: E.matmul(
                    psA[0:1, 0:512], onesCb_sb[:], x_sb[:, d * TOK + c0:d * TOK + c0 + 512],
                    start=(d == 0), stop=(d == NCK - 1)))
                p.op("tensor", lambda E, d=dc: E.matmul(
                    psA[0:1, 512:1024], onesC_sb[:], sq_sb[:, 0:512],
                    start=(d == 0), stop=(d == NCK - 1)), inc="pe")
            p.wait("vector", "pe")
            mu = row_sb[0:1, 0:512]
            wk = row_sb[0:1, 512:1024]
            p.op("vector", lambda E: E.tensor_scalar_mul(mu, psA[0:1, 0:512], 1.0 / D))
            p.op("vector", lambda E: E.tensor_mul(wk, mu, mu))
            p.op("vector", lambda E: E.scalar_tensor_tensor(
                wk, psA[0:1, 512:1024], 1.0 / D, wk, ALU.mult, ALU.subtract), inc="dve")
            p.wait("scalar", "dve")
            p.op("scalar", lambda E: E.activation(wk, wk, AF.Sqrt, bias=gb_sb[0:1, 12:13]), inc="act")
            p.wait("vector", "act")
            p.op("vector", lambda E: E.reciprocal(wk, wk))
            p.op("vector", lambda E: E.scalar_tensor_tensor(mu, mu, -1.0, wk, ALU.mult, ALU.mult), inc="dve")
            p.wait("tensor", "dve")
            p.op("tensor", lambda E: E.matmul(psB[:, 0:512], onesR_sb[:], wk, start=True, stop=True))
            p.op("tensor", lambda E: E.matmul(psB[:, 512:1024], onesR_sb[:], mu, start=True, stop=True), inc="pe")
            p.wait("vector", "pe")
            for dc in range(NCK):
                p.op("vector", lambda E, d=dc, c0=c0: E.tensor_mul(
                    h_sb[:, d * TOK + c0:d * TOK + c0 + 512],
                    x_sb[:, d * TOK + c0:d * TOK + c0 + 512], psB[:, 0:512]))
                p.op("vector", lambda E, d=dc, c0=c0: E.tensor_add(
                    h_sb[:, d * TOK + c0:d * TOK + c0 + 512],
                    h_sb[:, d * TOK + c0:d * TOK + c0 + 512], psB[:, 512:1024]),
                    inc=("dve" if dc == NCK - 1 else None))
            p.barrier(("tensor", "scalar"), ["dve"])
        p.barrier()

    def load_AB(acol, bcol, blen):
        p.dma("gpsimd", AVt, Aall[:, acol:acol + 512], "qG")
        p.dma("gpsimd", BV[:, 0:blen], Ball[:, bcol:bcol + blen], "qG")

    def emit_u():
        p.barrier(("tensor",), ["qG"])
        for dc in range(NCK):
            for g in range(4):
                p.op("tensor", lambda E, d=dc, g=g: E.matmul(
                    psA[0:32, g * 512:(g + 1) * 512], AVt[:, d * 32:(d + 1) * 32],
                    h_sb[:, d * TOK + g * 512:d * TOK + g * 512 + 512],
                    start=(d == 0), stop=(d == NCK - 1)),
                    inc=("pe" if (dc == NCK - 1 and g == 3) else None))
        p.wait("scalar", "pe")
        p.op("scalar", lambda E: E.copy(uV[:, 0:2048], psA[0:32, 0:2048]), inc="act")
        p.wait("tensor", "act")

    def need_cast(nm):
        while nm not in wcast and cast_pending:
            drain_casts(1)
        return wcast[nm]

    def fm_proj(Wt, noc, dst, dst_rows, wmark, lora=False, scale_col=None, rope=False):
        pe_done = {}
        cons = {}
        store_done = {}
        qc_m = {}
        p.wait("scalar", "qG", wmark)
        p.dma("scalar", wbuf[0][:, 0:2048], Wt[:, 0:2048], "qC")
        qc_m[0] = p.mark("qC")
        for och in range(noc):
            wb = wbuf[och % 2]
            if och + 1 < noc:
                if och >= 1:
                    p.wait("scalar", "pe", pe_done[och - 1])
                p.dma("scalar", wbuf[(och + 1) % 2][:, 0:2048],
                      Wt[:, (och + 1) * 2048:(och + 2) * 2048], "qC")
                qc_m[och + 1] = p.mark("qC")
            p.wait("tensor", "qC", qc_m[och])
            for th in range(2):
                u = och * 2 + th
                ps = psA if u % 2 == 0 else psB
                c0 = th * 1024
                if u >= 2:
                    p.wait("tensor", *cons[u - 2])
                for dc in range(NCK):
                    for g in range(2):
                        p.op("tensor", lambda E, d=dc, g=g, ps=ps, c0=c0, wb=wb: E.matmul(
                            ps[:, g * 512:(g + 1) * 512], wb[:, d * 128:(d + 1) * 128],
                            h_sb[:, d * TOK + c0 + g * 512:d * TOK + c0 + g * 512 + 512],
                            start=(d == 0), stop=(d == NCK - 1)))
                if lora:
                    for g in range(2):
                        p.op("tensor", lambda E, g=g, ps=ps, c0=c0, och=och: E.matmul(
                            ps[:, 1024 + g * 512:1024 + (g + 1) * 512],
                            BV[:, och * 128:(och + 1) * 128],
                            uV[:, c0 + g * 512:c0 + g * 512 + 512],
                            start=True, stop=True),
                            inc=("pe" if g == 1 else None))
                else:
                    p.q["tensor"][-1] = (p.q["tensor"][-1][0], "pe", 1, None)
                    p.cnt["pe"] += 1
                if th == 1:
                    pe_done[och] = p.mark("pe")
                p.wait("scalar", "pe")
                st_ = stg[u % 2]
                if u >= 2:
                    p.wait("scalar", "qS", store_done[u - 2])
                if scale_col is not None:
                    p.op("scalar", lambda E, ps=ps, st=st_, sc=scale_col: E.activation(
                        st[:], ps[:, 0:1024], AF.Copy, scale=gb_sb[:, sc:sc + 1]), inc="act")
                else:
                    p.op("scalar", lambda E, ps=ps, st=st_: E.copy(st[:], ps[:, 0:1024]), inc="act")
                if lora:
                    p.wait("vector", "act")
                    p.op("vector", lambda E, ps=ps, st=st_: E.tensor_add(st[:], st[:], ps[:, 1024:2048]), inc="dve")
                    cons[u] = ("dve", p.mark("dve"))
                else:
                    cons[u] = ("act", p.mark("act"))
                if rope:
                    p.wait("gpsimd", *cons[u])
                    t_b0 = rt_sb[0:32, 0:1024]
                    t_b32 = rt_sb[32:64, 0:1024]
                    co_b0 = csV[0:32, c0:c0 + 1024]
                    si_b32 = csV[32:64, c0:c0 + 1024]
                    si_b0 = csV[0:32, TOK + c0:TOK + c0 + 1024]
                    co_b32 = csV[32:64, TOK + c0:TOK + c0 + 1024]
                    q1 = st_[0:32, :]
                    q2 = st_[32:64, :]
                    p.op("gpsimd", lambda E, q1=q1, si=si_b0, t=t_b32: E.tensor_mul(t, q1, si))
                    p.op("gpsimd", lambda E, q1=q1, co=co_b0: E.tensor_mul(q1, q1, co))
                    p.op("gpsimd", lambda E, q2=q2, si=si_b32, t=t_b0: E.tensor_mul(t, q2, si))
                    p.op("gpsimd", lambda E, q1=q1, t=t_b0: E.tensor_add(q1, q1, t))
                    p.op("gpsimd", lambda E, q2=q2, co=co_b32: E.tensor_mul(q2, q2, co))
                    p.op("gpsimd", lambda E, q2=q2, t=t_b32: E.tensor_sub(q2, q2, t), inc="po")
                    cons[u] = ("po", p.mark("po"))
                p.wait("sync", *cons[u])
                p.dma("sync", dst[dst_rows + och * 128:dst_rows + (och + 1) * 128, c0:c0 + 1024], st_[:], "qS")
                store_done[u] = p.mark("qS")
        p.barrier(("gpsimd", "vector", "scalar"), ["qS"])

    IOA = bass.IndirectOffsetOnAxis

    def gather_rows(dst_cols, src, idxcol):
        p.dmad("gpsimd", lambda E, dyn, dc=dst_cols, src=src, ic=idxcol: E.indirect_dma_start(
            h_sb[:, dc:dc + 2048], None, src, IOA(gidx_sb[:, ic:ic + 1], 0)), "qG")

    def v_transpose(n_tiles=64):
        marks = {}
        for t in range(n_tiles):
            stf = sq_sb[:, (t % 2) * 128:(t % 2) * 128 + 128]
            pslot = psA[:, 1024 + (t % 2) * 512:1024 + (t % 2) * 512 + 128]
            if t >= 2:
                p.wait("scalar", "pe", marks[t - 2][0])
            p.op("scalar", lambda E, t=t, stf=stf: E.copy(
                stf, h_sb[:, 16384 + t * 128:16384 + (t + 1) * 128]), inc="act")
            p.wait("tensor", "act")
            if t >= 2:
                p.wait("tensor", "dve", marks[t - 2][1])
            p.op("tensor", lambda E, stf=stf, ps=pslot: E.transpose(ps, stf, ident_sb[:]), inc="pe")
            p.wait("vector", "pe")
            p.op("vector", lambda E, t=t, ps=pslot: E.tensor_copy(
                h_sb[:, 24576 + t * 128:24576 + (t + 1) * 128], ps), inc="dve")
            marks[t] = (p.mark("pe"), p.mark("dve"))
        p.barrier(("tensor", "scalar", "gpsimd"), ["dve"])

    def collective(cin, cout, ccname):
        p.barrier(("gpsimd",), ["qS", "qG", "dve", "act", "po", "pe"])
        p.op("gpsimd", lambda E, i=cin, o=cout: E.collective_compute(
            "AllGather", ALU.bypass, replica_groups=[[0, 1, 2, 3], [4, 5, 6, 7]],
            ins=[i.opt()], outs=[o.opt()]), inc=ccname, n=1)
        for e in ENGINES:
            p.wait(e, ccname, 1)

    def head_norm(colbase, gaincol):
        for blk in range(16):
            c0 = colbase + blk * 512
            p.wait("scalar", "pe")
            p.op("scalar", lambda E, c0=c0: E.activation(
                sq_sb[:], h_sb[:, c0:c0 + 512], AF.Square), inc="act")
            p.wait("tensor", "act")
            p.op("tensor", lambda E: E.matmul(
                psA[0:1, 0:512], onesC_sb[:], sq_sb[:], start=True, stop=True), inc="pe")
            p.wait("scalar", "pe")
            p.op("scalar", lambda E: E.activation(row_sb[0:1, 0:512], psA[0:1, 0:512],
                                                  AF.Sqrt, bias=gb_sb[0:1, 13:14], scale=1.0 / 128.0), inc="act")
            p.wait("vector", "act")
            p.op("vector", lambda E: E.reciprocal(row_sb[0:1, 0:512], row_sb[0:1, 0:512]), inc="dve")
            p.wait("tensor", "dve")
            p.op("tensor", lambda E: E.matmul(
                psA[:, 1024:1536], onesR_sb[:], row_sb[0:1, 0:512], start=True, stop=True), inc="pe")
            p.wait("vector", "pe")
            if gaincol is not None:
                p.op("vector", lambda E, c0=c0, gc=gaincol: E.scalar_tensor_tensor(
                    h_sb[:, c0:c0 + 512], h_sb[:, c0:c0 + 512], gb_sb[:, gc:gc + 1],
                    psA[:, 1024:1536], ALU.mult, ALU.mult), inc="dve")
            else:
                p.op("vector", lambda E, c0=c0: E.tensor_mul(
                    h_sb[:, c0:c0 + 512], h_sb[:, c0:c0 + 512], psA[:, 1024:1536]), inc="dve")
            p.wait("tensor", "dve")
            p.wait("scalar", "dve")

    def head_attn(is_b, j, g2in):
        qh = h_sb[:, 0:8192]
        kh = h_sb[:, 8192:16384]
        vhf = h_sb[:, 16384:24576]
        vh = h_sb[:, 24576:32768]
        if not is_b:
            for g in range(16):
                p.op("vector", lambda E, g=g: E.reduce_sum(
                    cv_sb[:, g:g + 1], vhf[:, g * 512:(g + 1) * 512], axis=mybir.AxisListType.X),
                    inc="dve")
                p.wait("vector", "dve")
            p.wait("tensor", "dve")
            p.op("tensor", lambda E: E.transpose(psA[0:16, 0:128], cv_sb[:], ident_sb[:]), inc="pe")
            p.wait("vector", "pe")
            p.op("vector", lambda E: E.tensor_copy(cvT_sb[:], psA[0:16, 0:128]), inc="dve")
            p.wait("tensor", "dve")
            p.op("tensor", lambda E: E.matmul(psA[:, 0:16], cvT_sb[:], UT_sb[:],
                                              start=True, stop=True), inc="pe")
            p.wait("vector", "pe")
            p.op("vector", lambda E: E.tensor_copy(sv_sb[:], psA[:, 0:16]), inc="dve")
            p.wait("tensor", "dve")

        avs = [psB[:, 0:512], psB[:, 512:1024]]
        zs = [psB[0:1, 1024:1536], psB[0:1, 1536:2048]]
        bcs = [psB[:, 1024:1536], psB[:, 1536:2048]]
        rows = [row_sb[0:1, 0:512], row_sb[0:1, 512:1024]]
        rzVs = [sq_sb.bitcast(BF16)[:, 0:512], sq_sb.bitcast(BF16)[:, 512:1024]]
        exp_m = {}; av_m = {}; s2_m = {}
        zmm_m = {}; recip_m = {}; rzcopy_m = {}; norm_m = {}; store_m = {}
        av_last = {}
        gq = [0]

        def tail1(q):
            sl = q % 2
            if not is_b:
                p.wait("vector", "pe", av_last[q])
                p.op("vector", lambda E, sl=sl, q=q: E.tensor_scalar_add(
                    avs[sl], avs[sl], sv_sb[:, q:q + 1]), inc="dve")
            if q >= 2:
                p.wait("tensor", "act", rzcopy_m[q - 2])
            p.wait("tensor", "dve")
            for g in range(4):
                p.op("tensor", lambda E, g=g, sl=sl: E.matmul(
                    zs[sl], onesCb_sb[:], S2s[sl][:, g * 512:(g + 1) * 512],
                    start=(g == 0), stop=(g == 3)), inc=("pe" if g == 3 else None))
            zmm_m[q] = p.mark("pe")
            p.wait("vector", "pe", zmm_m[q])
            if not is_b and q < 15:
                p.op("vector", lambda E, sl=sl, q=q: E.tensor_scalar_add(
                    zs[sl], zs[sl], float(128 * (60 - 4 * q))), inc="dve")
                p.wait("vector", "dve")
            p.op("vector", lambda E, sl=sl: E.reciprocal(rows[sl], zs[sl]), inc="dve")
            recip_m[q] = p.mark("dve")

        def tail2(q):
            sl = q % 2
            if q >= 2:
                p.wait("tensor", "act", rzcopy_m[q - 2])
            p.wait("tensor", "dve", recip_m[q])
            p.op("tensor", lambda E, sl=sl: E.matmul(
                bcs[sl], onesR_sb[:], rows[sl], start=True, stop=True), inc="pe")
            bc_m = p.mark("pe")
            p.wait("scalar", "pe", bc_m)
            if q >= 2:
                p.wait("scalar", "dve", norm_m[q - 2])
            p.op("scalar", lambda E, sl=sl: E.copy(rzVs[sl], bcs[sl]), inc="act")
            rzcopy_m[q] = p.mark("act")
            p.wait("vector", "act", rzcopy_m[q])
            if q >= 2:
                p.wait("vector", "qS", store_m[q - 2])
            p.op("vector", lambda E, sl=sl: E.tensor_mul(
                stg[sl][:, 0:512], avs[sl], rzVs[sl]), inc="dve")
            norm_m[q] = p.mark("dve")
            p.wait("sync", "dve", norm_m[q])
            p.dma("sync", g2in[j * 128:(j + 1) * 128, q * QT:(q + 1) * QT], stg[sl][:, 0:512], "qS")
            store_m[q] = p.mark("qS")

        for q in range(NQT):
            nq = q + 1
            sl = q % 2
            if q >= 2:
                p.wait("vector", "pe", zmm_m[q - 2])
            p.op("vector", lambda E, sl=sl: E.memset(S2s[sl][:], 0.0), inc="dve")
            for qd in range(nq):
                g_ = gq[0]
                at = attnT2[g_ % 2]
                diag = (qd == nq - 1)
                if g_ >= 1:
                    p.wait("tensor", "act", exp_m[g_ - 1])
                for ci in range(4):
                    c = 4 * qd + ci
                    p.op("tensor", lambda E, c=c, q=q, ci=ci: E.matmul(
                        psA[:, ci * 512:(ci + 1) * 512], kh[:, c * 128:(c + 1) * 128],
                        qh[:, q * QT:(q + 1) * QT], start=True, stop=True),
                        inc=("pe" if ci == 3 else None))
                pe_m = p.mark("pe")
                if diag:
                    p.wait("vector", "pe", pe_m)
                    op_ = "tensor_add" if is_b else "tensor_mul"
                    for ci in range(4):
                        p.op("vector", lambda E, ci=ci, op_=op_: getattr(E, op_)(
                            psA[:, ci * 512:(ci + 1) * 512], psA[:, ci * 512:(ci + 1) * 512],
                            mask_sb[:, ci * QT:(ci + 1) * QT]),
                            inc=("dve" if ci == 3 else None))
                    p.wait("scalar", "dve")
                else:
                    p.wait("scalar", "pe", pe_m)
                if g_ >= 2:
                    p.wait("scalar", "pe", av_m[g_ - 2])
                    p.wait("scalar", "dve", s2_m[g_ - 2])
                if is_b:
                    p.op("scalar", lambda E, at=at, j=j: E.activation(
                        at, psA[:], AF.Exp, bias=gb_sb[:, 8 + j:9 + j], scale=SCALE_B), inc="act")
                else:
                    p.op("scalar", lambda E, at=at: E.activation(
                        at, psA[:], AF.Exp, scale=SCALE_A), inc="act")
                exp_m[g_] = p.mark("act")
                p.wait("vector", "act", exp_m[g_])
                p.op("vector", lambda E, at=at, sl=sl: E.tensor_add(S2s[sl][:], S2s[sl][:], at), inc="dve")
                s2_m[g_] = p.mark("dve")
                p.wait("tensor", "act", exp_m[g_])
                if qd == 0 and q >= 2:
                    p.wait("tensor", "dve", norm_m[q - 2])
                for ci in range(4):
                    c = 4 * qd + ci
                    p.op("tensor", lambda E, c=c, at=at, ci=ci, sl=sl,
                         s=(qd == 0 and ci == 0), st=(qd == nq - 1 and ci == 3): E.matmul(
                        avs[sl], vh[:, c * 128:(c + 1) * 128], at[:, ci * 512:(ci + 1) * 512],
                        start=s, stop=st), inc=("pe" if ci == 3 else None))
                av_m[g_] = p.mark("pe")
                gq[0] = g_ + 1
                if q >= 1 and qd == 0:
                    tail1(q - 1)
                if q >= 1 and qd == min(2, nq - 1):
                    tail2(q - 1)
            av_last[q] = av_m[gq[0] - 1]
        tail1(NQT - 1)
        tail2(NQT - 1)

    def o_proj(g2out, Wot, lora, scale_col, aocol, wmark):
        for hdc in range(NCK):
            p.dmad("gpsimd", lambda E, dyn, hd=hdc, ic=aocol + hdc: E.indirect_dma_start(
                h_sb[:, hd * TOK:(hd + 1) * TOK], None,
                g2out.rearrange("r (b x) -> (r b) x", b=4),
                IOA(gidx_sb[:, ic:ic + 1], 0)), "qG")
        p.barrier(("tensor", "scalar", "vector", "gpsimd"), ["qG"])
        if lora:
            load_AB(1536, 4096, 2048)
            emit_u()
        pe_done = {}
        cons = {}
        qc_m = {}
        p.wait("scalar", "qG", wmark)
        p.dma("scalar", wbuf[0][:, 0:2048], Wot[:, 0:2048], "qC")
        qc_m[0] = p.mark("qC")
        for och in range(NCK):
            wb = wbuf[och % 2]
            if och + 1 < NCK:
                if och >= 1:
                    p.wait("scalar", "pe", pe_done[och - 1])
                p.dma("scalar", wbuf[(och + 1) % 2][:, 0:2048],
                      Wot[:, (och + 1) * 2048:(och + 2) * 2048], "qC")
                qc_m[och + 1] = p.mark("qC")
            p.wait("tensor", "qC", qc_m[och])
            for th in range(2):
                u = och * 2 + th
                ps = psA if u % 2 == 0 else psB
                c0 = th * 1024
                if u >= 2:
                    p.wait("tensor", *cons[u - 2])
                for dc in range(NCK):
                    for g in range(2):
                        p.op("tensor", lambda E, d=dc, g=g, ps=ps, c0=c0, wb=wb: E.matmul(
                            ps[:, g * 512:(g + 1) * 512], wb[:, d * 128:(d + 1) * 128],
                            h_sb[:, d * TOK + c0 + g * 512:d * TOK + c0 + g * 512 + 512],
                            start=(d == 0), stop=(d == NCK - 1)))
                if lora:
                    for g in range(2):
                        p.op("tensor", lambda E, g=g, ps=ps, c0=c0, och=och: E.matmul(
                            ps[:, 1024 + g * 512:1024 + (g + 1) * 512],
                            BV[:, och * 128:(och + 1) * 128],
                            uV[:, c0 + g * 512:c0 + g * 512 + 512],
                            start=True, stop=True), inc=("pe" if g == 1 else None))
                else:
                    p.q["tensor"][-1] = (p.q["tensor"][-1][0], "pe", 1, None)
                    p.cnt["pe"] += 1
                if th == 1:
                    pe_done[och] = p.mark("pe")
                p.wait("scalar", "pe")
                st_ = stg[u % 2]
                if u >= 2:
                    p.wait("scalar", *cons[u - 2])
                if scale_col is not None:
                    p.op("scalar", lambda E, ps=ps, st=st_, sc=scale_col: E.activation(
                        st[:], ps[:, 0:1024], AF.Copy, scale=gb_sb[:, sc:sc + 1]), inc="act")
                else:
                    p.op("scalar", lambda E, ps=ps, st=st_: E.copy(st[:], ps[:, 0:1024]), inc="act")
                p.wait("vector", "act")
                if lora:
                    p.op("vector", lambda E, ps=ps, st=st_: E.tensor_add(st[:], st[:], ps[:, 1024:2048]))
                p.op("vector", lambda E, st=st_, och=och, c0=c0: E.tensor_add(
                    x_sb[:, och * TOK + c0:och * TOK + c0 + 1024],
                    x_sb[:, och * TOK + c0:och * TOK + c0 + 1024], st[:]), inc="dve")
                cons[u] = ("dve", p.mark("dve"))
        p.barrier()

    def attn_sublayer(is_b):
        emit_ln()
        if DBG and not is_b and not hasattr(attn_sublayer, "_h_dumped"):
            attn_sublayer._h_dumped = True
            p.barrier()
            p.dma("sync", dbg_h, h_sb[:], "qS")
            p.barrier()
        if not is_b:
            p.dma("sync", mask_sb[:], maskAi, "qS")
            fm_proj(WqaB, 16, g1a_in, 0, need_cast("Wqa"))
            fm_proj(WkaB, 16, g1a_in, 2048, need_cast("Wka"))
            fm_proj(WvaB, 16, g1a_in, 4096, need_cast("Wva"))
            collective(g1a_in, g1a_out, "cc0")
            for j in range(4):
                for s in range(4):
                    gather_rows(s * 2048, g1a_out, 12 * j + s)
                    gather_rows(8192 + s * 2048, g1a_out, 12 * j + 4 + s)
                    gather_rows(16384 + s * 2048, g1a_out, 12 * j + 8 + s)
                p.barrier(("tensor", "scalar", "vector", "gpsimd"), ["qG"])
                drain_casts(3)
                v_transpose()
                head_attn(False, j, g2a_in)
                p.barrier()
            collective(g2a_in, g2a_out, "cc1")
            o_proj(g2a_out, WoaB, False, None, 80, need_cast("Woa"))
        else:
            p.dma("sync", mask_sb[:], maskBi, "qS")
            p.dma("sync", csV, csO, "qS")
            load_AB(0, 0, 2048)
            emit_u()
            fm_proj(WqbB, 16, g1b_in, 0, need_cast("Wqb"), lora=True, scale_col=4, rope=True)
            load_AB(512, 2048, 1024)
            emit_u()
            fm_proj(WkbB, 8, g1b_in, 2048, need_cast("Wkb"), lora=True, scale_col=5, rope=True)
            load_AB(1024, 3072, 1024)
            emit_u()
            fm_proj(WvbB, 8, g1b_in, 3072, need_cast("Wvb"), lora=True, scale_col=6)
            collective(g1b_in, g1b_out, "cc2")
            for jj in range(2):
                for s in range(4):
                    gather_rows(8192 + s * 2048, g1b_out, 64 + 4 * jj + s)
                    gather_rows(16384 + s * 2048, g1b_out, 72 + 4 * jj + s)
                p.barrier(("tensor", "scalar", "vector", "gpsimd"), ["qG"])
                drain_casts(2)
                v_transpose()
                head_norm(8192, None)
                for j2 in range(2):
                    j = jj * 2 + j2
                    for s in range(4):
                        gather_rows(s * 2048, g1b_out, 48 + 4 * j + s)
                    p.barrier(("tensor", "scalar", "vector", "gpsimd"), ["qG"])
                    head_norm(0, j)
                    head_attn(True, j, g2b_in)
                    p.barrier()
            collective(g2b_in, g2b_out, "cc3")
            o_proj(g2b_out, WobB, True, 7, 96, need_cast("Wob"))

    def emit_router():
        for dc in range(NCK):
            for g in range(4):
                p.op("tensor", lambda E, d=dc, g=g: E.matmul(
                    psB[0:8, g * 512:(g + 1) * 512], rW_sb[:, d * 8:(d + 1) * 8],
                    h_sb[:, d * TOK + g * 512:d * TOK + g * 512 + 512],
                    start=(d == 0), stop=(d == NCK - 1)),
                    inc=("pe" if (dc == NCK - 1 and g == 3) else None))
        p.wait("scalar", "pe")
        p.op("scalar", lambda E: E.activation(sfmV[:, 0:2048], psB[0:8, 0:2048], AF.Sigmoid), inc="act")
        for tc in range(NCK):
            p.wait("tensor", "act")
            if tc >= 1:
                p.wait("tensor", "dve")
            p.op("tensor", lambda E, tc=tc: E.transpose(
                psA[:, 1536:1544], sfmV[0:8, tc * 128:(tc + 1) * 128], ident_sb[0:8, 0:8]), inc="pe")
            p.wait("vector", "pe")
            m1 = tp_sb[:, 0:1]; m2 = tp_sb[:, 1:2]; r_ = tp_sb[:, 3:4]
            chain = [
                lambda E: E.tensor_copy(stm_sb[:], psA[:, 1536:1544]),
                lambda E: E.reduce_max(m1, stm_sb[:], axis=mybir.AxisListType.X),
                lambda E: E.tensor_scalar(msk1_sb[:], stm_sb[:], m1, None, ALU.subtract),
                lambda E: E.tensor_single_scalar(msk1_sb[:], msk1_sb[:], 0.0, ALU.is_equal),
                lambda E: E.scalar_tensor_tensor(msk1_sb[:], msk1_sb[:], NEG, stm_sb[:], ALU.mult, ALU.add),
                lambda E: E.reduce_max(m2, msk1_sb[:], axis=mybir.AxisListType.X),
                lambda E: E.tensor_add(tp_sb[:, 2:3], m1, m2),
                lambda E: E.reciprocal(r_, tp_sb[:, 2:3]),
                lambda E: E.tensor_single_scalar(msk1_sb[:], stm_sb[:], m2, ALU.is_ge),
                lambda E: E.scalar_tensor_tensor(
                    ctm_holder[0][:], stm_sb[:], r_, msk1_sb[:], ALU.mult, ALU.mult),
            ]
            for fn_ in chain:
                p.op("vector", fn_, inc="dve")
                p.wait("vector", "dve")
            if DBG and tc == 0 and not getattr(emit_router, "_td", False):
                emit_router._td = True
                p.barrier()
                p.dma("gpsimd", dbg_t2[:, 0:8], stm_sb[:], "qG", allow_slow_non_contiguous=True)
                p.dma("gpsimd", dbg_t2[:, 8:16], msk1_sb[:], "qG", allow_slow_non_contiguous=True)
                p.dma("gpsimd", dbg_t2[:, 16:32], tp_sb[:], "qG", allow_slow_non_contiguous=True)
                p.dma("gpsimd", dbg_t2[:, 32:40], ctm_holder[0][:], "qG", allow_slow_non_contiguous=True)
                p.barrier()
            p.wait("tensor", "dve")
            p.op("tensor", lambda E: E.transpose(psA[0:8, 1536:1664], ctm_holder[0][:], ident_sb[:]), inc="pe")
            p.wait("vector", "pe")
            p.op("vector", lambda E, tc=tc: E.tensor_copy(c_fm[:, tc * 128:(tc + 1) * 128], psA[0:8, 1536:1664]), inc="dve")
        p.barrier(("tensor", "scalar", "gpsimd"), ["dve"])
        if DBG and not getattr(emit_router, "_rd", False):
            emit_router._rd = True
            p.barrier()
            p.dma("gpsimd", dbg_rt[0:8, 0:2048], sfmV[:, 0:2048], "qG")
            p.dma("gpsimd", dbg_rt[0:8, 2048:4096], c_fm[:], "qG")
            p.barrier()

    def emit_ffn(final):
        emit_ln()
        emit_router()
        quarters = [(0, 3), (3, 6), (6, 9), (9, 11)]
        def emit_cb(e, buf):
            p.wait("tensor", "dve")
            for g in range(4):
                p.op("tensor", lambda E, g=g, e=e: E.matmul(
                    psA[:, g * 512:(g + 1) * 512], sel_sb[:, e * 128:(e + 1) * 128],
                    c_fm[:, g * 512:(g + 1) * 512], start=True, stop=True),
                    inc=("pe" if g == 3 else None))
            p.wait("scalar", "pe")
            p.op("scalar", lambda E, buf=buf: E.copy(cb2[buf][:], psA[:, 0:2048]), inc="act")
            p.wait("vector", "act")
            p.wait("tensor", "act")

        for Ei in range(10):
            drain_casts(2)
            routed = Ei < 8
            cb_sb = cb2[Ei % 2]
            if routed:
                emit_cb(Ei, Ei % 2)
            pe_fc = {}
            qc_fc = {}
            p.wait("scalar", "qG", need_cast(f"w13_{Ei}"))
            p.wait("scalar", "qG", need_cast(f"w2_{Ei}"))
            p.dma("scalar", wbuf[0][:], w13B[Ei, 0], "qC")
            qc_fc[0] = p.mark("qC")
            for (f0, f1) in quarters:
                nf = f1 - f0
                p.wait("scalar", "pe")
                p.dma("scalar", w2V[:, 0:nf * 2048], w2sB[Ei][:, f0 * 2048:f1 * 2048], "qC")
                w2_m = p.mark("qC")
                he_marks = []
                for fi in range(nf):
                    fc = f0 + fi
                    wb = wbuf[fc % 2]
                    if fc + 1 < NFC:
                        if fc - 1 in pe_fc:
                            p.wait("scalar", "pe", pe_fc[fc - 1])
                        p.dma("scalar", wbuf[(fc + 1) % 2][:], w13B[Ei, fc + 1], "qC")
                        qc_fc[fc + 1] = p.mark("qC")
                    p.wait("tensor", "qC", qc_fc[fc])
                    for th in range(2):
                        u = fi * 2 + th
                        ps = psA if u % 2 == 0 else psB
                        c0 = th * 1024
                        if u >= 2:
                            p.wait("tensor", "dve", he_marks[u - 2])
                        for dc in range(NCK):
                            for g in range(2):
                                p.op("tensor", lambda E, d=dc, g=g, ps=ps, c0=c0, wb=wb: E.matmul(
                                    ps[:, g * 512:(g + 1) * 512], wb[:, d * 128:(d + 1) * 128],
                                    h_sb[:, d * TOK + c0 + g * 512:d * TOK + c0 + g * 512 + 512],
                                    start=(d == 0), stop=(d == NCK - 1)))
                        for dc in range(NCK):
                            for g in range(2):
                                p.op("tensor", lambda E, d=dc, g=g, ps=ps, c0=c0, wb=wb: E.matmul(
                                    ps[:, 1024 + g * 512:1024 + (g + 1) * 512],
                                    wb[:, 2048 + d * 128:2048 + (d + 1) * 128],
                                    h_sb[:, d * TOK + c0 + g * 512:d * TOK + c0 + g * 512 + 512],
                                    start=(d == 0), stop=(d == NCK - 1)),
                                    inc=("pe" if (dc == NCK - 1 and g == 1) else None))
                        if th == 1:
                            pe_fc[fc] = p.mark("pe")
                        p.wait("scalar", "pe")
                        p.wait("scalar", "dve")
                        p.op("scalar", lambda E, ps=ps: E.activation(
                            silu_sb[:], ps[:, 0:1024], AF.Silu), inc="act")
                        p.wait("vector", "act")
                        he_sl = heV[:, fi * 2048 + c0:fi * 2048 + c0 + 1024]
                        p.op("vector", lambda E, ps=ps, he=he_sl: E.tensor_mul(he, silu_sb[:], ps[:, 1024:2048]))
                        if routed:
                            p.op("vector", lambda E, he=he_sl, c0=c0, cb=cb_sb: E.tensor_mul(
                                he, he, cb[:, c0:c0 + 1024]), inc="dve")
                        else:
                            p.q["vector"][-1] = (p.q["vector"][-1][0], "dve", 1, None)
                            p.cnt["dve"] += 1
                        he_marks.append(p.mark("dve"))
                if DBG and Ei == 0 and f0 == 0 and not getattr(emit_ffn, "_fd", False):
                    emit_ffn._fd = True
                    p.barrier()
                    p.dma("gpsimd", dbg_ffn[:, 0:6144], heV, "qG")
                    p.dma("gpsimd", dbg_ffn[:, 6144:8192], cb_sb[:], "qG")
                    p.barrier()
                p.wait("tensor", "qC", w2_m)
                p.wait("tensor", "dve")
                for och in range(16):
                    ps = psA if och % 2 == 0 else psB
                    if och >= 2:
                        p.wait("tensor", "dve")
                    for fi in range(nf):
                        for g in range(4):
                            p.op("tensor", lambda E, fi=fi, och=och, g=g, ps=ps, s=(fi == 0), st=(fi == nf - 1): E.matmul(
                                ps[:, g * 512:(g + 1) * 512],
                                w2V[:, fi * 2048 + och * 128:fi * 2048 + (och + 1) * 128],
                                heV[:, fi * 2048 + g * 512:fi * 2048 + g * 512 + 512],
                                start=s, stop=st),
                                inc=("pe" if (fi == nf - 1 and g == 3) else None))
                    p.wait("vector", "pe")
                    p.op("vector", lambda E, ps=ps, och=och: E.tensor_add(
                        x_sb[:, och * TOK:och * TOK + 2048],
                        x_sb[:, och * TOK:och * TOK + 2048], ps[:, 0:2048]), inc="dve")
        p.barrier()

    ctm_holder = [ctx.enter_context(nc.sbuf_tensor("ctm_sb", [128, 8], F32))]


    def dump(dst, src):
        if DBG:
            p.barrier()
            p.dma("gpsimd", dst, src, "qG")
            p.barrier()

    def dump_sb(dst, src):
        if DBG:
            p.barrier()
            p.dma("sync", dst, src[:], "qS")
            p.barrier()

    if DBG:
        emit_ln._dump_h = dbg_h

    attn_sublayer(False)
    if DBG:
        dump(dbg_qkv, g1a_in)
        dump(dbg_att, g2a_in)
        dump_sb(dbg_x1, x_sb)
    emit_ffn(False)
    if DBG:
        dump_sb(dbg_x2, x_sb)
    attn_sublayer(True)
    if DBG:
        dump(dbg_qkvb, g1b_in)
        dump(dbg_attb, g2b_in)
        dump_sb(dbg_x3, x_sb)
    emit_ffn(True)

    for och in range(NCK):
        for th in range(2):
            u = och * 2 + th
            if u >= 1:
                p.wait("scalar", "qS", out_m[u - 1] if u >= 1 else 0)
            p.op("scalar", lambda E, och=och, th=th: E.copy(
                outV[:, 0:1024], x_sb[:, och * TOK + th * 1024:och * TOK + th * 1024 + 1024]), inc="act")
            p.wait("sync", "act")
            if u == 0:
                out_m = {}
            p.dma("sync", out_d[:, och * TOK + th * 1024:och * TOK + th * 1024 + 1024], outV[:, 0:1024], "qS")
            out_m[u] = p.mark("qS")
    p.wait("sync", "qS")
    p.wait("sync", "qG")
    p.wait("sync", "qC")

    with nc.Block() as block:
        def make_runner(name):
            def runner(engine):
                dyn = {}
                for fn, inc, n, dfn in p.q[name]:
                    if dfn is not None:
                        ins = dfn(engine, dyn)
                        ins.then_inc(sems[inc], n)
                    elif isinstance(fn, tuple) and fn[0] == "W":
                        engine.wait_ge(sems[fn[1]], fn[2])
                    else:
                        ins = fn(engine)
                        if inc is not None:
                            ins.then_inc(sems[inc], n)
            return runner

        r_sync = make_runner("sync"); r_scalar = make_runner("scalar")
        r_vector = make_runner("vector"); r_tensor = make_runner("tensor")
        r_gpsimd = make_runner("gpsimd")

        @block.sync
        def _(sync):
            r_sync(sync)

        @block.scalar
        def _(scalar):
            r_scalar(scalar)

        @block.vector
        def _(vector):
            r_vector(vector)

        @block.tensor
        def _(tensor):
            r_tensor(tensor)

        @block.gpsimd
        def _(gpsimd):
            r_gpsimd(gpsimd)

    ctx.close()
    return nc



def tile_stationary(W):
    d, o = W.shape
    ndc, noc = d // 128, o // 128
    return np.ascontiguousarray(
        W.reshape(ndc, 128, noc, 128).transpose(1, 2, 0, 3).reshape(128, noc * ndc * 128))


def tile_moving(W):
    d, o = W.shape
    ndc = d // 128
    return np.ascontiguousarray(W.reshape(ndc, 128, o).transpose(1, 0, 2).reshape(128, ndc * o))


def tile_a(A):
    At = np.ascontiguousarray(A.T)
    ndc = At.shape[0] // 128
    return np.ascontiguousarray(At.reshape(ndc, 128, 32).transpose(1, 0, 2).reshape(128, ndc * 32))


def tile_w13(w1, w3):
    nfc = w1.shape[1] // 128
    out = np.empty((nfc, 128, 4096), np.float32)
    for which, w in ((0, w1), (1, w3)):
        t = w.reshape(16, 128, nfc, 128).transpose(2, 1, 0, 3).reshape(nfc, 128, 2048)
        out[:, :, which * 2048:(which + 1) * 2048] = t
    return out


def tile_w2(w2):
    nfc = w2.shape[0] // 128
    return np.ascontiguousarray(
        w2.reshape(nfc, 128, 16, 128).transpose(1, 0, 2, 3).reshape(128, nfc * 2048))


_CACHE = {}


def kernel(**inputs):
    ip = {k: np.asarray(v, dtype=np.float32) for k, v in inputs.items()}
    x = ip["L0_attn_a_x"]

    shared = {}
    shared["Wqa"] = tile_stationary(ip["L0_attn_a_Wq"])
    shared["Wka"] = tile_stationary(ip["L0_attn_a_Wk"])
    shared["Wva"] = tile_stationary(ip["L0_attn_a_Wv"])
    shared["Woa"] = tile_stationary(ip["L0_attn_a_Wo"])
    shared["Wqb"] = tile_stationary(np.ascontiguousarray(ip["L0_attn_b_W_q"].T))
    shared["Wkb"] = tile_stationary(np.ascontiguousarray(ip["L0_attn_b_W_k"].T))
    shared["Wvb"] = tile_stationary(np.ascontiguousarray(ip["L0_attn_b_W_v"].T))
    shared["Wob"] = tile_stationary(np.ascontiguousarray(ip["L0_attn_b_W_o"].T))
    shared["Aall"] = np.concatenate([tile_a(ip["L0_attn_b_A_q"]), tile_a(ip["L0_attn_b_A_k"]),
                                     tile_a(ip["L0_attn_b_A_v"]), tile_a(ip["L0_attn_b_A_o"])], axis=1)
    shared["Ball"] = np.concatenate([np.ascontiguousarray(ip["L0_attn_b_B_q"].T),
                                     np.ascontiguousarray(ip["L0_attn_b_B_k"].T),
                                     np.ascontiguousarray(ip["L0_attn_b_B_v"].T),
                                     np.ascontiguousarray(ip["L0_attn_b_B_o"].T)], axis=1)
    rt = ip["L0_ffn_w_router"]
    shared["rW"] = np.ascontiguousarray(
        rt.reshape(16, 128, 8).transpose(1, 0, 2).reshape(128, 128))
    w13l = [tile_w13(ip["L0_ffn_w1"][e], ip["L0_ffn_w3"][e]) for e in range(8)]
    ws1, ws3, ws2 = ip["L0_ffn_ws1"], ip["L0_ffn_ws3"], ip["L0_ffn_ws2"]
    w13l.append(tile_w13(ws1[:, 0:1408], ws3[:, 0:1408]))
    w13l.append(tile_w13(ws1[:, 1408:2816], ws3[:, 1408:2816]))
    shared["w13"] = np.stack(w13l)
    w2l = [tile_w2(ip["L0_ffn_w2"][e]) for e in range(8)]
    w2l.append(tile_w2(ws2[0:1408]))
    w2l.append(tile_w2(ws2[1408:2816]))
    shared["w2s"] = np.stack(w2l)

    tqi = np.arange(QT)[None, :]
    tsi = np.arange(128)[:, None]
    mA = np.zeros((128, 4 * QT), ml_dtypes.bfloat16)
    mB = np.zeros((128, 4 * QT), ml_dtypes.bfloat16)
    for d_ in range(4):
        keep = (tqi - d_ * 128) >= tsi
        mA[:, d_ * QT:(d_ + 1) * QT] = keep.astype(ml_dtypes.bfloat16)
        mB[:, d_ * QT:(d_ + 1) * QT] = np.where(keep, 0.0, NEG).astype(ml_dtypes.bfloat16)
    shared["maskAi"] = mA
    shared["maskBi"] = mB
    shared["identF"] = np.eye(128, dtype=np.float32)
    shared["onesC"] = np.ones((128, 1), np.float32)
    shared["onesCb"] = np.ones((128, 1), ml_dtypes.bfloat16)
    shared["onesR"] = np.ones((1, 128), np.float32)
    g_ = np.arange(16)
    shared["UTc"] = (g_[:, None] > g_[None, :]).astype(np.float32)
    sel = np.zeros((8, 8 * 128), ml_dtypes.bfloat16)
    for e in range(8):
        sel[e, e * 128:(e + 1) * 128] = 1
    shared["sel8"] = sel
    shared["svec"] = np.array([[ip["L0_attn_b_s_q"][0], ip["L0_attn_b_s_k"][0],
                                ip["L0_attn_b_s_v"][0], ip["L0_attn_b_s_o"][0]]], np.float32)
    shared["epsv"] = np.array([[LN_EPS, RMS_EPS]], np.float32)

    inv = 1.0 / (10000.0 ** (np.arange(0, 64, 2, dtype=np.float32) / 64.0))
    freqs = np.outer(np.arange(T, dtype=np.float32), inv)
    cosf = np.cos(freqs).T
    sinf = np.sin(freqs).T

    in_maps = []
    for c in range(8):
        b, r = c // 4, c % 4
        xs = np.ascontiguousarray(x[b, r * TOK:(r + 1) * TOK, :].T)
        m = dict(shared)
        m["x_in"] = np.ascontiguousarray(
            xs.reshape(16, 128, TOK).transpose(1, 0, 2).reshape(128, 16 * TOK))
        cs = np.empty((64, 2 * TOK), ml_dtypes.bfloat16)
        cs[0:32, 0:TOK] = cosf[:, r * TOK:(r + 1) * TOK].astype(ml_dtypes.bfloat16)
        cs[32:64, 0:TOK] = sinf[:, r * TOK:(r + 1) * TOK].astype(ml_dtypes.bfloat16)
        cs[0:32, TOK:] = sinf[:, r * TOK:(r + 1) * TOK].astype(ml_dtypes.bfloat16)
        cs[32:64, TOK:] = cosf[:, r * TOK:(r + 1) * TOK].astype(ml_dtypes.bfloat16)
        m["csO"] = cs
        m["gain4"] = np.ascontiguousarray(ip["L0_attn_b_q_gain"][4 * r:4 * r + 4][None, :])
        pp = np.arange(128, dtype=np.uint32)
        gi = np.zeros((128, 112), np.uint32)
        for j in range(4):
            for s_ in range(4):
                gi[:, 12 * j + s_] = 512 * r + s_ * 6144 + j * 128 + pp
                gi[:, 12 * j + 4 + s_] = 512 * r + s_ * 6144 + 2048 + j * 128 + pp
                gi[:, 12 * j + 8 + s_] = 512 * r + s_ * 6144 + 4096 + j * 128 + pp
        for j in range(4):
            for s_ in range(4):
                gi[:, 48 + 4 * j + s_] = 512 * r + s_ * 4096 + j * 128 + pp
        for jj in range(2):
            for s_ in range(4):
                gi[:, 64 + 4 * jj + s_] = 256 * r + s_ * 4096 + 2048 + jj * 128 + pp
                gi[:, 72 + 4 * jj + s_] = 256 * r + s_ * 4096 + 3072 + jj * 128 + pp
        for hdc in range(16):
            gi[:, 80 + hdc] = (hdc * 128 + pp) * 4 + r
            gi[:, 96 + hdc] = (hdc * 128 + pp) * 4 + r
        m["gidx"] = gi
        in_maps.append(m)

    if "nc" not in _CACHE:
        _CACHE["nc"] = build_program()
    nc = _CACHE["nc"]
    res = run_bass_kernel_spmd(nc, in_maps, core_ids=list(range(8)))

    out = np.empty((2, T, D), np.float32)
    for c in range(8):
        b, r = c // 4, c % 4
        o = res.results[c]["out_d"]
        out[b, r * TOK:(r + 1) * TOK, :] = o.reshape(128, 16, TOK).transpose(1, 0, 2).reshape(D, TOK).T
    return out
